# version 07
# baseline (speedup 1.0000x reference)
import os
import math
from contextlib import ExitStack
import numpy as np
import concourse.bass as bass
import concourse.mybir as mybir
from concourse.bass_utils import run_bass_kernel_spmd

F32 = mybir.dt.float32
BF16 = mybir.dt.bfloat16
AF = mybir.ActivationFunctionType
ALU = mybir.AluOpType
AX = mybir.AxisListType.X

STAGE = int(os.environ.get("MK_STAGE", "5"))

NCORE = 8
Bb, L, D, H, HD = 4, 4096, 1024, 16, 64
TPC, NT = 2048, 16
HID = 2730
NMC = 22
MC = [128] * 21 + [42]
EPSA = 1e-6
RG = [[0, 1], [2, 3], [4, 5], [6, 7]]
ENGS = ("tensor", "vector", "scalar", "sync", "gpsimd")


class PB:
    def __init__(self):
        self.dsems = {"sync": [f"d{i}" for i in range(6)],
                      "gpsimd": [f"g{i}" for i in range(3)]}
        self.SEMS = list(ENGS) + self.dsems["sync"] + self.dsems["gpsimd"]
        self.q = {e: [] for e in ENGS}
        self.cnt = {s: 0 for s in self.SEMS}
        self.dirty = {e: False for e in ENGS}
        self.hw = {(a, b): 0 for a in ENGS for b in self.SEMS}
        self.rr = {"sync": 0, "gpsimd": 0}
        self.nops = 0

    def do(self, e, fn, inc=0):
        if e in ("vector", "scalar") and inc == 0:
            inc = 1
        self.q[e].append(("op", fn, e if inc else None, inc))
        self.nops += 1
        if inc:
            self.cnt[e] += inc
            self.dirty[e] = False
            if e in ("vector", "scalar"):
                self.q[e].append(("wait", e, self.cnt[e]))
        else:
            self.dirty[e] = True
        return (e, self.cnt[e])

    def dma(self, e, fn):
        pool = self.dsems[e]
        s = pool[self.rr[e] % len(pool)]
        self.rr[e] += 1
        self.xwait(e, (s, self.cnt[s]))
        self.q[e].append(("op", fn, s, 16))
        self.nops += 1
        self.cnt[s] += 16
        return (s, self.cnt[s])

    def tok(self, e):
        if self.dirty[e]:
            for i in range(len(self.q[e]) - 1, -1, -1):
                ent = self.q[e][i]
                if ent[0] == "op" and ent[2] is None:
                    self.cnt[e] += 1
                    self.q[e][i] = ("op", ent[1], e, 1)
                    if e in ("vector", "scalar"):
                        self.q[e].append(("wait", e, self.cnt[e]))
                    break
            self.dirty[e] = False
        return (e, self.cnt[e])

    def wait(self, w, tokn):
        if tokn is None:
            return
        e, v = tokn
        if e == w or v <= 0 or self.hw[(w, e)] >= v:
            return
        self.hw[(w, e)] = v
        self.q[w].append(("wait", e, v))

    def xwait(self, w, tokn):
        e, v = tokn
        if v <= 0:
            return
        if e != w and self.hw[(w, e)] >= v:
            return
        if e != w:
            self.hw[(w, e)] = v
        self.q[w].append(("wait", e, v))

    def barrier(self):
        toks = [self.tok(e) for e in ENGS]
        toks += [(s, self.cnt[s]) for s in self.SEMS if s not in ENGS]
        for w in ENGS:
            for t in toks:
                self.xwait(w, t) if t[0] == w else self.wait(w, t)

    def dmaq(self, w):
        for s_ in self.dsems["sync"] + self.dsems["gpsimd"]:
            self.wait(w, (s_, self.cnt[s_]))

    def tick(self, engines=("tensor", "vector", "scalar")):
        toks = [self.tok(e) for e in engines]
        for w in engines:
            for t in toks:
                self.wait(w, t)


def build(stage=STAGE):
    nc = bass.Bass("TRN2", target_bir_lowering=False)

    dx = nc.dram_tensor("x", [TPC, D], F32, kind="ExternalInput").ap()
    dy = nc.dram_tensor("y", [TPC, D], F32, kind="ExternalOutput").ap()

    def din(name, shape):
        return nc.dram_tensor(name, shape, F32, kind="ExternalInput").ap()

    dWq = din("aWq", [D, D]); dbq = din("abq", [1, D])
    dWk = din("aWk", [D, D]); dbk = din("abk", [1, D])
    dWv = din("aWv", [D, D]); dbv = din("abv", [1, D])
    dWo = din("aWo", [D, D]); dbo = din("abo", [1, D])
    dgw = din("gw", [D, 8]); dgb = din("gb", [1, 8])
    dew = din("ew", [8, D, D]); deb = din("eb", [8, D])
    dln1w = din("ln1w", [1, D]); dln1b = din("ln1b", [1, D])
    dbwq = din("bwq", [D, D]); dbwk = din("bwk", [D, D])
    dbwv = din("bwv", [D, D]); dbwg = din("bwg", [D, D])
    dbwo = din("bwo", [D, D])
    dln2w = din("ln2w", [1, D]); dln2b = din("ln2b", [1, D])
    dwin = din("win", [D, 2 * HID]); dbin = din("bin", [1, 2 * HID])
    dwout = din("wout", [HID, D]); dbout = din("bout", [1, D])
    dm1 = din("m1", [128, 1])
    dm2 = din("m2", [128, 1])

    sp_k = nc.dram_tensor("sp_k", [128, NT * 512], F32).ap()
    sp_b = nc.dram_tensor("sp_b", [128, NT * 512], F32).ap()
    sp_x2 = nc.dram_tensor("sp_x2", [128, NT * 1024], F32).ap()
    cc1i = nc.dram_tensor("cc1i", [1, 1024], F32).ap()
    cc1o = nc.dram_tensor("cc1o", [1, 1024], F32).ap()
    cc3i = nc.dram_tensor("cc3i", [128, 264], F32).ap()
    cc3o = nc.dram_tensor("cc3o", [128, 264], F32).ap()
    ccsi = nc.dram_tensor("ccsi", [128, 128], F32).ap()
    ccso = nc.dram_tensor("ccso", [128, 128], F32).ap()

    TAP = os.environ.get("MK_TAP", "")
    ddbg = (nc.dram_tensor("dbg", [128, 4096], F32, kind="ExternalOutput").ap()
            if TAP else None)

    pb = PB()

    with ExitStack() as ctx:
        sbal = lambda n, c: ctx.enter_context(nc.sbuf_tensor(n, [128, c], F32)).ap()
        xsb = sbal("xsb", NT * 1024)
        aA = sbal("aA", NT * 1024)
        aB = sbal("aB", 8192)
        aW = sbal("aW", 6144)
        aM = sbal("aM", 5760)
        pp = [ctx.enter_context(nc.psum_tensor(f"pp{i}", [128, 512], F32)).ap()
              for i in range(8)]
        sems = {n: ctx.enter_context(nc.semaphore(f"s_{n}")) for n in pb.SEMS}

        o_ = [0]
        def am(n):
            v = o_[0]; o_[0] += n
            assert o_[0] <= 5760, f"aM overflow {o_[0]}"
            return aM[:, v:v + n]
        ident = am(128)
        onesrow = am(128)
        onescol = am(1)
        lngp = am(8)
        lngb = am(16)
        g128 = am(8)
        gvcol = am(16)
        union = am(768)
        mcol = am(16); rcol = am(16)
        rows = am(1024)
        scr = am(1024)
        scr2 = am(1024)
        kapb = am(512)
        kapb2 = am(512)
        seps = am(48)
        ccst = am(44)
        smA = am(128)
        kvsbm = am(256)
        sfA = union[:, 0:128]; sfB = union[:, 128:256]; eAm = union[:, 256:384]
        dmask = union[:, 0:512]; gqmat = union[:, 512:768]
        gat = union[:, 0:128]
        eksi = seps[:, 0:8]; ekso = seps[:, 8:16]; scb = seps[:, 16:24]
        m1c = seps[:, 24:25]; m2c = seps[:, 25:26]
        l1wc = seps[:, 26:34]; l1bc = seps[:, 34:42]
        ones11 = onesrow[0:1, 0:1]

        RW = [0]
        def row(n):
            v = RW[0]; RW[0] += n
            assert RW[0] <= 1024, f"rows overflow {RW[0]}"
            return rows[0:1, v:v + n]
        def row_reset():
            RW[0] = 0

        oB = [0]
        def ab(n):
            v = oB[0]; oB[0] += n
            assert oB[0] <= 8192, f"aB overflow {oB[0]}"
            return aB[:, v:v + n]
        def ab_reset():
            oB[0] = 0

        oW = [0]
        def aw(n):
            v = oW[0]; oW[0] += n
            assert oW[0] <= 6144, f"aW overflow {oW[0]}"
            return aW[:, v:v + n]
        def aw_reset():
            oW[0] = 0

        bank_free = {i: None for i in range(8)}

        def mm(dst, lhsT, rhs, start, stop, inc=0):
            return pb.do("tensor",
                         lambda e, d=dst, l=lhsT, r=rhs, s1=start, s2=stop:
                         e.matmul(d, l, r, start=s1, stop=s2, skip_group_check=True),
                         inc)

        def tr(dst, src, inc=0):
            return pb.do("tensor",
                         lambda e, d=dst, s=src: e.transpose(d, s, ident), inc)

        def V(fn, inc=0):
            return pb.do("vector", fn, inc)

        def SC(fn, inc=0):
            return pb.do("scalar", fn, inc)

        def G(fn, inc=0):
            pb.xwait("gpsimd", ("gpsimd", pb.cnt["gpsimd"]))
            return pb.do("gpsimd", fn, inc if inc else 1)

        def DMA(e, dst, src):
            return pb.dma(e, lambda eng, d=dst, s=src: eng.dma_start(out=d, in_=s))

        def tap(off, ap, n):
            if not TAP:
                return
            pb.barrier()
            DMA("sync", ddbg[0:ap.shape[0], off:off + n], ap)
            pb.barrier()

        def collective(ci, co):
            t = pb.do("gpsimd",
                      lambda e, i=ci, o=co: e.collective_compute(
                          "AllReduce", ALU.add, replica_groups=RG,
                          ins=[i.opt()], outs=[o.opt()]), 1)
            return t

        def ttb(dst, a, colap, nh, op, inc=0, dd=64):
            d3 = dst.rearrange("p (h d) -> p h d", d=dd)
            a3 = a.rearrange("p (h d) -> p h d", d=dd)
            b3 = colap.rearrange("p (h o) -> p h o", o=1).broadcast_to([128, nh, dd])
            return V(lambda e, d=d3, x=a3, y=b3, o=op: e.tensor_tensor(d, x, y, op=o),
                     inc)

        def xt(t):
            return xsb[:, 1024 * t:1024 * (t + 1)]

        def nT(t):
            return aA[:, 1024 * t:1024 * (t + 1)]

        def rsqrt_newton(dst, src, eps, t1, t2):
            V(lambda e, d=t1, s=src, ep=eps: e.tensor_scalar_add(d, s, ep), 1)
            pb.wait("scalar", pb.tok("vector"))
            SC(lambda e, d=t2, s=t1: e.activation(d, s, AF.Sqrt), 1)
            pb.wait("vector", pb.tok("scalar"))
            V(lambda e, d=dst, s=t2: e.reciprocal(d, s))
            V(lambda e, d=t2, a=dst, b=dst: e.tensor_mul(d, a, b))
            V(lambda e, d=t2, a=t1, b=t2: e.tensor_mul(d, a, b))
            V(lambda e, d=t2, a=t2: e.tensor_scalar(d, a, -0.5, 1.5,
                                                    op0=ALU.mult, op1=ALU.add))
            V(lambda e, d=dst, a=dst, b=t2: e.tensor_mul(d, a, b))

        def emit_stats(t, mdst, rdst, src_fn=xt, dbg=False):
            sums = scr2[:, 0:1]; sqs = scr2[:, 1:2]
            V(lambda e, d=sums, s=src_fn(t): e.reduce_sum(d, s, axis=AX), 1)
            pb.wait("scalar", pb.tok("vector"))
            SC(lambda e, d=scr[:, 0:1024], s=src_fn(t):
               e.activation(d, s, AF.Square), 1)
            pb.wait("vector", pb.tok("scalar"))
            V(lambda e, d=sqs, s=scr[:, 0:1024]: e.reduce_sum(d, s, axis=AX))
            V(lambda e, d=mdst, s=sums: e.tensor_scalar_mul(d, s, 1.0 / 1024.0))
            tv = scr2[:, 2:3]
            V(lambda e, d=tv, s=sqs: e.tensor_scalar_mul(d, s, 1.0 / 1024.0))
            tm = scr2[:, 3:4]
            V(lambda e, d=tm, a=mdst, b=mdst: e.tensor_mul(d, a, b))
            V(lambda e, d=tv, a=tv, b=tm: e.tensor_sub(d, a, b))
            rsqrt_newton(rdst, tv, 1e-5, scr2[:, 4:5], scr2[:, 5:6])
            return tv

        def emit_xnt(t, mdst, rdst):
            xn = scr[:, 0:1024]
            pb.wait("vector", pb.tok("scalar"))
            V(lambda e, d=xn, s=xt(t), m=mdst, r=rdst:
              e.tensor_scalar(d, s, m, r, op0=ALU.subtract, op1=ALU.mult), 1)
            vtok = pb.tok("vector")
            pb.wait("tensor", vtok)
            pb.wait("tensor", bank_free[0])
            pb.wait("tensor", bank_free[1])
            for j in range(8):
                b = 0 if j < 4 else 1
                tr(pp[b][:, 128 * (j % 4):128 * (j % 4 + 1)],
                   xn[:, 128 * j:128 * (j + 1)], inc=(1 if j in (3, 7) else 0))
            ttok = pb.tok("tensor")
            pb.wait("scalar", ttok)
            for j in range(8):
                b = 0 if j < 4 else 1
                SC(lambda e, d=nT(t)[:, 128 * j:128 * (j + 1)],
                   s=pp[b][:, 128 * (j % 4):128 * (j % 4 + 1)]: e.copy(d, s),
                   inc=(1 if j in (3, 7) else 0))
            stok = pb.tok("scalar")
            bank_free[0] = stok
            bank_free[1] = stok
            return stok

        aAb = aA.bitcast(BF16)

        def nTh(t):
            return aAb[:, 1024 * t:1024 * (t + 1)]

        def nTl(t):
            return aAb[:, 16384 + 1024 * t:16384 + 1024 * (t + 1)]

        def emit_xnt_bf(t, mdst, rdst):
            xn = scr[:, 0:1024]
            pb.wait("vector", pb.tok("scalar"))
            V(lambda e, d=xn, s=xt(t), m=mdst, r=rdst:
              e.tensor_scalar(d, s, m, r, op0=ALU.subtract, op1=ALU.mult), 1)
            vtok = pb.tok("vector")
            pb.wait("tensor", vtok)
            pb.wait("tensor", bank_free[0])
            pb.wait("tensor", bank_free[1])
            for j in range(8):
                b = 0 if j < 4 else 1
                tr(pp[b][:, 128 * (j % 4):128 * (j % 4 + 1)],
                   xn[:, 128 * j:128 * (j + 1)], inc=(1 if j in (3, 7) else 0))
            ttok = pb.tok("tensor")
            pb.wait("scalar", ttok)
            for j in range(8):
                b = 0 if j < 4 else 1
                SC(lambda e, d=nTh(t)[:, 128 * j:128 * (j + 1)],
                   s=pp[b][:, 128 * (j % 4):128 * (j % 4 + 1)]: e.copy(d, s),
                   inc=(1 if j in (3, 7) else 0))
            hstok = pb.tok("scalar")
            pb.wait("vector", hstok)
            for j in range(8):
                b = 0 if j < 4 else 1
                V(lambda e, d=nTl(t)[:, 128 * j:128 * (j + 1)],
                  a=pp[b][:, 128 * (j % 4):128 * (j % 4 + 1)],
                  c=nTh(t)[:, 128 * j:128 * (j + 1)]:
                  e.scalar_tensor_tensor(d, a, 1.0, c, op0=ALU.mult, op1=ALU.subtract),
                  inc=(1 if j in (3, 7) else 0))
            lstok = pb.tok("vector")
            bank_free[0] = lstok
            bank_free[1] = lstok
            return lstok

        def emit_proj3(t, psum_ap, whi, wlo, start=True, inc=0):
            tk = None
            for j in range(8):
                tk = mm(psum_ap, nTh(t)[:, 128 * j:128 * (j + 1)], whi[:, j, :],
                        start and j == 0, False)
                tk = mm(psum_ap, nTh(t)[:, 128 * j:128 * (j + 1)], wlo[:, j, :],
                        False, False)
                tk = mm(psum_ap, nTl(t)[:, 128 * j:128 * (j + 1)], whi[:, j, :],
                        False, False, inc=(inc if j == 7 else 0))
            return tk

        def split_w(staging_flat, hi_flat, lo_flat, ld_tok, cons=()):
            pb.wait("scalar", ld_tok)
            for tk_ in cons:
                pb.wait("scalar", tk_)
            SC(lambda e, d=hi_flat, s=staging_flat: e.copy(d, s), 1)
            hs = pb.tok("scalar")
            pb.wait("vector", hs)
            for tk_ in cons:
                pb.wait("vector", tk_)
            V(lambda e, d=lo_flat, a=staging_flat, c=hi_flat:
              e.scalar_tensor_tensor(d, a, 1.0, c, op0=ALU.mult, op1=ALU.subtract), 1)
            return pb.tok("vector")

        def emit_proj(t, psum_ap, wsl, start=True, inc=0, nj=8):
            tk = None
            for j in range(nj):
                tk = mm(psum_ap, nT(t)[:, 128 * j:128 * (j + 1)], wsl[:, j, :],
                        start and j == 0, False, inc=(inc if j == nj - 1 else 0))
            return tk

        def biasfold(psum_ap, brow, stop=True, inc=1):
            return mm(psum_ap, onesrow[0:1, 0:128], brow, False, stop, inc=inc)

        def row2col(rowap, colap, n, bank=2):
            nchunks = (n + 127) // 128
            pb.wait("tensor", bank_free[bank])
            for j in range(nchunks):
                w = min(128, n - 128 * j)
                mm(pp[bank][0:w, j:j + 1], rowap[0:1, 128 * j:128 * j + w], ones11,
                   True, True, inc=(1 if j == nchunks - 1 else 0))
            tt = pb.tok("tensor")
            pb.wait("vector", tt)
            for j in range(nchunks):
                w = min(128, n - 128 * j)
                V(lambda e, d=colap[0:w, j:j + 1], s=pp[bank][0:w, j:j + 1]:
                  e.tensor_copy(d, s), inc=(1 if j == nchunks - 1 else 0))
            bank_free[bank] = pb.tok("vector")

        def bcast_row(rowap, dst, n, bank=2):
            pb.wait("tensor", bank_free[bank])
            mm(pp[bank][:, 0:n], onesrow[0:1, 0:128], rowap, True, True, inc=1)
            tt = pb.tok("tensor")
            pb.wait("vector", tt)
            V(lambda e, d=dst, s=pp[bank][:, 0:n]: e.tensor_copy(d, s), 1)
            bank_free[bank] = pb.tok("vector")

        G(lambda e: e.memset(ident, 0.0), 1)
        pb.xwait("gpsimd", pb.tok("gpsimd"))
        G(lambda e: e.affine_select(out=ident, in_=ident, compare_op=ALU.not_equal,
                                    fill=1.0, base=0, pattern=[[-1, 128]],
                                    channel_multiplier=1))
        G(lambda e: e.memset(onesrow[0:1, :], 1.0))
        G(lambda e: e.memset(onescol, 1.0))
        G(lambda e: e.memset(rows, 0.0))
        gr = row(16); lgr = row(16)
        for h in range(16):
            gam = 1.0 - 2.0 ** (-5.0 - h)
            G(lambda e, d=gr[0:1, h:h + 1], v=gam: e.memset(d, v),
              (1 if h == 15 else 0))
        pb.wait("scalar", pb.tok("gpsimd"))
        SC(lambda e, d=lgr, s=gr: e.activation(d, s, AF.Ln), 1)
        pb.wait("tensor", pb.tok("scalar"))
        pb.wait("tensor", pb.tok("gpsimd"))
        bcast_row(lgr, lngb, 16, bank=2)
        for j in range(8):
            V(lambda e, d=lngp[0:64, j:j + 1], s=lngb[0:64, 2 * j:2 * j + 1]:
              e.tensor_copy(d, s))
            V(lambda e, d=lngp[64:128, j:j + 1], s=lngb[64:128, 2 * j + 1:2 * j + 2]:
              e.tensor_copy(d, s), (1 if j == 7 else 0))
        pb.wait("scalar", pb.tok("vector"))
        SC(lambda e, d=g128, s=lngp: e.activation(d, s, AF.Exp, scale=128.0), 1)
        c127 = scr2[:, 8:9]
        G(lambda e, d=c127: e.iota(d, pattern=[[0, 1]], base=127, channel_multiplier=-1,
                                   allow_small_or_imprecise_dtypes=True), 1)
        pb.wait("vector", pb.tok("gpsimd"))
        V(lambda e, d=gvcol, a=c127.broadcast_to([128, 16]), b=lngb:
          e.tensor_tensor(d, a, b, op=ALU.mult), 1)
        pb.wait("scalar", pb.tok("vector"))
        SC(lambda e, d=gvcol, s=gvcol: e.activation(d, s, AF.Exp), 1)
        row_reset()
        pb.barrier()

        DMA("sync", xsb.rearrange("p (t d) -> p t d", d=1024),
            dx.rearrange("(t p) d -> p t d", p=128))
        DMA("sync", m1c, dm1)
        DMA("sync", m2c, dm2)
        pb.barrier()

        def phase_attn_a():
            for t in range(NT):
                emit_stats(t, mcol[:, t:t + 1], rcol[:, t:t + 1])
                emit_xnt_bf(t, mcol[:, t:t + 1], rcol[:, t:t + 1])
            pb.barrier()
            tap(0, nT(0), 1024)
            tap(3072, mcol, 16)
            tap(3088, rcol, 16)
            for g in range(2):
                row_reset()
                ab_reset()
                qst = ab(8192).rearrange("p (t n) -> p t n", n=512)
                aw_reset()
                wstage = aw(2048).rearrange("p (j n) -> p j n", n=256)
                wbfa = aw(2048).bitcast(BF16).rearrange("p (v j n) -> p v j n", v=2, n=256)
                wbfb = aw(2048).bitcast(BF16).rearrange("p (v j n) -> p v j n", v=2, n=256)
                slices = [(dWq, dbq, 0), (dWq, dbq, 1), (dWk, dbk, 0), (dWk, dbk, 1)]
                rowsum_slc = {0: (4, 0), 1: (4, 256), 2: (5, 0), 3: (5, 256)}
                slice_cons = {}
                spl_prev = {}
                klist = []
                for si_, (wd, bd, half) in enumerate(slices):
                    wsl_bf = wbfa if si_ % 2 == 0 else wbfb
                    c0 = g * 512 + half * 256
                    pb.wait("sync", spl_prev.get(si_ - 1))
                    ld = DMA("sync", wstage,
                             wd.rearrange("(j p) o -> p j o", p=128)[:, :, c0:c0 + 256])
                    br = scr2[0:1, 768:1024]
                    pb.wait("sync", slice_cons.get(si_ - 1))
                    ldb = DMA("sync", br, bd[:, c0:c0 + 256])
                    spl = split_w(wstage.rearrange("p j n -> p (j n)"),
                                  wsl_bf[:, 0].rearrange("p j n -> p (j n)"),
                                  wsl_bf[:, 1].rearrange("p j n -> p (j n)"),
                                  ld, cons=(slice_cons.get(si_ - 2),) if si_ >= 2 else ())
                    spl_prev[si_] = spl
                    pb.wait("tensor", spl)
                    pb.wait("tensor", ldb)
                    rb, rc = rowsum_slc[si_]
                    if si_ == 0:
                        pb.wait("tensor", bank_free[4])
                        pb.wait("tensor", bank_free[5])
                    last = None
                    for t in range(NT):
                        bk = 2 if t % 2 == 0 else 3
                        pb.wait("tensor", bank_free[bk])
                        emit_proj3(t, pp[bk][:, 0:256], wsl_bf[:, 0], wsl_bf[:, 1])
                        biasfold(pp[bk][:, 0:256], br, stop=True, inc=1)
                        mtok = pb.tok("tensor")
                        pb.wait("scalar", mtok)
                        if si_ < 2:
                            dst = qst[:, t, half * 256:half * 256 + 256]
                        else:
                            dst = scr[:, 0:256] if t % 2 == 0 else scr[:, 256:512]
                            if len(klist) >= 2:
                                for tk_ in klist[-2]:
                                    pb.wait("scalar", tk_)
                        SC(lambda e, d=dst, s=pp[bk][:, 0:256]:
                           e.activation(d, s, AF.Sigmoid), 1)
                        etok = pb.tok("scalar")
                        bank_free[bk] = etok
                        if si_ >= 2:
                            pb.wait("sync", etok)
                            kdma = DMA("sync",
                                       sp_k[:, 512 * t + (si_ - 2) * 256:512 * t + (si_ - 2) * 256 + 256],
                                       dst)
                        pb.wait("tensor", etok)
                        last = mm(pp[rb][0:1, rc:rc + 256], onescol, dst,
                                  t == 0, t == NT - 1, inc=(1 if t == NT - 1 else 0))
                        if si_ >= 2:
                            klist.append((pb.tok("tensor"), kdma))
                    slice_cons[si_] = pb.tok("tensor")
                rtok = pb.tok("tensor")
                pb.wait("vector", rtok)
                rqk = row(1024)
                V(lambda e, d=rqk[0:1, 0:512], s=pp[4][0:1, 0:512]: e.tensor_copy(d, s))
                V(lambda e, d=rqk[0:1, 512:1024], s=pp[5][0:1, 0:512]:
                  e.tensor_copy(d, s), 1)
                bank_free[4] = pb.tok("vector"); bank_free[5] = bank_free[4]
                pb.wait("gpsimd", bank_free[4])
                pb.xwait("gpsimd", ("gpsimd", pb.cnt["gpsimd"]))
                ob = pb.dma("gpsimd", lambda e, d=cc1i, s=rqk: e.dma_start(out=d, in_=s))
                pb.xwait("gpsimd", ob)
                cct = collective(cc1i, cc1o)
                pb.xwait("gpsimd", cct)
                rbk = pb.dma("gpsimd", lambda e, d=rqk, s=cc1o: e.dma_start(out=d, in_=s))
                pb.wait("vector", rbk)
                V(lambda e, d=rqk[0:1, 512:1024], s=rqk[0:1, 512:1024]:
                  e.tensor_scalar_add(d, s, EPSA))
                V(lambda e, d=rqk[0:1, 0:512], s=rqk[0:1, 0:512]:
                  e.tensor_scalar_add(d, s, EPSA), 1)
                er1 = scr2[0:1, 16:24]; er2 = scr2[0:1, 24:32]
                V(lambda e, d=er1, s=rqk[0:1, 512:1024].rearrange("o (h d) -> o h d", d=64):
                  e.reduce_sum(d, s, axis=AX))
                V(lambda e, d=er2, s=rqk[0:1, 0:512].rearrange("o (h d) -> o h d", d=64):
                  e.reduce_sum(d, s, axis=AX))
                V(lambda e, d=er1, s=er1: e.tensor_scalar_mul(d, s, EPSA))
                V(lambda e, d=er2, s=er2: e.tensor_scalar_mul(d, s, EPSA), 1)
                pb.wait("tensor", pb.tok("vector"))
                bcast_row(rqk[0:1, 512:1024], kapb, 512, bank=2)
                bcast_row(rqk[0:1, 0:512], kapb2, 512, bank=3)
                bcast_row(er1, eksi, 8, bank=2)
                bcast_row(er2, ekso, 8, bank=3)
                if g == 0:
                    tap(1024, qst[:, 0, :], 512)
                    tap(1536, kapb, 512)
                    tap(2048, kapb2, 512)

                pb.dmaq("vector"); pb.dmaq("scalar"); pb.dmaq("tensor")
                sf = sfA if g == 0 else sfB
                sfv = sf.rearrange("p (t h) -> p t h", h=8)
                aw_reset()
                krb = [aw(512), aw(512)]
                rdt = {}
                vcons = {}
                rdt[0] = DMA("sync", krb[0], sp_k[:, 0:512])
                rdt[1] = DMA("sync", krb[1], sp_k[:, 512:1024])
                pb.wait("tensor", bank_free[4]); pb.wait("tensor", bank_free[5])
                for t in range(NT):
                    pb.tick()
                    ktile = krb[t % 2]
                    pb.wait("vector", rdt[t])
                    qtile = qst[:, t, :]
                    tq = scr[:, 0:512]
                    V(lambda e, d=tq, a=qtile, b=kapb: e.tensor_mul(d, a, b))
                    th8 = scr2[:, 32:40]
                    V(lambda e, d=th8, s=tq.rearrange("p (h d) -> p h d", d=64):
                      e.reduce_sum(d, s, axis=AX))
                    V(lambda e, d=th8, a=th8, b=eksi: e.tensor_add(d, a, b))
                    V(lambda e, d=sfv[:, t, :], s=th8: e.reciprocal(d, s))
                    tk2 = scr[:, 512:1024]
                    V(lambda e, d=tk2, a=ktile, b=kapb2: e.tensor_mul(d, a, b))
                    th8b = scr2[:, 40:48]
                    V(lambda e, d=th8b, s=tk2.rearrange("p (h d) -> p h d", d=64):
                      e.reduce_sum(d, s, axis=AX))
                    so8 = scr2[:, 48:56]
                    V(lambda e, d=th8b, a=th8b, b=ekso: e.tensor_add(d, a, b))
                    V(lambda e, d=so8, s=th8b: e.reciprocal(d, s))
                    kso = scr[:, 512:1024]
                    ttb(kso, ktile, so8, 8, ALU.mult)
                    qsi = scr[:, 0:512]
                    ttb(qsi, qtile, sfv[:, t, :], 8, ALU.mult, 1)
                    vtk = pb.tok("vector")
                    vcons[t] = vtk
                    pb.wait("tensor", vtk)
                    mm(pp[4][0:1, 0:512], onescol, kso, t == 0, t == NT - 1)
                    mm(pp[5][0:1, 0:512], onescol, qsi, t == 0, t == NT - 1,
                       inc=(1 if t == NT - 1 else 0))
                    if t + 2 < NT:
                        pb.wait("sync", vcons[t])
                        rdt[t + 2] = DMA("sync", krb[t % 2],
                                         sp_k[:, 512 * (t + 2):512 * (t + 3)])
                rtok = pb.tok("tensor")
                pb.wait("vector", rtok)
                rab = rqk
                V(lambda e, d=rab[0:1, 0:512], s=pp[4][0:1, 0:512]: e.tensor_copy(d, s))
                V(lambda e, d=rab[0:1, 512:1024], s=pp[5][0:1, 0:512]:
                  e.tensor_copy(d, s), 1)
                bank_free[4] = pb.tok("vector"); bank_free[5] = bank_free[4]
                pb.wait("gpsimd", bank_free[4])
                pb.xwait("gpsimd", ("gpsimd", pb.cnt["gpsimd"]))
                ob = pb.dma("gpsimd", lambda e, d=cc1i, s=rab: e.dma_start(out=d, in_=s))
                pb.xwait("gpsimd", ob)
                cct = collective(cc1i, cc1o)
                pb.xwait("gpsimd", cct)
                rbk = pb.dma("gpsimd", lambda e, d=rab, s=cc1o: e.dma_start(out=d, in_=s))
                pb.wait("vector", rbk)
                V(lambda e, d=rab[0:1, 0:512], s=rab[0:1, 0:512]:
                  e.tensor_scalar_add(d, s, EPSA))
                V(lambda e, d=rab[0:1, 512:1024], s=rab[0:1, 512:1024]:
                  e.tensor_scalar_add(d, s, EPSA), 1)
                V(lambda e, d=er1, s=rab[0:1, 0:512].rearrange("o (h d) -> o h d", d=64):
                  e.reduce_sum(d, s, axis=AX))
                V(lambda e, d=er2, s=rab[0:1, 512:1024].rearrange("o (h d) -> o h d", d=64):
                  e.reduce_sum(d, s, axis=AX))
                V(lambda e, d=er1, s=er1: e.tensor_scalar_mul(d, s, EPSA))
                V(lambda e, d=er2, s=er2: e.tensor_scalar_mul(d, s, EPSA), 1)
                pb.wait("tensor", pb.tok("vector"))
                bcast_row(rab[0:1, 0:512], kapb, 512, bank=2)
                bcast_row(rab[0:1, 512:1024], kapb2, 512, bank=3)
                bcast_row(er1, eksi, 8, bank=2)
                bcast_row(er2, ekso, 8, bank=3)

                if g == 0:
                    tap(2560, sfv[:, 0, :], 8)
                    tap(2880, rab, 1024)
                pb.dmaq("vector"); pb.dmaq("scalar"); pb.dmaq("tensor")
                aw_reset()
                row_reset()
                pb.wait("sync", pb.tok("tensor"))
                wv0b = aw(2048).bitcast(BF16).rearrange("p (v j n) -> p v j n", v=2, n=256)
                wv1b = aw(2048).bitcast(BF16).rearrange("p (v j n) -> p v j n", v=2, n=256)
                krb = [aw(512), aw(512)]
                vstage = aw(1024).rearrange("p (j n) -> p j n", n=256)
                dwvr = dWv.rearrange("(j p) o -> p j o", p=128)
                spv = None
                for half in range(2):
                    wvb = wv0b if half == 0 else wv1b
                    cw = g * 512 + half * 256
                    for ch in range(2):
                        pb.wait("sync", spv)
                        ldc = DMA("sync", vstage,
                                  dwvr[:, 4 * ch:4 * ch + 4, cw:cw + 256])
                        spv = split_w(
                            vstage.rearrange("p j n -> p (j n)"),
                            wvb[:, 0, 4 * ch:4 * ch + 4, :].rearrange("p j n -> p (j n)"),
                            wvb[:, 1, 4 * ch:4 * ch + 4, :].rearrange("p j n -> p (j n)"),
                            ldc)
                brv = row(512)
                ldb = DMA("sync", brv, dbv[:, g * 512:g * 512 + 512])
                pb.wait("tensor", spv); pb.wait("tensor", ldb)
                evv = eAm.rearrange("p (t h) -> p t h", h=8)
                kvsb = kvsbm
                V(lambda e, d=kvsb: e.memset(d, 0.0), 1)
                rdt = {}
                rdt[0] = DMA("sync", krb[0], sp_k[:, 0:512])
                rdt[1] = DMA("sync", krb[1], sp_k[:, 512:1024])
                pb.wait("tensor", bank_free[6]); pb.wait("tensor", bank_free[7])
                kvcons = {}
                mtoks = {}
                etoks = {}

                def a3_A(t):
                    p = t % 2
                    ktile = krb[p]
                    blk_ = scr if p == 0 else scr2
                    sm_ = smA[:, 32 + 32 * p:32 + 32 * p + 24]
                    pb.wait("vector", rdt[t])
                    for tk_ in kvcons.get(t - 2, ()):
                        pb.wait("vector", tk_)
                    qtile = qst[:, t, :]
                    tq = blk_[:, 0:512]
                    V(lambda e, d=tq, a=qtile, b=kapb: e.tensor_mul(d, a, b))
                    th8 = sm_[:, 0:8]
                    V(lambda e, d=th8, s=tq.rearrange("p (h d) -> p h d", d=64):
                      e.reduce_sum(d, s, axis=AX))
                    V(lambda e, d=th8, a=th8, b=eksi: e.tensor_add(d, a, b), 1)
                    pb.wait("scalar", pb.tok("vector"))
                    sa8 = sm_[:, 8:16]
                    SC(lambda e, d=sa8, s=th8: e.activation(d, s, AF.Sigmoid), 1)
                    pb.wait("vector", pb.tok("scalar"))
                    V(lambda e, d=sfv[:, t, :], a=sfv[:, t, :], b=sa8: e.tensor_mul(d, a, b))
                    tk2 = blk_[:, 0:512]
                    V(lambda e, d=tk2, a=ktile, b=kapb2: e.tensor_mul(d, a, b))
                    th8b = sm_[:, 16:24]
                    V(lambda e, d=th8b, s=tk2.rearrange("p (h d) -> p h d", d=64):
                      e.reduce_sum(d, s, axis=AX))
                    V(lambda e, d=th8b, a=th8b, b=ekso: e.tensor_add(d, a, b))
                    V(lambda e, d=th8b, a=th8b: e.tensor_scalar(d, a, -1.0, 1.0,
                                                                op0=ALU.max, op1=ALU.min), 1)
                    pb.wait("scalar", pb.tok("vector"))
                    SC(lambda e, d=evv[:, t, :], s=th8b: e.activation(d, s, AF.Exp), 1)
                    etoks[t] = pb.tok("scalar")
                    hts = []
                    for half in range(2):
                        bk = (2 + half) if p == 0 else (4 + half)
                        wvb = wv0b if half == 0 else wv1b
                        pb.wait("tensor", bank_free[bk])
                        emit_proj3(t, pp[bk][:, 0:256], wvb[:, 0], wvb[:, 1])
                        biasfold(pp[bk][:, 0:256], brv[0:1, 256 * half:256 * half + 256],
                                 stop=True, inc=1)
                        hts.append(pb.tok("tensor"))
                    mtoks[t] = hts

                def a3_B(t):
                    p = t % 2
                    ktile = krb[p]
                    blk_ = scr if p == 0 else scr2
                    pb.wait("tensor", etoks[t])
                    mm(pp[7][0:1, 0:8], onescol, evv[:, t, :], t == 0, t == NT - 1)
                    vs = []
                    for half in range(2):
                        bk = (2 + half) if p == 0 else (4 + half)
                        pb.wait("vector", mtoks[t][half])
                        pb.wait("vector", etoks[t])
                        vsl = blk_[:, 512 + 256 * half:512 + 256 * half + 256]
                        ttb(vsl, pp[bk][:, 0:256], evv[:, t, 4 * half:4 * half + 4],
                            4, ALU.mult, 1)
                        vtok2 = pb.tok("vector")
                        bank_free[bk] = vtok2
                        vs.append((vsl, vtok2))
                    pb.wait("tensor", bank_free[6])
                    for half in range(2):
                        vsl, vtok2 = vs[half]
                        pb.wait("tensor", vtok2)
                        for hh in range(4):
                            h = half * 4 + hh
                            dst = pp[6][(h % 2) * 64:(h % 2) * 64 + 64,
                                        128 * half + (hh // 2) * 64:128 * half + (hh // 2) * 64 + 64]
                            mm(dst, ktile[:, 64 * h:64 * h + 64], vsl[:, 64 * hh:64 * hh + 64],
                               True, True, inc=(1 if hh == 3 else 0))
                    kvt = pb.tok("tensor")
                    pb.wait("vector", kvt)
                    V(lambda e, d=kvsb, a=pp[6][:, 0:256], b=kvsb: e.tensor_add(d, a, b), 1)
                    bank_free[6] = pb.tok("vector")
                    kvcons[t] = (kvt, bank_free[6])
                    if t + 2 < NT:
                        pb.wait("sync", kvt)
                        rdt[t + 2] = DMA("sync", krb[p], sp_k[:, 512 * (t + 2):512 * (t + 3)])

                a3_A(0)
                for t in range(NT):
                    if t + 1 < NT:
                        a3_A(t + 1)
                    a3_B(t)
                ktok = pb.tok("tensor")
                pb.wait("vector", ktok)
                sstage = scr2[:, 0:264]
                V(lambda e, d=sstage[:, 256:264]: e.memset(d, 0.0))
                V(lambda e, d=sstage[:, 0:256], s=kvsb: e.tensor_copy(d, s))
                V(lambda e, d=sstage[0:1, 256:264], s=pp[7][0:1, 0:8]:
                  e.tensor_copy(d, s), 1)
                bank_free[6] = pb.tok("vector"); bank_free[7] = bank_free[6]
                pb.wait("gpsimd", bank_free[6])
                pb.xwait("gpsimd", ("gpsimd", pb.cnt["gpsimd"]))
                ob = pb.dma("gpsimd", lambda e, d=cc3i, s=sstage: e.dma_start(out=d, in_=s))
                pb.xwait("gpsimd", ob)
                cct = collective(cc3i, cc3o)
                pb.xwait("gpsimd", cct)
                pb.wait("gpsimd", pb.tok("vector"))
                pb.wait("gpsimd", pb.tok("tensor"))
                kvg = kapb[:, 0:256]
                rb1 = pb.dma("gpsimd", lambda e, d=kvg, s=cc3o[:, 0:256]:
                             e.dma_start(out=d, in_=s))
                esr = scr2[0:1, 16:24]
                rb2 = pb.dma("gpsimd", lambda e, d=esr, s=cc3o[0:1, 256:264]:
                             e.dma_start(out=d, in_=s))
                pb.wait("vector", rb2)
                if g == 0:
                    tap(2848, esr, 8)
                V(lambda e, d=esr, s=esr: e.reciprocal(d, s))
                V(lambda e, d=esr, s=esr: e.tensor_scalar_mul(d, s, float(L)), 1)
                pb.wait("tensor", pb.tok("vector"))
                bcast_row(esr, scb, 8, bank=2)
                if g == 0:
                    tap(2568, evv[:, 0, :], 8)
                    tap(2576, kvg, 256)
                    tap(2832, scb, 8)
                    tap(2840, sfv[:, 0, :], 8)

                pb.dmaq("vector"); pb.dmaq("scalar"); pb.dmaq("tensor")
                aw_reset()
                row_reset()
                pb.wait("sync", pb.tok("tensor"))
                wo_sl = aw(4096).rearrange("p (j n) -> p j n", n=1024)
                ldo = DMA("sync", wo_sl,
                          dWo.rearrange("(j p) o -> p j o", p=128)[:, 4 * g:4 * g + 4, :])
                bor = row(1024) if g == 0 else None
                pb.wait("tensor", ldo)
                if g == 0:
                    ldb2 = DMA("sync", bor, dbo)
                    pb.wait("tensor", ldb2)
                pb.wait("vector", rb1)
                a4cons = {}
                a4ev = {}

                def a4_A(t):
                    p = t % 2
                    qe = scr[:, 0:512] if p == 0 else scr2[:, 0:512]
                    qeT = scr[:, 512:1024] if p == 0 else scr2[:, 512:1024]
                    trb = 0 if p == 0 else 6
                    otb = 1 if p == 0 else 7
                    f2 = smA[:, 8 * p:8 * p + 8]
                    for tk_ in a4cons.get(t - 2, ()):
                        pb.wait("vector", tk_)
                    V(lambda e, d=f2, a=sfv[:, t, :], b=scb: e.tensor_mul(d, a, b))
                    ttb(qe, qst[:, t, :], f2, 8, ALU.mult, 1)
                    vtok3 = pb.tok("vector")
                    pb.wait("tensor", vtok3)
                    pb.wait("tensor", bank_free[trb])
                    for j in range(4):
                        tr(pp[trb][:, 128 * j:128 * (j + 1)], qe[:, 128 * j:128 * (j + 1)],
                           inc=(1 if j == 3 else 0))
                    trt = pb.tok("tensor")
                    pb.wait("scalar", trt)
                    for tk_ in a4cons.get(t - 2, ()):
                        pb.wait("scalar", tk_)
                    for j in range(4):
                        SC(lambda e, d=qeT[:, 128 * j:128 * (j + 1)],
                           s=pp[trb][:, 128 * j:128 * (j + 1)]: e.copy(d, s),
                           inc=(1 if j == 3 else 0))
                    et = pb.tok("scalar")
                    bank_free[trb] = et
                    pb.wait("tensor", et)
                    pb.wait("tensor", bank_free[otb])
                    for h in range(8):
                        kv_h = kvg[(h % 2) * 64:(h % 2) * 64 + 64,
                                   (h // 2) * 64:(h // 2) * 64 + 64]
                        qh = qeT[(h % 2) * 64:(h % 2) * 64 + 64,
                                 128 * (h // 2):128 * (h // 2) + 128]
                        dst = pp[otb][(h % 2) * 64:(h % 2) * 64 + 64,
                                      128 * (h // 2):128 * (h // 2) + 128]
                        mm(dst, kv_h, qh, True, True, inc=(1 if h == 7 else 0))
                    a4ev[t] = pb.tok("tensor")

                def a4_B(t):
                    p = t % 2
                    qeT = scr[:, 512:1024] if p == 0 else scr2[:, 512:1024]
                    otb = 1 if p == 0 else 7
                    outT = qeT
                    pb.wait("vector", a4ev[t])
                    V(lambda e, d=outT, s=pp[otb][:, 0:512]: e.tensor_copy(d, s), 1)
                    ovt = pb.tok("vector")
                    bank_free[otb] = ovt
                    pb.wait("tensor", ovt)
                    bk2 = (2, 3) if p == 0 else (4, 5)
                    pb.wait("tensor", bank_free[bk2[0]])
                    pb.wait("tensor", bank_free[bk2[1]])
                    for half in range(2):
                        for j in range(4):
                            mm(pp[bk2[half]][:, 0:512], outT[:, 128 * j:128 * (j + 1)],
                               wo_sl[:, j, 512 * half:512 * half + 512],
                               j == 0, g == 1 and j == 3,
                               inc=(1 if (g == 1 and j == 3 and half == 1) else 0))
                        if g == 0:
                            biasfold(pp[bk2[half]][:, 0:512],
                                     bor[0:1, 512 * half:512 * half + 512],
                                     stop=True, inc=(1 if half == 1 else 0))
                    wt = pb.tok("tensor")
                    pb.wait("vector", wt)
                    for half in range(2):
                        V(lambda e, d=xt(t)[:, 512 * half:512 * half + 512],
                          a=pp[bk2[half]][:, 0:512],
                          b=xt(t)[:, 512 * half:512 * half + 512]:
                          e.tensor_add(d, a, b), (1 if half == 1 else 0))
                    xvt = pb.tok("vector")
                    bank_free[bk2[0]] = xvt
                    bank_free[bk2[1]] = xvt
                    a4cons[t] = (wt, xvt)

                a4_A(0)
                for t in range(NT):
                    if t + 1 < NT:
                        a4_A(t + 1)
                    a4_B(t)
                pb.barrier()

        def phase_ffn():
            row_reset()
            ab_reset()
            aw_reset()
            eb_sb = aw(1024)
            ldeb = DMA("sync", eb_sb[0:8, 0:1024], deb)
            gwsl = aw(64).rearrange("p (j n) -> p j n", n=8)
            ldgw = DMA("sync", gwsl, dgw.rearrange("(j p) o -> p j o", p=128))
            gbr = row(8)
            ldgb = DMA("sync", gbr, dgb)
            gwb = aw(64).bitcast(BF16).rearrange("p (v j n) -> p v j n", v=2, n=8)
            pb.wait("scalar", ldgw)
            SC(lambda e, d=gwb[:, 0], s=gwsl: e.copy(d, s), 1)
            pb.wait("vector", pb.tok("scalar"))
            V(lambda e, d=gwb[:, 1], a=gwsl, c=gwb[:, 0]:
              e.scalar_tensor_tensor(d, a, 1.0, c, op0=ALU.mult, op1=ALU.subtract), 1)
            gsp = pb.tok("vector")
            gatv = gat.rearrange("p (t h) -> p t h", h=8)
            for t in range(NT):
                pb.tick()
                emit_stats(t, mcol[:, t:t + 1], rcol[:, t:t + 1])
                stok = emit_xnt_bf(t, mcol[:, t:t + 1], rcol[:, t:t + 1])
                if t == 0:
                    pb.wait("tensor", ldeb)
                    pb.wait("tensor", ldgb)
                    pb.wait("tensor", gsp)
                pb.wait("tensor", stok)
                pb.wait("tensor", bank_free[2])
                for j in range(8):
                    mm(pp[2][:, 0:8], nTh(t)[:, 128 * j:128 * (j + 1)], gwb[:, 0, j, :],
                       j == 0, False)
                    mm(pp[2][:, 0:8], nTh(t)[:, 128 * j:128 * (j + 1)], gwb[:, 1, j, :],
                       False, False)
                    mm(pp[2][:, 0:8], nTl(t)[:, 128 * j:128 * (j + 1)], gwb[:, 0, j, :],
                       False, False)
                biasfold(pp[2][:, 0:8], gbr, stop=True, inc=1)
                gt = pb.tok("tensor")
                pb.wait("vector", gt)
                lg = scr2[:, 32:40]
                V(lambda e, d=lg, s=pp[2][:, 0:8]: e.tensor_copy(d, s), 1)
                bank_free[2] = pb.tok("vector")
                mx = scr2[:, 40:41]
                V(lambda e, d=mx, s=lg: e.reduce_max(d, s, axis=AX))
                V(lambda e, d=mx, s=mx: e.tensor_scalar_mul(d, s, -1.0), 1)
                pb.wait("scalar", pb.tok("vector"))
                eg = scr2[:, 41:49]
                SC(lambda e, d=eg, s=lg, b=mx: e.activation(d, s, AF.Exp, bias=b), 1)
                pb.wait("vector", pb.tok("scalar"))
                sm = scr2[:, 49:50]
                V(lambda e, d=sm, s=eg: e.reduce_sum(d, s, axis=AX))
                V(lambda e, d=sm, s=sm: e.reciprocal(d, s))
                gg = scr2[:, 50:58]
                V(lambda e, d=gg, a=eg, b=sm: e.tensor_scalar_mul(d, a, b))
                sel = scr2[:, 58:66]
                V(lambda e, d=sel, s=gg: e.tensor_single_scalar(d, s, 0.1, op=ALU.is_ge))
                V(lambda e, d=gg, a=gg, b=sel: e.tensor_mul(d, a, b))
                ws = scr2[:, 66:67]
                V(lambda e, d=ws, s=gg: e.reduce_sum(d, s, axis=AX))
                V(lambda e, d=ws, s=ws: e.reciprocal(d, s))
                V(lambda e, d=gatv[:, t, :], a=gg, b=ws: e.tensor_scalar_mul(d, a, b), 1)
                gvt = pb.tok("vector")
                pb.wait("tensor", gvt)
                pb.wait("tensor", bank_free[3])
                tr(pp[3][0:8, 0:128], gatv[:, t, :], inc=1)
                wtt = pb.tok("tensor")
                pb.wait("scalar", wtt)
                wT = scr2[:, 128:256]
                SC(lambda e, d=wT[0:8, 0:128], s=pp[3][0:8, 0:128]: e.copy(d, s), 1)
                stt = pb.tok("scalar")
                bank_free[3] = stt
                pb.wait("tensor", stt)
                bk2 = (4, 5) if t % 2 == 0 else (6, 7)
                pb.wait("tensor", bank_free[bk2[0]]); pb.wait("tensor", bank_free[bk2[1]])
                for half in range(2):
                    mm(pp[bk2[half]][:, 0:512], wT[0:8, 0:128],
                       eb_sb[0:8, 512 * half:512 * half + 512], True, True,
                       inc=(1 if half == 1 else 0))
                wbt = pb.tok("tensor")
                pb.wait("vector", wbt)
                for half in range(2):
                    V(lambda e, d=xt(t)[:, 512 * half:512 * half + 512],
                      a=pp[bk2[half]][:, 0:512], b=xt(t)[:, 512 * half:512 * half + 512]:
                      e.tensor_add(d, a, b), (1 if half == 1 else 0))
                bank_free[bk2[0]] = pb.tok("vector")
                bank_free[bk2[1]] = bank_free[bk2[0]]
            pb.barrier()
            stage_w = aB[:, 0:2048].rearrange("p (j n) -> p j n", n=256)
            wqb = [aw(2048).bitcast(BF16).rearrange("p (v j n) -> p v j n", v=2, n=256)
                   for _ in range(2)]
            order = [(e_, q_) for e_ in range(8) for q_ in range(4)]
            loads = {}
            splits = {}
            mmdone = {}
            dre = dew.rearrange("e (j p) o -> e p j o", p=128)

            def f1_load(idx):
                e_, q_ = order[idx]
                pb.wait("sync", splits.get(idx - 1))
                loads[idx] = DMA("sync", stage_w, dre[e_][:, :, 256 * q_:256 * q_ + 256])

            def f1_split(idx):
                wb = wqb[idx % 2]
                pb.wait("scalar", loads[idx])
                pb.wait("scalar", mmdone.get(idx - 2))
                SC(lambda e, d=wb[:, 0].rearrange("p j n -> p (j n)"),
                   s=stage_w.rearrange("p j n -> p (j n)"): e.copy(d, s), 1)
                hs = pb.tok("scalar")
                pb.wait("vector", hs)
                pb.wait("vector", mmdone.get(idx - 2))
                V(lambda e, d=wb[:, 1].rearrange("p j n -> p (j n)"),
                  a=stage_w.rearrange("p j n -> p (j n)"),
                  c=wb[:, 0].rearrange("p j n -> p (j n)"):
                  e.scalar_tensor_tensor(d, a, 1.0, c, op0=ALU.mult, op1=ALU.subtract), 1)
                splits[idx] = pb.tok("vector")

            f1_load(0)
            f1_split(0)
            f1_load(1)
            for idx, (e_, q_) in enumerate(order):
                wb = wqb[idx % 2]
                if idx + 1 < len(order):
                    f1_split(idx + 1)
                if idx + 2 < len(order):
                    f1_load(idx + 2)
                pb.wait("tensor", splits[idx])
                for t in range(NT):
                    bk = 2 if (idx * NT + t) % 2 == 0 else 3
                    pb.wait("tensor", bank_free[bk])
                    emit_proj3(t, pp[bk][:, 0:256], wb[:, 0], wb[:, 1], inc=1)
                    pt = pb.tok("tensor")
                    pb.wait("vector", pt)
                    c0 = 256 * q_
                    V(lambda e, d=xt(t)[:, c0:c0 + 256], a=pp[bk][:, 0:256],
                      s=gatv[:, t, e_:e_ + 1], b=xt(t)[:, c0:c0 + 256]:
                      e.scalar_tensor_tensor(d, a, s, b, op0=ALU.mult, op1=ALU.add), 1)
                    bank_free[bk] = pb.tok("vector")
                mmdone[idx] = pb.tok("tensor")
            pb.barrier()

        def phase_attn_b():
            row_reset()
            DMA("sync", sp_x2, xsb)
            for t in range(NT):
                pb.tick()
                var = emit_stats(t, mcol[:, t:t + 1], rcol[:, t:t + 1])
                tv2 = scr2[:, 16:17]
                V(lambda e, d=tv2, s=var: e.tensor_scalar_add(d, s, 1e-5))
                V(lambda e, d=tv2, s=tv2: e.reciprocal(d, s))
                V(lambda e, d=tv2, a=var, b=tv2: e.tensor_mul(d, a, b))
                r1 = scr2[:, 17:18]
                rsqrt_newton(r1, tv2, 1e-5, scr2[:, 18:19], scr2[:, 19:20])
                rt_ = scr2[:, 20:21]
                V(lambda e, d=rt_, a=rcol[:, t:t + 1], b=r1: e.tensor_mul(d, a, b), 1)
                emit_xnt_bf(t, mcol[:, t:t + 1], rt_)
            pb.barrier()
            l1r = row(1024)
            ld1 = DMA("sync", l1r, dln1w)
            pb.wait("tensor", ld1)
            for j in range(8):
                row2col(l1r[0:1, 128 * j:128 * (j + 1)], l1wc[:, j:j + 1], 128, bank=2)
            pb.wait("sync", pb.tok("tensor"))
            ld2 = DMA("sync", l1r, dln1b)
            pb.wait("tensor", ld2)
            for j in range(8):
                row2col(l1r[0:1, 128 * j:128 * (j + 1)], l1bc[:, j:j + 1], 128, bank=2)
            pb.barrier()

            for g in range(4):
                ab_reset()
                sprefix = ab(2048).rearrange("p (t n) -> p t n", n=128)
                rbs = [ab(512), ab(512)]
                S_run = ab(128)
                c0 = 256 * g
                dmv = dmask.rearrange("p (h n) -> p h n", n=128)
                for i in range(4):
                    h = 4 * g + i
                    G(lambda e, d=dmv[:, i, :]: e.iota(
                        d, pattern=[[1, 128]], base=0, channel_multiplier=-1,
                        allow_small_or_imprecise_dtypes=True), 1)
                    pb.wait("scalar", pb.tok("gpsimd"))
                    SC(lambda e, d=dmv[:, i, :], s=dmv[:, i, :], sc=lngb[:, h:h + 1]:
                       e.activation(d, s, AF.Exp, scale=sc), 1)
                    pb.wait("gpsimd", pb.tok("scalar"))
                    G(lambda e, d=dmv[:, i, :]: e.affine_select(
                        out=d, in_=d, compare_op=ALU.is_ge, fill=0.0, base=0,
                        pattern=[[1, 128]], channel_multiplier=-1), 1)
                gqv = gqmat.rearrange("p (c n) -> p c n", n=128)
                iof = scr2[:, 0:128]
                G(lambda e, d=iof: e.iota(d, pattern=[[1, 128]], base=1,
                                          channel_multiplier=0,
                                          allow_small_or_imprecise_dtypes=True), 1)
                pb.wait("scalar", pb.tok("gpsimd"))
                for c_ in range(2):
                    pr = 2 * g + c_
                    SC(lambda e, d=gqv[:, c_, :], s=iof, sc=lngp[:, pr:pr + 1]:
                       e.activation(d, s, AF.Exp, scale=sc), 1)
                pb.barrier()

                aw_reset()
                pb.wait("sync", pb.tok("tensor"))
                wk_b = aw(2048).bitcast(BF16).rearrange("p (v j n) -> p v j n", v=2, n=256)
                wv_b = aw(2048).bitcast(BF16).rearrange("p (v j n) -> p v j n", v=2, n=256)
                bstage = aB[:, 5248:7296].rearrange("p (j n) -> p j n", n=256)
                row_reset()
                ckr = row(256)
                cvr = row(256)
                bspl = None
                for wi_, (wd_, wb_, crow_, cc_) in enumerate(
                        [(dbwk, wk_b, ckr, slice(0, 256)), (dbwv, wv_b, cvr, slice(256, 512))]):
                    pb.wait("sync", bspl)
                    ldw_ = DMA("sync", bstage,
                               wd_.rearrange("(j p) o -> p j o", p=128)[:, :, c0:c0 + 256])
                    pb.wait("tensor", ldw_)
                    if wi_ == 0:
                        pb.wait("tensor", bank_free[4])
                    for j in range(8):
                        mm(pp[4][0:1, cc_], l1bc[:, j:j + 1], bstage[:, j, :],
                           wi_ == 0 and j == 0, wi_ == 1 and j == 7,
                           inc=(1 if j == 7 else 0))
                    ct_ = pb.tok("tensor")
                    pb.wait("vector", ct_)
                    V(lambda e, d=crow_, s=pp[4][0:1, cc_]: e.tensor_copy(d, s), 1)
                    if wi_ == 1:
                        bank_free[4] = pb.tok("vector")
                    for j in range(8):
                        V(lambda e, d=bstage[:, j, :], a=bstage[:, j, :], s=l1wc[:, j:j + 1]:
                          e.tensor_scalar_mul(d, a, s), (1 if j == 7 else 0))
                    wst_ = pb.tok("vector")
                    bspl = split_w(bstage.rearrange("p j n -> p (j n)"),
                                   wb_[:, 0].rearrange("p j n -> p (j n)"),
                                   wb_[:, 1].rearrange("p j n -> p (j n)"), wst_)
                pb.wait("tensor", bspl)
                V(lambda e, d=S_run: e.memset(d, 0.0), 1)
                spdma = {}
                kvcons = {}
                evtoks = {}
                prtoks = {}
                pbuf = [aB[:, 3200 + 1024 * p:3200 + 1024 * (p + 1)] for p in range(2)]

                def p1a_A(t):
                    p = t % 2
                    bk = 2 if p == 0 else 3
                    pb.wait("tensor", bank_free[bk])
                    emit_proj3(t, pp[bk][:, 0:256], wk_b[:, 0], wk_b[:, 1])
                    biasfold(pp[bk][:, 0:256], ckr, stop=True, inc=0)
                    emit_proj3(t, pp[bk][:, 256:512], wv_b[:, 0], wv_b[:, 1], start=True)
                    biasfold(pp[bk][:, 256:512], cvr, stop=True, inc=1)
                    pt = pb.tok("tensor")
                    prtoks[t] = pt
                    pb.wait("scalar", pt)
                    for tk_ in kvcons.get(t - 2, ()):
                        pb.wait("scalar", tk_)
                    ktile = pbuf[p][:, 0:256]
                    vtile = pbuf[p][:, 256:512]
                    SC(lambda e, d=ktile, s=pp[bk][:, 0:256]: e.copy(d, s), 1)
                    SC(lambda e, d=vtile, s=pp[bk][:, 256:512]: e.copy(d, s), 1)
                    ek = pb.tok("scalar")
                    bank_free[bk] = ek
                    evtoks[t] = ek

                def p1a_B(t):
                    p = t % 2
                    ktile = pbuf[p][:, 0:256]
                    vtile = pbuf[p][:, 256:512]
                    kT = pbuf[p][:, 512:768]
                    vd = pbuf[p][:, 768:1024]
                    trb = 4 if p == 0 else 6
                    cb = 5 if p == 0 else 7
                    pb.wait("tensor", evtoks[t])
                    pb.wait("tensor", bank_free[trb])
                    for c_ in range(2):
                        tr(pp[trb][:, 128 * c_:128 * (c_ + 1)],
                           ktile[:, 128 * c_:128 * (c_ + 1)], inc=(1 if c_ == 1 else 0))
                    ttk = pb.tok("tensor")
                    pb.wait("scalar", ttk)
                    for tk_ in spdma.get(t - 2, ()):
                        pb.wait("scalar", tk_)
                    for c_ in range(2):
                        SC(lambda e, d=kT[:, 128 * c_:128 * (c_ + 1)],
                           s=pp[trb][:, 128 * c_:128 * (c_ + 1)]: e.copy(d, s),
                           (1 if c_ == 1 else 0))
                    ktt = pb.tok("scalar")
                    bank_free[trb] = ktt
                    pb.wait("sync", ktt)
                    spk_ = DMA("sync", sp_b[:, 512 * t:512 * t + 256], kT)
                    spv_ = DMA("sync", sp_b[:, 512 * t + 256:512 * t + 512], vtile)
                    spdma[t] = (spk_, spv_)
                    pb.wait("vector", evtoks[t])
                    for tk_ in kvcons.get(t - 2, ()):
                        pb.wait("vector", tk_)
                    V(lambda e, d=sprefix[:, t, :], s=S_run: e.tensor_copy(d, s))
                    ttb(vd, vtile, gvcol[:, 4 * g:4 * g + 4], 4, ALU.mult, 1)
                    cvt = pb.tok("vector")
                    pb.wait("tensor", cvt)
                    pb.wait("tensor", bank_free[cb])
                    for i in range(4):
                        dst = pp[cb][(i % 2) * 64:(i % 2) * 64 + 64,
                                     (i // 2) * 64:(i // 2) * 64 + 64]
                        mm(dst, ktile[:, 64 * i:64 * i + 64], vd[:, 64 * i:64 * i + 64],
                           True, True, inc=(1 if i == 3 else 0))
                    ctk = pb.tok("tensor")
                    pb.wait("vector", ctk)
                    for c_ in range(2):
                        pr = 2 * g + c_
                        V(lambda e, d=S_run[:, 64 * c_:64 * c_ + 64],
                          a=S_run[:, 64 * c_:64 * c_ + 64], s=g128[:, pr:pr + 1]:
                          e.tensor_scalar_mul(d, a, s))
                        V(lambda e, d=S_run[:, 64 * c_:64 * c_ + 64],
                          a=pp[cb][:, 64 * c_:64 * c_ + 64],
                          b=S_run[:, 64 * c_:64 * c_ + 64]:
                          e.tensor_add(d, a, b), (1 if c_ == 1 else 0))
                    svt = pb.tok("vector")
                    bank_free[cb] = svt
                    kvcons[t] = (ctk, svt, spk_, spv_)

                p1a_A(0)
                for t in range(NT):
                    if t + 1 < NT:
                        p1a_A(t + 1)
                    p1a_B(t)
                pb.dmaq("vector")
                S_msk = scr[:, 0:128]
                V(lambda e, d=S_msk, a=S_run, s=m1c: e.tensor_scalar_mul(d, a, s), 1)
                pb.wait("gpsimd", pb.tok("vector"))
                pb.xwait("gpsimd", ("gpsimd", pb.cnt["gpsimd"]))
                ob = pb.dma("gpsimd", lambda e, d=ccsi, s=S_msk: e.dma_start(out=d, in_=s))
                pb.xwait("gpsimd", ob)
                cct = collective(ccsi, ccso)
                pb.xwait("gpsimd", cct)
                S_init = scr[:, 128:256]
                rbk = pb.dma("gpsimd", lambda e, d=S_init, s=ccso: e.dma_start(out=d, in_=s))
                pb.wait("vector", rbk)
                V(lambda e, d=S_init, a=S_init, s=m2c: e.tensor_scalar_mul(d, a, s), 1)
                gm = scr[:, 256:258]
                V(lambda e, d=gm[:, 0:1]: e.memset(d, 1.0))
                V(lambda e, d=gm[:, 1:2]: e.memset(d, 1.0))
                for t in range(NT):
                    for c_ in range(2):
                        pr = 2 * g + c_
                        if t > 0:
                            V(lambda e, d=gm[:, c_:c_ + 1], a=gm[:, c_:c_ + 1],
                              b=g128[:, pr:pr + 1]: e.tensor_mul(d, a, b))
                        V(lambda e, d=sprefix[:, t, 64 * c_:64 * c_ + 64],
                          a=S_init[:, 64 * c_:64 * c_ + 64], s=gm[:, c_:c_ + 1],
                          b=sprefix[:, t, 64 * c_:64 * c_ + 64]:
                          e.scalar_tensor_tensor(d, a, s, b, op0=ALU.mult, op1=ALU.add),
                          (1 if c_ == 1 else 0))
                pb.barrier()

                aw_reset()
                pb.wait("sync", pb.tok("tensor"))
                wq_b = aw(2048).bitcast(BF16).rearrange("p (v j n) -> p v j n", v=2, n=256)
                wg_b = aw(2048).bitcast(BF16).rearrange("p (v j n) -> p v j n", v=2, n=256)
                wo_sl = aw(2048).rearrange("p (j n) -> p j n", n=1024)
                ldo = DMA("sync", wo_sl,
                          dbwo.rearrange("(j p) o -> p j o", p=128)[:, 2 * g:2 * g + 2, :])
                cqr = row(256); cgr = row(256)
                bstage = aB[:, 5248:7296].rearrange("p (j n) -> p j n", n=256)
                bspl = None
                for wi_, (wd_, wb_, crow_, cc_) in enumerate(
                        [(dbwq, wq_b, cqr, slice(0, 256)), (dbwg, wg_b, cgr, slice(256, 512))]):
                    pb.wait("sync", bspl)
                    ldw_ = DMA("sync", bstage,
                               wd_.rearrange("(j p) o -> p j o", p=128)[:, :, c0:c0 + 256])
                    pb.wait("tensor", ldw_)
                    if wi_ == 0:
                        pb.wait("tensor", bank_free[4])
                    for j in range(8):
                        mm(pp[4][0:1, cc_], l1bc[:, j:j + 1], bstage[:, j, :],
                           wi_ == 0 and j == 0, wi_ == 1 and j == 7,
                           inc=(1 if j == 7 else 0))
                    ct_ = pb.tok("tensor")
                    pb.wait("vector", ct_)
                    V(lambda e, d=crow_, s=pp[4][0:1, cc_]: e.tensor_copy(d, s), 1)
                    if wi_ == 1:
                        bank_free[4] = pb.tok("vector")
                    for j in range(8):
                        V(lambda e, d=bstage[:, j, :], a=bstage[:, j, :], s=l1wc[:, j:j + 1]:
                          e.tensor_scalar_mul(d, a, s), (1 if j == 7 else 0))
                    wst_ = pb.tok("vector")
                    bspl = split_w(bstage.rearrange("p j n -> p (j n)"),
                                   wb_[:, 0].rearrange("p j n -> p (j n)"),
                                   wb_[:, 1].rearrange("p j n -> p (j n)"), wst_)
                pb.wait("tensor", bspl)
                pb.wait("tensor", ldo)
                rdk = {0: DMA("sync", rbs[0], sp_b[:, 0:512]),
                       1: DMA("sync", rbs[1], sp_b[:, 512:1024])}
                rcons = {}
                bcons = {}
                evtoks = {}
                qbuf = [aB[:, 3200 + 512 * p:3200 + 512 * (p + 1)] for p in range(2)]

                def p1b_A(t):
                    p = t % 2
                    bk = 2 if p == 0 else 4
                    pb.wait("tensor", bank_free[bk])
                    emit_proj3(t, pp[bk][:, 0:256], wq_b[:, 0], wq_b[:, 1])
                    biasfold(pp[bk][:, 0:256], cqr, stop=True, inc=0)
                    emit_proj3(t, pp[bk][:, 256:512], wg_b[:, 0], wg_b[:, 1], start=True)
                    biasfold(pp[bk][:, 256:512], cgr, stop=True, inc=1)
                    pt = pb.tok("tensor")
                    pb.wait("scalar", pt)
                    for tk_ in bcons.get(t - 2, ()):
                        pb.wait("scalar", tk_)
                    qtile = qbuf[p][:, 0:256]
                    sgt = qbuf[p][:, 256:512]
                    SC(lambda e, d=qtile, s=pp[bk][:, 0:256]:
                       e.activation(d, s, AF.Copy, scale=float(HD) ** -0.5))
                    SC(lambda e, d=sgt, s=pp[bk][:, 256:512]:
                       e.activation(d, s, AF.Silu), 1)
                    qst_ = pb.tok("scalar")
                    bank_free[bk] = qst_
                    evtoks[t] = qst_

                def p1b_B(t):
                    p = t % 2
                    qtile = qbuf[p][:, 0:256]
                    sgt = qbuf[p][:, 256:512]
                    pb.wait("tensor", evtoks[t])
                    pb.wait("tensor", bank_free[3])
                    for c_ in range(2):
                        tr(pp[3][:, 128 * c_:128 * (c_ + 1)],
                           qtile[:, 128 * c_:128 * (c_ + 1)], inc=(1 if c_ == 1 else 0))
                    qtt = pb.tok("tensor")
                    pb.wait("vector", qtt)
                    qT = scr[:, 512:768]
                    qTg = scr2[:, 0:256]
                    for c_ in range(2):
                        V(lambda e, d=qT[:, 128 * c_:128 * (c_ + 1)],
                          s=pp[3][:, 128 * c_:128 * (c_ + 1)]: e.tensor_copy(d, s))
                        V(lambda e, d=qTg[:, 128 * c_:128 * (c_ + 1)],
                          a=pp[3][:, 128 * c_:128 * (c_ + 1)], b=gqv[:, c_, :]:
                          e.tensor_tensor(d, a, b, op=ALU.mult), (1 if c_ == 1 else 0))
                    qvt = pb.tok("vector")
                    bank_free[3] = qvt
                    pb.wait("tensor", bank_free[5])
                    pb.wait("tensor", qvt)
                    pb.wait("tensor", rdk[t])
                    ktv = rbs[t % 2]
                    for i in range(4):
                        base = (i % 2) * 64
                        kTh = ktv[base:base + 64, 128 * (i // 2):128 * (i // 2) + 128]
                        qTh = qT[base:base + 64, 128 * (i // 2):128 * (i // 2) + 128]
                        mm(pp[5][:, 128 * i:128 * (i + 1)], kTh, qTh, True, True, inc=1)
                        pb.xwait("tensor", ("tensor", pb.cnt["tensor"]))
                    sct = pb.tok("tensor")
                    pb.wait("vector", sct)
                    sD = scr2[:, 256:768]
                    V(lambda e, d=sD, a=pp[5][:, 0:512], b=dmask:
                      e.tensor_tensor(d, a, b, op=ALU.mult), 1)
                    sdt = pb.tok("vector")
                    bank_free[5] = sdt
                    pb.wait("tensor", bank_free[6])
                    pb.wait("tensor", sdt)
                    for i in range(4):
                        base = (i % 2) * 64
                        qgh = qTg[base:base + 64, 128 * (i // 2):128 * (i // 2) + 128]
                        Sh = sprefix[base:base + 64, t, 64 * (i // 2):64 * (i // 2) + 64]
                        mm(pp[6][:, 64 * i:64 * i + 64], sD[:, 128 * i:128 * (i + 1)],
                           ktv[:, 256 + 64 * i:256 + 64 * i + 64], True, False)
                        mm(pp[6][:, 64 * i:64 * i + 64], qgh, Sh, False, True, inc=1)
                        pb.xwait("tensor", ("tensor", pb.cnt["tensor"]))
                    ot = pb.tok("tensor")
                    rcons[t] = ot
                    pb.wait("vector", ot)
                    ov = scr2[:, 256:512]
                    V(lambda e, d=ov, s=pp[6][:, 0:256]: e.tensor_copy(d, s), 1)
                    bank_free[6] = pb.tok("vector")
                    mu = scr2[:, 512:516]
                    V(lambda e, d=mu, s=ov.rearrange("p (h d) -> p h d", d=64):
                      e.reduce_sum(d, s, axis=AX))
                    V(lambda e, d=mu, s=mu: e.tensor_scalar_mul(d, s, 1.0 / 64.0), 1)
                    pb.wait("scalar", pb.tok("vector"))
                    sq = scr2[:, 768:1024]
                    SC(lambda e, d=sq, s=ov: e.activation(d, s, AF.Square), 1)
                    pb.wait("vector", pb.tok("scalar"))
                    sqm = scr2[:, 516:520]
                    V(lambda e, d=sqm, s=sq.rearrange("p (h d) -> p h d", d=64):
                      e.reduce_sum(d, s, axis=AX))
                    V(lambda e, d=sqm, s=sqm: e.tensor_scalar_mul(d, s, 1.0 / 64.0))
                    mu2 = scr2[:, 520:524]
                    V(lambda e, d=mu2, a=mu, b=mu: e.tensor_mul(d, a, b))
                    V(lambda e, d=sqm, a=sqm, b=mu2: e.tensor_sub(d, a, b), 1)
                    rs = scr2[:, 524:528]
                    rsqrt_newton(rs, sqm, 1e-6, scr2[:, 528:532], scr2[:, 532:536])
                    gated = scr2[:, 256:512]
                    ttb(gated, ov, mu, 4, ALU.subtract)
                    ttb(gated, gated, rs, 4, ALU.mult)
                    V(lambda e, d=gated, a=gated, b=sgt: e.tensor_mul(d, a, b), 1)
                    gvt = pb.tok("vector")
                    pb.wait("tensor", gvt)
                    pb.wait("tensor", bank_free[3])
                    for c_ in range(2):
                        tr(pp[3][:, 128 * c_:128 * (c_ + 1)],
                           gated[:, 128 * c_:128 * (c_ + 1)], inc=(1 if c_ == 1 else 0))
                    gtt = pb.tok("tensor")
                    pb.wait("scalar", gtt)
                    gT = scr[:, 512:768]
                    for c_ in range(2):
                        SC(lambda e, d=gT[:, 128 * c_:128 * (c_ + 1)],
                           s=pp[3][:, 128 * c_:128 * (c_ + 1)]: e.copy(d, s),
                           (1 if c_ == 1 else 0))
                    gst2 = pb.tok("scalar")
                    bank_free[3] = gst2
                    pb.wait("tensor", gst2)
                    bk2 = (7, 0) if p == 0 else (1, 7) if False else (7, 0)
                    pb.wait("tensor", bank_free[bk2[0]]); pb.wait("tensor", bank_free[bk2[1]])
                    for half in range(2):
                        for c_ in range(2):
                            mm(pp[bk2[half]][:, 0:512], gT[:, 128 * c_:128 * (c_ + 1)],
                               wo_sl[:, c_, 512 * half:512 * half + 512], c_ == 0, c_ == 1,
                               inc=(1 if (half == 1 and c_ == 1) else 0))
                    wt = pb.tok("tensor")
                    pb.wait("vector", wt)
                    if g == 0:
                        for half in range(2):
                            zt = scr2[:, 0:512]
                            V(lambda e, d=zt, s=xt(t)[:, 512 * half:512 * half + 512],
                              m=mcol[:, t:t + 1], r=rcol[:, t:t + 1]:
                              e.tensor_scalar(d, s, m, r, op0=ALU.subtract, op1=ALU.mult))
                            V(lambda e, d=xt(t)[:, 512 * half:512 * half + 512],
                              a=zt, b=xt(t)[:, 512 * half:512 * half + 512]:
                              e.tensor_add(d, a, b))
                            V(lambda e, d=xt(t)[:, 512 * half:512 * half + 512],
                              a=pp[bk2[half]][:, 0:512],
                              b=xt(t)[:, 512 * half:512 * half + 512]:
                              e.tensor_add(d, a, b), (1 if half == 1 else 0))
                    else:
                        for half in range(2):
                            V(lambda e, d=xt(t)[:, 512 * half:512 * half + 512],
                              a=pp[bk2[half]][:, 0:512],
                              b=xt(t)[:, 512 * half:512 * half + 512]:
                              e.tensor_add(d, a, b), (1 if half == 1 else 0))
                    xvt = pb.tok("vector")
                    bank_free[bk2[0]] = xvt
                    bank_free[bk2[1]] = xvt
                    bcons[t] = (wt, xvt)
                    if t + 2 < NT:
                        pb.wait("sync", rcons[t])
                        rdk[t + 2] = DMA("sync", rbs[t % 2],
                                         sp_b[:, 512 * (t + 2):512 * (t + 3)])

                p1b_A(0)
                for t in range(NT):
                    if t + 1 < NT:
                        p1b_A(t + 1)
                    p1b_B(t)
                pb.barrier()

        def phase_swiglu():
            row_reset()
            ab_reset()
            x2rb = [ab(1024), ab(1024)]
            dump = ab(1024)
            mid0 = ab(384); mid1 = ab(384)
            mids = [mid0, mid1]
            l2wb = kapb.rearrange("p n -> p n")
            l2wb2 = kapb2
            l2r = row(1024)
            ld1 = DMA("sync", l2r, dln2w)
            pb.wait("tensor", ld1)
            bcast_row(l2r[0:1, 0:512], kapb, 512, bank=2)
            bcast_row(l2r[0:1, 512:1024], kapb2, 512, bank=3)
            pb.wait("sync", pb.tok("tensor"))
            ld2 = DMA("sync", l2r, dln2b)
            pb.wait("tensor", ld2)
            l2bc = l1bc
            for j in range(8):
                row2col(l2r[0:1, 128 * j:128 * (j + 1)], l2bc[:, j:j + 1], 128, bank=2)
            pb.wait("sync", pb.tok("tensor"))
            boutr = l2r
            DMA("sync", boutr, dbout)
            rd = {0: DMA("sync", x2rb[0], sp_x2[:, 0:1024])}
            ucons = {}
            for t in range(NT):
                pb.tick()
                if t + 1 < NT:
                    pb.wait("sync", ucons.get(t - 1))
                    rd[t + 1] = DMA("sync", x2rb[(t + 1) % 2],
                                    sp_x2[:, 1024 * (t + 1):1024 * (t + 2)])
                pb.wait("vector", rd[t])
                u = scr[:, 0:1024]
                V(lambda e, d=u, a=xt(t), b=x2rb[t % 2]: e.tensor_sub(d, a, b), 1)
                ucons[t] = pb.tok("vector")
                sums = scr2[:, 0:1]; sqs = scr2[:, 1:2]
                V(lambda e, d=sums, s=u: e.reduce_sum(d, s, axis=AX), 1)
                pb.wait("scalar", pb.tok("vector"))
                SC(lambda e, d=dump, s=u: e.activation(d, s, AF.Square), 1)
                pb.wait("vector", pb.tok("scalar"))
                V(lambda e, d=sqs, s=dump: e.reduce_sum(d, s, axis=AX))
                V(lambda e, d=mcol[:, t:t + 1], s=sums: e.tensor_scalar_mul(d, s, 1.0 / 1024.0))
                tv = scr2[:, 2:3]
                V(lambda e, d=tv, s=sqs: e.tensor_scalar_mul(d, s, 1.0 / 1024.0))
                tm = scr2[:, 3:4]
                V(lambda e, d=tm, a=mcol[:, t:t + 1], b=mcol[:, t:t + 1]: e.tensor_mul(d, a, b))
                V(lambda e, d=tv, a=tv, b=tm: e.tensor_sub(d, a, b))
                rsqrt_newton(rcol[:, t:t + 1], tv, 1e-5, scr2[:, 4:5], scr2[:, 5:6])
                un = scr[:, 0:1024]
                V(lambda e, d=un, s=u, m=mcol[:, t:t + 1], r=rcol[:, t:t + 1]:
                  e.tensor_scalar(d, s, m, r, op0=ALU.subtract, op1=ALU.mult))
                V(lambda e, d=un[:, 0:512], a=un[:, 0:512], b=kapb: e.tensor_mul(d, a, b))
                V(lambda e, d=un[:, 512:1024], a=un[:, 512:1024], b=kapb2:
                  e.tensor_mul(d, a, b), 1)
                vtok = pb.tok("vector")
                pb.wait("tensor", vtok)
                pb.wait("tensor", bank_free[0]); pb.wait("tensor", bank_free[1])
                for j in range(8):
                    b = 0 if j < 4 else 1
                    tr(pp[b][:, 128 * (j % 4):128 * (j % 4 + 1)],
                       un[:, 128 * j:128 * (j + 1)], inc=(1 if j in (3, 7) else 0))
                ttok = pb.tok("tensor")
                pb.wait("scalar", ttok)
                for j in range(8):
                    b = 0 if j < 4 else 1
                    SC(lambda e, d=nT(t)[:, 128 * j:128 * (j + 1)],
                       s=pp[b][:, 128 * (j % 4):128 * (j % 4 + 1)]: e.copy(d, s),
                       inc=(1 if j in (3, 7) else 0))
                stok = pb.tok("scalar")
                bank_free[0] = stok; bank_free[1] = stok
            pb.barrier()

            sgs = [list(range(s, min(s + 3, NT))) for s in range(0, NT, 3)]
            aw_reset()
            wia = [aw(1024).rearrange("p (j n) -> p j n", n=128) for _ in range(2)]
            wib = [aw(1024).rearrange("p (j n) -> p j n", n=128) for _ in range(2)]
            wou = [aw(1024) for _ in range(2)]
            dwr = dwin.rearrange("(j p) o -> p j o", p=128)
            mccons = {}
            ccons = {}
            for sgi, sg in enumerate(sgs):
                n_t = len(sg)
                for mc_i in range(NMC):
                    gi = sgi * NMC + mc_i
                    w = MC[mc_i]
                    pbuf = gi % 2
                    pb.wait("sync", mccons.get(gi - 2))
                    lda = DMA("sync", wia[pbuf][:, :, 0:w],
                              dwr[:, :, 128 * mc_i:128 * mc_i + w])
                    ldb_ = DMA("sync", wib[pbuf][:, :, 0:w],
                               dwr[:, :, HID + 128 * mc_i:HID + 128 * mc_i + w])
                    ldo = DMA("sync", wou[pbuf][0:w, 0:1024],
                              dwout[128 * mc_i:128 * mc_i + w, :])
                    ca = ccst[:, 2 * mc_i:2 * mc_i + 1]
                    cb = ccst[:, 2 * mc_i + 1:2 * mc_i + 2]
                    pb.wait("tensor", lda); pb.wait("tensor", ldb_)
                    pb.wait("tensor", ldo)
                    if sgi == 0:
                        bra = scr2[0:1, 8:136]
                        brb = scr2[0:1, 136:264]
                        pb.wait("sync", ccons.get(gi - 1))
                        la = DMA("sync", bra[0:1, 0:w], dbin[:, 128 * mc_i:128 * mc_i + w])
                        lb = DMA("sync", brb[0:1, 0:w],
                                 dbin[:, HID + 128 * mc_i:HID + 128 * mc_i + w])
                        pb.wait("tensor", la); pb.wait("tensor", lb)
                        pb.wait("tensor", bank_free[0]); pb.wait("tensor", bank_free[1])
                        for j in range(8):
                            mm(pp[0][0:w, 0:1], wia[pbuf][:, j, 0:w], l2bc[:, j:j + 1],
                               j == 0, False)
                        for j in range(8):
                            mm(pp[0][0:w, 1:2], wib[pbuf][:, j, 0:w], l2bc[:, j:j + 1],
                               False, j == 7)
                        mm(pp[0][0:w, 2:3], bra[0:1, 0:w], ones11, False, True)
                        mm(pp[0][0:w, 3:4], brb[0:1, 0:w], ones11, False, True, inc=1)
                        cct = pb.tok("tensor")
                        ccons[gi] = cct
                        pb.wait("vector", cct)
                        V(lambda e, d=ca[0:w, :], s=pp[0][0:w, 0:1]: e.tensor_copy(d, s))
                        V(lambda e, d=ca[0:w, :], a=pp[0][0:w, 2:3], b=ca[0:w, :]:
                          e.tensor_add(d, a, b))
                        V(lambda e, d=cb[0:w, :], s=pp[0][0:w, 1:2]: e.tensor_copy(d, s))
                        V(lambda e, d=cb[0:w, :], a=pp[0][0:w, 3:4], b=cb[0:w, :]:
                          e.tensor_add(d, a, b), 1)
                        cvt = pb.tok("vector")
                        pb.wait("tensor", cvt)
                    pb.wait("tensor", bank_free[0]); pb.wait("tensor", bank_free[1])
                    for ti, t in enumerate(sg):
                        for j in range(8):
                            mm(pp[0][0:w, 128 * ti:128 * ti + 128], wia[pbuf][:, j, 0:w],
                               nT(t)[:, 128 * j:128 * (j + 1)], j == 0, j == 7)
                        for j in range(8):
                            mm(pp[1][0:w, 128 * ti:128 * ti + 128], wib[pbuf][:, j, 0:w],
                               nT(t)[:, 128 * j:128 * (j + 1)], j == 0,
                               j == 7,
                               inc=(1 if (j == 7 and ti == n_t - 1) else 0))
                    abt = pb.tok("tensor")
                    pb.wait("scalar", abt)
                    sb_ = scr[:, 0:384]
                    SC(lambda e, d=sb_[0:w, 0:128 * n_t], s=pp[1][0:w, 0:128 * n_t],
                       b=cb[0:w, :]: e.activation(d, s, AF.Silu, bias=b), 1)
                    sbt = pb.tok("scalar")
                    pb.wait("vector", sbt)
                    mtile = mids[pbuf]
                    V(lambda e, d=mtile[0:w, 0:128 * n_t], a=pp[0][0:w, 0:128 * n_t],
                      s=ca[0:w, :], b=sb_[0:w, 0:128 * n_t]:
                      e.scalar_tensor_tensor(d, a, s, b, op0=ALU.add, op1=ALU.mult), 1)
                    mvt = pb.tok("vector")
                    bank_free[0] = mvt; bank_free[1] = mvt
                    pb.wait("tensor", mvt)
                    for ti, t in enumerate(sg):
                        for half in range(2):
                            bk = 2 + 2 * ti + half
                            if mc_i == 0:
                                pb.wait("tensor", bank_free[bk])
                            mm(pp[bk][:, 0:512], mtile[0:w, 128 * ti:128 * ti + 128],
                               wou[pbuf][0:w, 512 * half:512 * half + 512],
                               mc_i == 0, mc_i == NMC - 1,
                               inc=(1 if (mc_i == NMC - 1 and half == 1 and ti == n_t - 1) else 0))
                    mccons[gi] = pb.tok("tensor")
                ot = pb.tok("tensor")
                pb.wait("vector", ot)
                for ti, t in enumerate(sg):
                    for half in range(2):
                        bk = 2 + 2 * ti + half
                        V(lambda e, d=xt(t)[:, 512 * half:512 * half + 512],
                          a=pp[bk][:, 0:512], b=xt(t)[:, 512 * half:512 * half + 512]:
                          e.tensor_add(d, a, b),
                          (1 if (ti == n_t - 1 and half == 1) else 0))
                vt = pb.tok("vector")
                for i_ in range(2, 8):
                    bank_free[i_] = vt
            pb.barrier()
            bcast_row(boutr[0:1, 0:512], kapb, 512, bank=2)
            bcast_row(boutr[0:1, 512:1024], kapb2, 512, bank=3)
            for t in range(NT):
                V(lambda e, d=xt(t)[:, 0:512], a=kapb, b=xt(t)[:, 0:512]:
                  e.tensor_add(d, a, b))
                V(lambda e, d=xt(t)[:, 512:1024], a=kapb2, b=xt(t)[:, 512:1024]:
                  e.tensor_add(d, a, b), (1 if t == NT - 1 else 0))
            pb.barrier()

        if stage >= 1:
            phase_attn_a()
        if stage >= 2:
            phase_ffn()
        if stage >= 3:
            phase_attn_b()
        if stage >= 4:
            phase_swiglu()
        if stage >= 5:
            phase_ffn()

        pb.barrier()
        otok = DMA("sync", dy.rearrange("(t p) d -> p t d", p=128),
                   xsb.rearrange("p (t d) -> p t d", d=1024))
        pb.xwait("sync", otok)

        blk = ctx.enter_context(nc.Block())
        for e in ENGS:
            def mk(e=e):
                def emit(eng):
                    for entry in pb.q[e]:
                        if entry[0] == "op":
                            _, a, sname, amt = entry
                            inst = a(eng)
                            if sname is not None:
                                inst.then_inc(sems[sname], amt)
                        else:
                            _, sname, v = entry
                            eng.wait_ge(sems[sname], v)
                return emit
            getattr(blk, e)(mk())

    print(f"[kernel] built stage={stage} ops={pb.nops} " +
          " ".join(f"{e}:{len(pb.q[e])}" for e in ENGS))
    return nc


def kernel(**inputs):
    nc = build()
    x = np.ascontiguousarray(inputs["L0_attn_a_x"], dtype=np.float32)
    base = {
        "aWq": inputs["L0_attn_a_Wq"], "abq": inputs["L0_attn_a_bq"].reshape(1, -1),
        "aWk": inputs["L0_attn_a_Wk"], "abk": inputs["L0_attn_a_bk"].reshape(1, -1),
        "aWv": inputs["L0_attn_a_Wv"], "abv": inputs["L0_attn_a_bv"].reshape(1, -1),
        "aWo": inputs["L0_attn_a_Wo"], "abo": inputs["L0_attn_a_bo"].reshape(1, -1),
        "gw": inputs["L0_ffn_gate_w"], "gb": inputs["L0_ffn_gate_b"].reshape(1, -1),
        "ew": inputs["L0_ffn_expert_w"], "eb": inputs["L0_ffn_expert_b"],
        "ln1w": inputs["L0_attn_b_ln1_w"].reshape(1, -1),
        "ln1b": inputs["L0_attn_b_ln1_b"].reshape(1, -1),
        "bwq": inputs["L0_attn_b_wq"], "bwk": inputs["L0_attn_b_wk"],
        "bwv": inputs["L0_attn_b_wv"], "bwg": inputs["L0_attn_b_wg"],
        "bwo": inputs["L0_attn_b_wo"],
        "ln2w": inputs["L0_attn_b_ln2_w"].reshape(1, -1),
        "ln2b": inputs["L0_attn_b_ln2_b"].reshape(1, -1),
        "win": inputs["L0_attn_b_w_in"], "bin": inputs["L0_attn_b_b_in"].reshape(1, -1),
        "wout": inputs["L0_attn_b_w_out"], "bout": inputs["L0_attn_b_b_out"].reshape(1, -1),
    }
    base = {k: np.ascontiguousarray(v, dtype=np.float32) for k, v in base.items()}
    in_maps = []
    for c in range(NCORE):
        m = dict(base)
        m["x"] = np.ascontiguousarray(x[c // 2, (c % 2) * TPC:(c % 2 + 1) * TPC])
        m["m1"] = np.full((128, 1), 1.0 if c % 2 == 0 else 0.0, np.float32)
        m["m2"] = np.full((128, 1), 0.0 if c % 2 == 0 else 1.0, np.float32)
        in_maps.append(m)
    res = run_bass_kernel_spmd(nc, in_maps, core_ids=list(range(NCORE)))
    out = np.empty((Bb, L, D), np.float32)
    for c in range(NCORE):
        out[c // 2, (c % 2) * TPC:(c % 2 + 1) * TPC] = res.results[c]["y"]
    if os.environ.get("MK_TAP"):
        global DBG
        DBG = [res.results[c].get("dbg") for c in range(NCORE)]
    return out


if __name__ == "__main__":
    build()


# version 08
# speedup vs baseline: 1.0142x; 1.0142x over previous
import os
import math
from contextlib import ExitStack
import numpy as np
import concourse.bass as bass
import concourse.mybir as mybir
from concourse.bass_utils import run_bass_kernel_spmd

F32 = mybir.dt.float32
BF16 = mybir.dt.bfloat16
AF = mybir.ActivationFunctionType
ALU = mybir.AluOpType
AX = mybir.AxisListType.X

STAGE = int(os.environ.get("MK_STAGE", "5"))

NCORE = 8
Bb, L, D, H, HD = 4, 4096, 1024, 16, 64
TPC, NT = 2048, 16
HID = 2730
NMC = 22
MC = [128] * 21 + [42]
EPSA = 1e-6
RG = [[0, 1], [2, 3], [4, 5], [6, 7]]
ENGS = ("tensor", "vector", "scalar", "sync", "gpsimd")


class PB:
    def __init__(self):
        self.dsems = {"sync": [f"d{i}" for i in range(6)],
                      "gpsimd": [f"g{i}" for i in range(3)]}
        self.SEMS = list(ENGS) + self.dsems["sync"] + self.dsems["gpsimd"]
        self.q = {e: [] for e in ENGS}
        self.cnt = {s: 0 for s in self.SEMS}
        self.dirty = {e: False for e in ENGS}
        self.hw = {(a, b): 0 for a in ENGS for b in self.SEMS}
        self.rr = {"sync": 0, "gpsimd": 0}
        self.nops = 0

    def do(self, e, fn, inc=0):
        if e in ("vector", "scalar") and inc == 0:
            inc = 1
        self.q[e].append(("op", fn, e if inc else None, inc))
        self.nops += 1
        if inc:
            self.cnt[e] += inc
            self.dirty[e] = False
            if e in ("vector", "scalar"):
                self.q[e].append(("wait", e, self.cnt[e]))
        else:
            self.dirty[e] = True
        return (e, self.cnt[e])

    def dma(self, e, fn):
        pool = self.dsems[e]
        s = pool[self.rr[e] % len(pool)]
        self.rr[e] += 1
        self.xwait(e, (s, self.cnt[s]))
        self.q[e].append(("op", fn, s, 16))
        self.nops += 1
        self.cnt[s] += 16
        return (s, self.cnt[s])

    def tok(self, e):
        if self.dirty[e]:
            for i in range(len(self.q[e]) - 1, -1, -1):
                ent = self.q[e][i]
                if ent[0] == "op" and ent[2] is None:
                    self.cnt[e] += 1
                    self.q[e][i] = ("op", ent[1], e, 1)
                    if e in ("vector", "scalar"):
                        self.q[e].append(("wait", e, self.cnt[e]))
                    break
            self.dirty[e] = False
        return (e, self.cnt[e])

    def wait(self, w, tokn):
        if tokn is None:
            return
        e, v = tokn
        if e == w or v <= 0 or self.hw[(w, e)] >= v:
            return
        self.hw[(w, e)] = v
        self.q[w].append(("wait", e, v))

    def xwait(self, w, tokn):
        e, v = tokn
        if v <= 0:
            return
        if e != w and self.hw[(w, e)] >= v:
            return
        if e != w:
            self.hw[(w, e)] = v
        self.q[w].append(("wait", e, v))

    def barrier(self):
        toks = [self.tok(e) for e in ENGS]
        toks += [(s, self.cnt[s]) for s in self.SEMS if s not in ENGS]
        for w in ENGS:
            for t in toks:
                self.xwait(w, t) if t[0] == w else self.wait(w, t)

    def dmaq(self, w):
        for s_ in self.dsems["sync"] + self.dsems["gpsimd"]:
            self.wait(w, (s_, self.cnt[s_]))

    def tick(self, engines=("tensor", "vector", "scalar")):
        toks = [self.tok(e) for e in engines]
        for w in engines:
            for t in toks:
                self.wait(w, t)


def build(stage=STAGE):
    nc = bass.Bass("TRN2", target_bir_lowering=False)

    dx = nc.dram_tensor("x", [TPC, D], F32, kind="ExternalInput").ap()
    dy = nc.dram_tensor("y", [TPC, D], F32, kind="ExternalOutput").ap()

    def din(name, shape):
        return nc.dram_tensor(name, shape, F32, kind="ExternalInput").ap()

    dWq = din("aWq", [D, D]); dbq = din("abq", [1, D])
    dWk = din("aWk", [D, D]); dbk = din("abk", [1, D])
    dWv = din("aWv", [D, D]); dbv = din("abv", [1, D])
    dWo = din("aWo", [D, D]); dbo = din("abo", [1, D])
    dgw = din("gw", [D, 8]); dgb = din("gb", [1, 8])
    dew = din("ew", [8, D, D]); deb = din("eb", [8, D])
    dln1w = din("ln1w", [1, D]); dln1b = din("ln1b", [1, D])
    dbwq = din("bwq", [D, D]); dbwk = din("bwk", [D, D])
    dbwv = din("bwv", [D, D]); dbwg = din("bwg", [D, D])
    dbwo = din("bwo", [D, D])
    dln2w = din("ln2w", [1, D]); dln2b = din("ln2b", [1, D])
    dwin = din("win", [D, 2 * HID]); dbin = din("bin", [1, 2 * HID])
    dwout = din("wout", [HID, D]); dbout = din("bout", [1, D])
    dm1 = din("m1", [128, 1])
    dm2 = din("m2", [128, 1])

    sp_k = nc.dram_tensor("sp_k", [128, NT * 512], F32).ap()
    sp_b = nc.dram_tensor("sp_b", [128, NT * 512], F32).ap()
    sp_x2 = nc.dram_tensor("sp_x2", [128, NT * 1024], F32).ap()
    cc1i = nc.dram_tensor("cc1i", [1, 1024], F32).ap()
    cc1o = nc.dram_tensor("cc1o", [1, 1024], F32).ap()
    cc3i = nc.dram_tensor("cc3i", [128, 264], F32).ap()
    cc3o = nc.dram_tensor("cc3o", [128, 264], F32).ap()
    ccsi = nc.dram_tensor("ccsi", [128, 128], F32).ap()
    ccso = nc.dram_tensor("ccso", [128, 128], F32).ap()

    TAP = os.environ.get("MK_TAP", "")
    ddbg = (nc.dram_tensor("dbg", [128, 4096], F32, kind="ExternalOutput").ap()
            if TAP else None)

    pb = PB()

    with ExitStack() as ctx:
        sbal = lambda n, c: ctx.enter_context(nc.sbuf_tensor(n, [128, c], F32)).ap()
        xsb = sbal("xsb", NT * 1024)
        aA = sbal("aA", NT * 1024)
        aB = sbal("aB", 8192)
        aW = sbal("aW", 6144)
        aM = sbal("aM", 6056)
        pp = [ctx.enter_context(nc.psum_tensor(f"pp{i}", [128, 512], F32)).ap()
              for i in range(8)]
        sems = {n: ctx.enter_context(nc.semaphore(f"s_{n}")) for n in pb.SEMS}

        o_ = [0]
        def am(n):
            v = o_[0]; o_[0] += n
            assert o_[0] <= 6056, f"aM overflow {o_[0]}"
            return aM[:, v:v + n]
        ident = am(128)
        onesrow = am(128)
        onescol = am(1)
        lngp = am(8)
        lngb = am(16)
        g128 = am(8)
        gvcol = am(16)
        union = am(1024)
        mcol = am(16); rcol = am(16)
        rows = am(1024)
        scr = am(1024)
        scr2 = am(1024)
        kapb = am(512)
        kapb2 = am(512)
        seps = am(48)
        ccst = am(44)
        smA = am(128)
        kvsbm = am(256)
        g128h = am(16)
        pbsc = am(24)
        sfA = union[:, 0:128]; sfB = union[:, 128:256]; eAm = union[:, 256:384]
        dmask = union[:, 0:512]; gqnew = union[0:64, 512:1024]
        gat = union[:, 0:128]
        eksi = seps[:, 0:8]; ekso = seps[:, 8:16]; scb = seps[:, 16:24]
        m1c = seps[:, 24:25]; m2c = seps[:, 25:26]
        l1wc = seps[:, 26:34]; l1bc = seps[:, 34:42]
        ones11 = onesrow[0:1, 0:1]

        RW = [0]
        def row(n):
            v = RW[0]; RW[0] += n
            assert RW[0] <= 1024, f"rows overflow {RW[0]}"
            return rows[0:1, v:v + n]
        def row_reset():
            RW[0] = 0

        oB = [0]
        def ab(n):
            v = oB[0]; oB[0] += n
            assert oB[0] <= 8192, f"aB overflow {oB[0]}"
            return aB[:, v:v + n]
        def ab_reset():
            oB[0] = 0

        oW = [0]
        def aw(n):
            v = oW[0]; oW[0] += n
            assert oW[0] <= 6144, f"aW overflow {oW[0]}"
            return aW[:, v:v + n]
        def aw_reset():
            oW[0] = 0

        bank_free = {i: None for i in range(8)}

        def mm(dst, lhsT, rhs, start, stop, inc=0):
            return pb.do("tensor",
                         lambda e, d=dst, l=lhsT, r=rhs, s1=start, s2=stop:
                         e.matmul(d, l, r, start=s1, stop=s2, skip_group_check=True),
                         inc)

        def tr(dst, src, inc=0):
            return pb.do("tensor",
                         lambda e, d=dst, s=src: e.transpose(d, s, ident), inc)

        def V(fn, inc=0):
            return pb.do("vector", fn, inc)

        def SC(fn, inc=0):
            return pb.do("scalar", fn, inc)

        def G(fn, inc=0):
            pb.xwait("gpsimd", ("gpsimd", pb.cnt["gpsimd"]))
            return pb.do("gpsimd", fn, inc if inc else 1)

        def DMA(e, dst, src):
            return pb.dma(e, lambda eng, d=dst, s=src: eng.dma_start(out=d, in_=s))

        def tap(off, ap, n):
            if not TAP:
                return
            pb.barrier()
            DMA("sync", ddbg[0:ap.shape[0], off:off + n], ap)
            pb.barrier()

        def collective(ci, co):
            t = pb.do("gpsimd",
                      lambda e, i=ci, o=co: e.collective_compute(
                          "AllReduce", ALU.add, replica_groups=RG,
                          ins=[i.opt()], outs=[o.opt()]), 1)
            return t

        def ttb(dst, a, colap, nh, op, inc=0, dd=64):
            d3 = dst.rearrange("p (h d) -> p h d", d=dd)
            a3 = a.rearrange("p (h d) -> p h d", d=dd)
            b3 = colap.rearrange("p (h o) -> p h o", o=1).broadcast_to([128, nh, dd])
            return V(lambda e, d=d3, x=a3, y=b3, o=op: e.tensor_tensor(d, x, y, op=o),
                     inc)

        def xt(t):
            return xsb[:, 1024 * t:1024 * (t + 1)]

        def nT(t):
            return aA[:, 1024 * t:1024 * (t + 1)]

        def rsqrt_newton(dst, src, eps, t1, t2):
            V(lambda e, d=t1, s=src, ep=eps: e.tensor_scalar_add(d, s, ep), 1)
            pb.wait("scalar", pb.tok("vector"))
            SC(lambda e, d=t2, s=t1: e.activation(d, s, AF.Sqrt), 1)
            pb.wait("vector", pb.tok("scalar"))
            V(lambda e, d=dst, s=t2: e.reciprocal(d, s))
            V(lambda e, d=t2, a=dst, b=dst: e.tensor_mul(d, a, b))
            V(lambda e, d=t2, a=t1, b=t2: e.tensor_mul(d, a, b))
            V(lambda e, d=t2, a=t2: e.tensor_scalar(d, a, -0.5, 1.5,
                                                    op0=ALU.mult, op1=ALU.add))
            V(lambda e, d=dst, a=dst, b=t2: e.tensor_mul(d, a, b))

        def emit_stats(t, mdst, rdst, src_fn=xt, dbg=False):
            sums = scr2[:, 0:1]; sqs = scr2[:, 1:2]
            V(lambda e, d=sums, s=src_fn(t): e.reduce_sum(d, s, axis=AX), 1)
            pb.wait("scalar", pb.tok("vector"))
            SC(lambda e, d=scr[:, 0:1024], s=src_fn(t):
               e.activation(d, s, AF.Square), 1)
            pb.wait("vector", pb.tok("scalar"))
            V(lambda e, d=sqs, s=scr[:, 0:1024]: e.reduce_sum(d, s, axis=AX))
            V(lambda e, d=mdst, s=sums: e.tensor_scalar_mul(d, s, 1.0 / 1024.0))
            tv = scr2[:, 2:3]
            V(lambda e, d=tv, s=sqs: e.tensor_scalar_mul(d, s, 1.0 / 1024.0))
            tm = scr2[:, 3:4]
            V(lambda e, d=tm, a=mdst, b=mdst: e.tensor_mul(d, a, b))
            V(lambda e, d=tv, a=tv, b=tm: e.tensor_sub(d, a, b))
            rsqrt_newton(rdst, tv, 1e-5, scr2[:, 4:5], scr2[:, 5:6])
            return tv

        def emit_xnt(t, mdst, rdst):
            xn = scr[:, 0:1024]
            pb.wait("vector", pb.tok("scalar"))
            V(lambda e, d=xn, s=xt(t), m=mdst, r=rdst:
              e.tensor_scalar(d, s, m, r, op0=ALU.subtract, op1=ALU.mult), 1)
            vtok = pb.tok("vector")
            pb.wait("tensor", vtok)
            pb.wait("tensor", bank_free[0])
            pb.wait("tensor", bank_free[1])
            for j in range(8):
                b = 0 if j < 4 else 1
                tr(pp[b][:, 128 * (j % 4):128 * (j % 4 + 1)],
                   xn[:, 128 * j:128 * (j + 1)], inc=(1 if j in (3, 7) else 0))
            ttok = pb.tok("tensor")
            pb.wait("scalar", ttok)
            for j in range(8):
                b = 0 if j < 4 else 1
                SC(lambda e, d=nT(t)[:, 128 * j:128 * (j + 1)],
                   s=pp[b][:, 128 * (j % 4):128 * (j % 4 + 1)]: e.copy(d, s),
                   inc=(1 if j in (3, 7) else 0))
            stok = pb.tok("scalar")
            bank_free[0] = stok
            bank_free[1] = stok
            return stok

        aAb = aA.bitcast(BF16)

        def nTh(t):
            return aAb[:, 1024 * t:1024 * (t + 1)]

        def nTl(t):
            return aAb[:, 16384 + 1024 * t:16384 + 1024 * (t + 1)]

        def emit_xnt_bf(t, mdst, rdst):
            xn = scr[:, 0:1024]
            pb.wait("vector", pb.tok("scalar"))
            V(lambda e, d=xn, s=xt(t), m=mdst, r=rdst:
              e.tensor_scalar(d, s, m, r, op0=ALU.subtract, op1=ALU.mult), 1)
            vtok = pb.tok("vector")
            pb.wait("tensor", vtok)
            pb.wait("tensor", bank_free[0])
            pb.wait("tensor", bank_free[1])
            for j in range(8):
                b = 0 if j < 4 else 1
                tr(pp[b][:, 128 * (j % 4):128 * (j % 4 + 1)],
                   xn[:, 128 * j:128 * (j + 1)], inc=(1 if j in (3, 7) else 0))
            ttok = pb.tok("tensor")
            pb.wait("scalar", ttok)
            for j in range(8):
                b = 0 if j < 4 else 1
                SC(lambda e, d=nTh(t)[:, 128 * j:128 * (j + 1)],
                   s=pp[b][:, 128 * (j % 4):128 * (j % 4 + 1)]: e.copy(d, s),
                   inc=(1 if j in (3, 7) else 0))
            hstok = pb.tok("scalar")
            pb.wait("vector", hstok)
            for j in range(8):
                b = 0 if j < 4 else 1
                V(lambda e, d=nTl(t)[:, 128 * j:128 * (j + 1)],
                  a=pp[b][:, 128 * (j % 4):128 * (j % 4 + 1)],
                  c=nTh(t)[:, 128 * j:128 * (j + 1)]:
                  e.scalar_tensor_tensor(d, a, 1.0, c, op0=ALU.mult, op1=ALU.subtract),
                  inc=(1 if j in (3, 7) else 0))
            lstok = pb.tok("vector")
            bank_free[0] = lstok
            bank_free[1] = lstok
            return lstok

        def emit_proj3(t, psum_ap, whi, wlo, start=True, inc=0):
            tk = None
            for j in range(8):
                tk = mm(psum_ap, nTh(t)[:, 128 * j:128 * (j + 1)], whi[:, j, :],
                        start and j == 0, False)
                tk = mm(psum_ap, nTh(t)[:, 128 * j:128 * (j + 1)], wlo[:, j, :],
                        False, False)
                tk = mm(psum_ap, nTl(t)[:, 128 * j:128 * (j + 1)], whi[:, j, :],
                        False, False, inc=(inc if j == 7 else 0))
            return tk

        def split_w(staging_flat, hi_flat, lo_flat, ld_tok, cons=()):
            pb.wait("scalar", ld_tok)
            for tk_ in cons:
                pb.wait("scalar", tk_)
            SC(lambda e, d=hi_flat, s=staging_flat: e.copy(d, s), 1)
            hs = pb.tok("scalar")
            pb.wait("vector", hs)
            for tk_ in cons:
                pb.wait("vector", tk_)
            V(lambda e, d=lo_flat, a=staging_flat, c=hi_flat:
              e.scalar_tensor_tensor(d, a, 1.0, c, op0=ALU.mult, op1=ALU.subtract), 1)
            return pb.tok("vector")

        def emit_proj(t, psum_ap, wsl, start=True, inc=0, nj=8):
            tk = None
            for j in range(nj):
                tk = mm(psum_ap, nT(t)[:, 128 * j:128 * (j + 1)], wsl[:, j, :],
                        start and j == 0, False, inc=(inc if j == nj - 1 else 0))
            return tk

        def biasfold(psum_ap, brow, stop=True, inc=1):
            return mm(psum_ap, onesrow[0:1, 0:128], brow, False, stop, inc=inc)

        def row2col(rowap, colap, n, bank=2):
            nchunks = (n + 127) // 128
            pb.wait("tensor", bank_free[bank])
            for j in range(nchunks):
                w = min(128, n - 128 * j)
                mm(pp[bank][0:w, j:j + 1], rowap[0:1, 128 * j:128 * j + w], ones11,
                   True, True, inc=(1 if j == nchunks - 1 else 0))
            tt = pb.tok("tensor")
            pb.wait("vector", tt)
            for j in range(nchunks):
                w = min(128, n - 128 * j)
                V(lambda e, d=colap[0:w, j:j + 1], s=pp[bank][0:w, j:j + 1]:
                  e.tensor_copy(d, s), inc=(1 if j == nchunks - 1 else 0))
            bank_free[bank] = pb.tok("vector")

        def bcast_row(rowap, dst, n, bank=2):
            pb.wait("tensor", bank_free[bank])
            mm(pp[bank][:, 0:n], onesrow[0:1, 0:128], rowap, True, True, inc=1)
            tt = pb.tok("tensor")
            pb.wait("vector", tt)
            V(lambda e, d=dst, s=pp[bank][:, 0:n]: e.tensor_copy(d, s), 1)
            bank_free[bank] = pb.tok("vector")

        G(lambda e: e.memset(ident, 0.0), 1)
        pb.xwait("gpsimd", pb.tok("gpsimd"))
        G(lambda e: e.affine_select(out=ident, in_=ident, compare_op=ALU.not_equal,
                                    fill=1.0, base=0, pattern=[[-1, 128]],
                                    channel_multiplier=1))
        G(lambda e: e.memset(onesrow[0:1, :], 1.0))
        G(lambda e: e.memset(onescol, 1.0))
        G(lambda e: e.memset(rows, 0.0))
        gr = row(16); lgr = row(16)
        for h in range(16):
            gam = 1.0 - 2.0 ** (-5.0 - h)
            G(lambda e, d=gr[0:1, h:h + 1], v=gam: e.memset(d, v),
              (1 if h == 15 else 0))
        pb.wait("scalar", pb.tok("gpsimd"))
        SC(lambda e, d=lgr, s=gr: e.activation(d, s, AF.Ln), 1)
        pb.wait("tensor", pb.tok("scalar"))
        pb.wait("tensor", pb.tok("gpsimd"))
        bcast_row(lgr, lngb, 16, bank=2)
        for j in range(8):
            V(lambda e, d=lngp[0:64, j:j + 1], s=lngb[0:64, 2 * j:2 * j + 1]:
              e.tensor_copy(d, s))
            V(lambda e, d=lngp[64:128, j:j + 1], s=lngb[64:128, 2 * j + 1:2 * j + 2]:
              e.tensor_copy(d, s), (1 if j == 7 else 0))
        pb.wait("scalar", pb.tok("vector"))
        SC(lambda e, d=g128, s=lngp: e.activation(d, s, AF.Exp, scale=128.0), 1)
        SC(lambda e, d=g128h, s=lngb: e.activation(d, s, AF.Exp, scale=128.0), 1)
        c127 = scr2[:, 8:9]
        G(lambda e, d=c127: e.iota(d, pattern=[[0, 1]], base=127, channel_multiplier=-1,
                                   allow_small_or_imprecise_dtypes=True), 1)
        pb.wait("vector", pb.tok("gpsimd"))
        V(lambda e, d=gvcol, a=c127.broadcast_to([128, 16]), b=lngb:
          e.tensor_tensor(d, a, b, op=ALU.mult), 1)
        pb.wait("scalar", pb.tok("vector"))
        SC(lambda e, d=gvcol, s=gvcol: e.activation(d, s, AF.Exp), 1)
        row_reset()
        pb.barrier()

        DMA("sync", xsb.rearrange("p (t d) -> p t d", d=1024),
            dx.rearrange("(t p) d -> p t d", p=128))
        DMA("sync", m1c, dm1)
        DMA("sync", m2c, dm2)
        pb.barrier()

        def phase_attn_a():
            for t in range(NT):
                emit_stats(t, mcol[:, t:t + 1], rcol[:, t:t + 1])
                emit_xnt_bf(t, mcol[:, t:t + 1], rcol[:, t:t + 1])
            pb.barrier()
            tap(0, nT(0), 1024)
            tap(3072, mcol, 16)
            tap(3088, rcol, 16)
            for g in range(2):
                row_reset()
                ab_reset()
                qst = ab(8192).rearrange("p (t n) -> p t n", n=512)
                aw_reset()
                wstage = aw(2048).rearrange("p (j n) -> p j n", n=256)
                wbfa = aw(2048).bitcast(BF16).rearrange("p (v j n) -> p v j n", v=2, n=256)
                wbfb = aw(2048).bitcast(BF16).rearrange("p (v j n) -> p v j n", v=2, n=256)
                slices = [(dWq, dbq, 0), (dWq, dbq, 1), (dWk, dbk, 0), (dWk, dbk, 1)]
                rowsum_slc = {0: (4, 0), 1: (4, 256), 2: (5, 0), 3: (5, 256)}
                slice_cons = {}
                spl_prev = {}
                klist = []
                for si_, (wd, bd, half) in enumerate(slices):
                    wsl_bf = wbfa if si_ % 2 == 0 else wbfb
                    c0 = g * 512 + half * 256
                    pb.wait("sync", spl_prev.get(si_ - 1))
                    ld = DMA("sync", wstage,
                             wd.rearrange("(j p) o -> p j o", p=128)[:, :, c0:c0 + 256])
                    br = scr2[0:1, 768:1024]
                    pb.wait("sync", slice_cons.get(si_ - 1))
                    ldb = DMA("sync", br, bd[:, c0:c0 + 256])
                    spl = split_w(wstage.rearrange("p j n -> p (j n)"),
                                  wsl_bf[:, 0].rearrange("p j n -> p (j n)"),
                                  wsl_bf[:, 1].rearrange("p j n -> p (j n)"),
                                  ld, cons=(slice_cons.get(si_ - 2),) if si_ >= 2 else ())
                    spl_prev[si_] = spl
                    pb.wait("tensor", spl)
                    pb.wait("tensor", ldb)
                    rb, rc = rowsum_slc[si_]
                    if si_ == 0:
                        pb.wait("tensor", bank_free[4])
                        pb.wait("tensor", bank_free[5])
                    last = None
                    for t in range(NT):
                        bk = 2 if t % 2 == 0 else 3
                        pb.wait("tensor", bank_free[bk])
                        emit_proj3(t, pp[bk][:, 0:256], wsl_bf[:, 0], wsl_bf[:, 1])
                        biasfold(pp[bk][:, 0:256], br, stop=True, inc=1)
                        mtok = pb.tok("tensor")
                        pb.wait("scalar", mtok)
                        if si_ < 2:
                            dst = qst[:, t, half * 256:half * 256 + 256]
                        else:
                            dst = scr[:, 0:256] if t % 2 == 0 else scr[:, 256:512]
                            if len(klist) >= 2:
                                for tk_ in klist[-2]:
                                    pb.wait("scalar", tk_)
                        SC(lambda e, d=dst, s=pp[bk][:, 0:256]:
                           e.activation(d, s, AF.Sigmoid), 1)
                        etok = pb.tok("scalar")
                        bank_free[bk] = etok
                        if si_ >= 2:
                            pb.wait("sync", etok)
                            kdma = DMA("sync",
                                       sp_k[:, 512 * t + (si_ - 2) * 256:512 * t + (si_ - 2) * 256 + 256],
                                       dst)
                        pb.wait("tensor", etok)
                        last = mm(pp[rb][0:1, rc:rc + 256], onescol, dst,
                                  t == 0, t == NT - 1, inc=(1 if t == NT - 1 else 0))
                        if si_ >= 2:
                            klist.append((pb.tok("tensor"), kdma))
                    slice_cons[si_] = pb.tok("tensor")
                rtok = pb.tok("tensor")
                pb.wait("vector", rtok)
                rqk = row(1024)
                V(lambda e, d=rqk[0:1, 0:512], s=pp[4][0:1, 0:512]: e.tensor_copy(d, s))
                V(lambda e, d=rqk[0:1, 512:1024], s=pp[5][0:1, 0:512]:
                  e.tensor_copy(d, s), 1)
                bank_free[4] = pb.tok("vector"); bank_free[5] = bank_free[4]
                pb.wait("gpsimd", bank_free[4])
                pb.xwait("gpsimd", ("gpsimd", pb.cnt["gpsimd"]))
                ob = pb.dma("gpsimd", lambda e, d=cc1i, s=rqk: e.dma_start(out=d, in_=s))
                pb.xwait("gpsimd", ob)
                cct = collective(cc1i, cc1o)
                pb.xwait("gpsimd", cct)
                rbk = pb.dma("gpsimd", lambda e, d=rqk, s=cc1o: e.dma_start(out=d, in_=s))
                pb.wait("vector", rbk)
                V(lambda e, d=rqk[0:1, 512:1024], s=rqk[0:1, 512:1024]:
                  e.tensor_scalar_add(d, s, EPSA))
                V(lambda e, d=rqk[0:1, 0:512], s=rqk[0:1, 0:512]:
                  e.tensor_scalar_add(d, s, EPSA), 1)
                er1 = scr2[0:1, 16:24]; er2 = scr2[0:1, 24:32]
                V(lambda e, d=er1, s=rqk[0:1, 512:1024].rearrange("o (h d) -> o h d", d=64):
                  e.reduce_sum(d, s, axis=AX))
                V(lambda e, d=er2, s=rqk[0:1, 0:512].rearrange("o (h d) -> o h d", d=64):
                  e.reduce_sum(d, s, axis=AX))
                V(lambda e, d=er1, s=er1: e.tensor_scalar_mul(d, s, EPSA))
                V(lambda e, d=er2, s=er2: e.tensor_scalar_mul(d, s, EPSA), 1)
                pb.wait("tensor", pb.tok("vector"))
                bcast_row(rqk[0:1, 512:1024], kapb, 512, bank=2)
                bcast_row(rqk[0:1, 0:512], kapb2, 512, bank=3)
                bcast_row(er1, eksi, 8, bank=2)
                bcast_row(er2, ekso, 8, bank=3)
                if g == 0:
                    tap(1024, qst[:, 0, :], 512)
                    tap(1536, kapb, 512)
                    tap(2048, kapb2, 512)

                pb.dmaq("vector"); pb.dmaq("scalar"); pb.dmaq("tensor")
                sf = sfA if g == 0 else sfB
                sfv = sf.rearrange("p (t h) -> p t h", h=8)
                aw_reset()
                krb = [aw(512), aw(512)]
                rdt = {}
                vcons = {}
                rdt[0] = DMA("sync", krb[0], sp_k[:, 0:512])
                rdt[1] = DMA("sync", krb[1], sp_k[:, 512:1024])
                pb.wait("tensor", bank_free[4]); pb.wait("tensor", bank_free[5])
                for t in range(NT):
                    pb.tick()
                    ktile = krb[t % 2]
                    pb.wait("vector", rdt[t])
                    qtile = qst[:, t, :]
                    tq = scr[:, 0:512]
                    V(lambda e, d=tq, a=qtile, b=kapb: e.tensor_mul(d, a, b))
                    th8 = scr2[:, 32:40]
                    V(lambda e, d=th8, s=tq.rearrange("p (h d) -> p h d", d=64):
                      e.reduce_sum(d, s, axis=AX))
                    V(lambda e, d=th8, a=th8, b=eksi: e.tensor_add(d, a, b))
                    V(lambda e, d=sfv[:, t, :], s=th8: e.reciprocal(d, s))
                    tk2 = scr[:, 512:1024]
                    V(lambda e, d=tk2, a=ktile, b=kapb2: e.tensor_mul(d, a, b))
                    th8b = scr2[:, 40:48]
                    V(lambda e, d=th8b, s=tk2.rearrange("p (h d) -> p h d", d=64):
                      e.reduce_sum(d, s, axis=AX))
                    so8 = scr2[:, 48:56]
                    V(lambda e, d=th8b, a=th8b, b=ekso: e.tensor_add(d, a, b))
                    V(lambda e, d=so8, s=th8b: e.reciprocal(d, s))
                    kso = scr[:, 512:1024]
                    ttb(kso, ktile, so8, 8, ALU.mult)
                    qsi = scr[:, 0:512]
                    ttb(qsi, qtile, sfv[:, t, :], 8, ALU.mult, 1)
                    vtk = pb.tok("vector")
                    vcons[t] = vtk
                    pb.wait("tensor", vtk)
                    mm(pp[4][0:1, 0:512], onescol, kso, t == 0, t == NT - 1)
                    mm(pp[5][0:1, 0:512], onescol, qsi, t == 0, t == NT - 1,
                       inc=(1 if t == NT - 1 else 0))
                    if t + 2 < NT:
                        pb.wait("sync", vcons[t])
                        rdt[t + 2] = DMA("sync", krb[t % 2],
                                         sp_k[:, 512 * (t + 2):512 * (t + 3)])
                rtok = pb.tok("tensor")
                pb.wait("vector", rtok)
                rab = rqk
                V(lambda e, d=rab[0:1, 0:512], s=pp[4][0:1, 0:512]: e.tensor_copy(d, s))
                V(lambda e, d=rab[0:1, 512:1024], s=pp[5][0:1, 0:512]:
                  e.tensor_copy(d, s), 1)
                bank_free[4] = pb.tok("vector"); bank_free[5] = bank_free[4]
                pb.wait("gpsimd", bank_free[4])
                pb.xwait("gpsimd", ("gpsimd", pb.cnt["gpsimd"]))
                ob = pb.dma("gpsimd", lambda e, d=cc1i, s=rab: e.dma_start(out=d, in_=s))
                pb.xwait("gpsimd", ob)
                cct = collective(cc1i, cc1o)
                pb.xwait("gpsimd", cct)
                rbk = pb.dma("gpsimd", lambda e, d=rab, s=cc1o: e.dma_start(out=d, in_=s))
                pb.wait("vector", rbk)
                V(lambda e, d=rab[0:1, 0:512], s=rab[0:1, 0:512]:
                  e.tensor_scalar_add(d, s, EPSA))
                V(lambda e, d=rab[0:1, 512:1024], s=rab[0:1, 512:1024]:
                  e.tensor_scalar_add(d, s, EPSA), 1)
                V(lambda e, d=er1, s=rab[0:1, 0:512].rearrange("o (h d) -> o h d", d=64):
                  e.reduce_sum(d, s, axis=AX))
                V(lambda e, d=er2, s=rab[0:1, 512:1024].rearrange("o (h d) -> o h d", d=64):
                  e.reduce_sum(d, s, axis=AX))
                V(lambda e, d=er1, s=er1: e.tensor_scalar_mul(d, s, EPSA))
                V(lambda e, d=er2, s=er2: e.tensor_scalar_mul(d, s, EPSA), 1)
                pb.wait("tensor", pb.tok("vector"))
                bcast_row(rab[0:1, 0:512], kapb, 512, bank=2)
                bcast_row(rab[0:1, 512:1024], kapb2, 512, bank=3)
                bcast_row(er1, eksi, 8, bank=2)
                bcast_row(er2, ekso, 8, bank=3)

                if g == 0:
                    tap(2560, sfv[:, 0, :], 8)
                    tap(2880, rab, 1024)
                pb.dmaq("vector"); pb.dmaq("scalar"); pb.dmaq("tensor")
                aw_reset()
                row_reset()
                pb.wait("sync", pb.tok("tensor"))
                wv0b = aw(2048).bitcast(BF16).rearrange("p (v j n) -> p v j n", v=2, n=256)
                wv1b = aw(2048).bitcast(BF16).rearrange("p (v j n) -> p v j n", v=2, n=256)
                krb = [aw(512), aw(512)]
                vstage = aw(1024).rearrange("p (j n) -> p j n", n=256)
                dwvr = dWv.rearrange("(j p) o -> p j o", p=128)
                spv = None
                for half in range(2):
                    wvb = wv0b if half == 0 else wv1b
                    cw = g * 512 + half * 256
                    for ch in range(2):
                        pb.wait("sync", spv)
                        ldc = DMA("sync", vstage,
                                  dwvr[:, 4 * ch:4 * ch + 4, cw:cw + 256])
                        spv = split_w(
                            vstage.rearrange("p j n -> p (j n)"),
                            wvb[:, 0, 4 * ch:4 * ch + 4, :].rearrange("p j n -> p (j n)"),
                            wvb[:, 1, 4 * ch:4 * ch + 4, :].rearrange("p j n -> p (j n)"),
                            ldc)
                brv = row(512)
                ldb = DMA("sync", brv, dbv[:, g * 512:g * 512 + 512])
                pb.wait("tensor", spv); pb.wait("tensor", ldb)
                evv = eAm.rearrange("p (t h) -> p t h", h=8)
                kvsb = kvsbm
                V(lambda e, d=kvsb: e.memset(d, 0.0), 1)
                rdt = {}
                rdt[0] = DMA("sync", krb[0], sp_k[:, 0:512])
                rdt[1] = DMA("sync", krb[1], sp_k[:, 512:1024])
                pb.wait("tensor", bank_free[6]); pb.wait("tensor", bank_free[7])
                kvcons = {}
                mtoks = {}
                etoks = {}

                def a3_A(t):
                    p = t % 2
                    ktile = krb[p]
                    blk_ = scr if p == 0 else scr2
                    sm_ = smA[:, 32 + 32 * p:32 + 32 * p + 24]
                    pb.wait("vector", rdt[t])
                    for tk_ in kvcons.get(t - 2, ()):
                        pb.wait("vector", tk_)
                    qtile = qst[:, t, :]
                    tq = blk_[:, 0:512]
                    V(lambda e, d=tq, a=qtile, b=kapb: e.tensor_mul(d, a, b))
                    th8 = sm_[:, 0:8]
                    V(lambda e, d=th8, s=tq.rearrange("p (h d) -> p h d", d=64):
                      e.reduce_sum(d, s, axis=AX))
                    V(lambda e, d=th8, a=th8, b=eksi: e.tensor_add(d, a, b), 1)
                    pb.wait("scalar", pb.tok("vector"))
                    sa8 = sm_[:, 8:16]
                    SC(lambda e, d=sa8, s=th8: e.activation(d, s, AF.Sigmoid), 1)
                    pb.wait("vector", pb.tok("scalar"))
                    V(lambda e, d=sfv[:, t, :], a=sfv[:, t, :], b=sa8: e.tensor_mul(d, a, b))
                    tk2 = blk_[:, 0:512]
                    V(lambda e, d=tk2, a=ktile, b=kapb2: e.tensor_mul(d, a, b))
                    th8b = sm_[:, 16:24]
                    V(lambda e, d=th8b, s=tk2.rearrange("p (h d) -> p h d", d=64):
                      e.reduce_sum(d, s, axis=AX))
                    V(lambda e, d=th8b, a=th8b, b=ekso: e.tensor_add(d, a, b))
                    V(lambda e, d=th8b, a=th8b: e.tensor_scalar(d, a, -1.0, 1.0,
                                                                op0=ALU.max, op1=ALU.min), 1)
                    pb.wait("scalar", pb.tok("vector"))
                    SC(lambda e, d=evv[:, t, :], s=th8b: e.activation(d, s, AF.Exp), 1)
                    etoks[t] = pb.tok("scalar")
                    hts = []
                    for half in range(2):
                        bk = (2 + half) if p == 0 else (4 + half)
                        wvb = wv0b if half == 0 else wv1b
                        pb.wait("tensor", bank_free[bk])
                        emit_proj3(t, pp[bk][:, 0:256], wvb[:, 0], wvb[:, 1])
                        biasfold(pp[bk][:, 0:256], brv[0:1, 256 * half:256 * half + 256],
                                 stop=True, inc=1)
                        hts.append(pb.tok("tensor"))
                    mtoks[t] = hts

                def a3_B(t):
                    p = t % 2
                    ktile = krb[p]
                    blk_ = scr if p == 0 else scr2
                    pb.wait("tensor", etoks[t])
                    mm(pp[7][0:1, 0:8], onescol, evv[:, t, :], t == 0, t == NT - 1)
                    vs = []
                    for half in range(2):
                        bk = (2 + half) if p == 0 else (4 + half)
                        pb.wait("vector", mtoks[t][half])
                        pb.wait("vector", etoks[t])
                        vsl = blk_[:, 512 + 256 * half:512 + 256 * half + 256]
                        ttb(vsl, pp[bk][:, 0:256], evv[:, t, 4 * half:4 * half + 4],
                            4, ALU.mult, 1)
                        vtok2 = pb.tok("vector")
                        bank_free[bk] = vtok2
                        vs.append((vsl, vtok2))
                    pb.wait("tensor", bank_free[6])
                    for half in range(2):
                        vsl, vtok2 = vs[half]
                        pb.wait("tensor", vtok2)
                        for hh in range(4):
                            h = half * 4 + hh
                            dst = pp[6][(h % 2) * 64:(h % 2) * 64 + 64,
                                        128 * half + (hh // 2) * 64:128 * half + (hh // 2) * 64 + 64]
                            mm(dst, ktile[:, 64 * h:64 * h + 64], vsl[:, 64 * hh:64 * hh + 64],
                               True, True, inc=(1 if hh == 3 else 0))
                    kvt = pb.tok("tensor")
                    pb.wait("vector", kvt)
                    V(lambda e, d=kvsb, a=pp[6][:, 0:256], b=kvsb: e.tensor_add(d, a, b), 1)
                    bank_free[6] = pb.tok("vector")
                    kvcons[t] = (kvt, bank_free[6])
                    if t + 2 < NT:
                        pb.wait("sync", kvt)
                        rdt[t + 2] = DMA("sync", krb[p], sp_k[:, 512 * (t + 2):512 * (t + 3)])

                a3_A(0)
                for t in range(NT):
                    if t + 1 < NT:
                        a3_A(t + 1)
                    a3_B(t)
                ktok = pb.tok("tensor")
                pb.wait("vector", ktok)
                sstage = scr2[:, 0:264]
                V(lambda e, d=sstage[:, 256:264]: e.memset(d, 0.0))
                V(lambda e, d=sstage[:, 0:256], s=kvsb: e.tensor_copy(d, s))
                V(lambda e, d=sstage[0:1, 256:264], s=pp[7][0:1, 0:8]:
                  e.tensor_copy(d, s), 1)
                bank_free[6] = pb.tok("vector"); bank_free[7] = bank_free[6]
                pb.wait("gpsimd", bank_free[6])
                pb.xwait("gpsimd", ("gpsimd", pb.cnt["gpsimd"]))
                ob = pb.dma("gpsimd", lambda e, d=cc3i, s=sstage: e.dma_start(out=d, in_=s))
                pb.xwait("gpsimd", ob)
                cct = collective(cc3i, cc3o)
                pb.xwait("gpsimd", cct)
                pb.wait("gpsimd", pb.tok("vector"))
                pb.wait("gpsimd", pb.tok("tensor"))
                kvg = kapb[:, 0:256]
                rb1 = pb.dma("gpsimd", lambda e, d=kvg, s=cc3o[:, 0:256]:
                             e.dma_start(out=d, in_=s))
                esr = scr2[0:1, 16:24]
                rb2 = pb.dma("gpsimd", lambda e, d=esr, s=cc3o[0:1, 256:264]:
                             e.dma_start(out=d, in_=s))
                pb.wait("vector", rb2)
                if g == 0:
                    tap(2848, esr, 8)
                V(lambda e, d=esr, s=esr: e.reciprocal(d, s))
                V(lambda e, d=esr, s=esr: e.tensor_scalar_mul(d, s, float(L)), 1)
                pb.wait("tensor", pb.tok("vector"))
                bcast_row(esr, scb, 8, bank=2)
                if g == 0:
                    tap(2568, evv[:, 0, :], 8)
                    tap(2576, kvg, 256)
                    tap(2832, scb, 8)
                    tap(2840, sfv[:, 0, :], 8)

                pb.dmaq("vector"); pb.dmaq("scalar"); pb.dmaq("tensor")
                aw_reset()
                row_reset()
                pb.wait("sync", pb.tok("tensor"))
                wo_sl = aw(4096).rearrange("p (j n) -> p j n", n=1024)
                ldo = DMA("sync", wo_sl,
                          dWo.rearrange("(j p) o -> p j o", p=128)[:, 4 * g:4 * g + 4, :])
                bor = row(1024) if g == 0 else None
                pb.wait("tensor", ldo)
                if g == 0:
                    ldb2 = DMA("sync", bor, dbo)
                    pb.wait("tensor", ldb2)
                pb.wait("vector", rb1)
                a4cons = {}
                a4ev = {}

                def a4_A(t):
                    p = t % 2
                    qe = scr[:, 0:512] if p == 0 else scr2[:, 0:512]
                    qeT = scr[:, 512:1024] if p == 0 else scr2[:, 512:1024]
                    trb = 0 if p == 0 else 6
                    otb = 1 if p == 0 else 7
                    f2 = smA[:, 8 * p:8 * p + 8]
                    for tk_ in a4cons.get(t - 2, ()):
                        pb.wait("vector", tk_)
                    V(lambda e, d=f2, a=sfv[:, t, :], b=scb: e.tensor_mul(d, a, b))
                    ttb(qe, qst[:, t, :], f2, 8, ALU.mult, 1)
                    vtok3 = pb.tok("vector")
                    pb.wait("tensor", vtok3)
                    pb.wait("tensor", bank_free[trb])
                    for j in range(4):
                        tr(pp[trb][:, 128 * j:128 * (j + 1)], qe[:, 128 * j:128 * (j + 1)],
                           inc=(1 if j == 3 else 0))
                    trt = pb.tok("tensor")
                    pb.wait("scalar", trt)
                    for tk_ in a4cons.get(t - 2, ()):
                        pb.wait("scalar", tk_)
                    for j in range(4):
                        SC(lambda e, d=qeT[:, 128 * j:128 * (j + 1)],
                           s=pp[trb][:, 128 * j:128 * (j + 1)]: e.copy(d, s),
                           inc=(1 if j == 3 else 0))
                    et = pb.tok("scalar")
                    bank_free[trb] = et
                    pb.wait("tensor", et)
                    pb.wait("tensor", bank_free[otb])
                    for h in range(8):
                        kv_h = kvg[(h % 2) * 64:(h % 2) * 64 + 64,
                                   (h // 2) * 64:(h // 2) * 64 + 64]
                        qh = qeT[(h % 2) * 64:(h % 2) * 64 + 64,
                                 128 * (h // 2):128 * (h // 2) + 128]
                        dst = pp[otb][(h % 2) * 64:(h % 2) * 64 + 64,
                                      128 * (h // 2):128 * (h // 2) + 128]
                        mm(dst, kv_h, qh, True, True, inc=(1 if h == 7 else 0))
                    a4ev[t] = pb.tok("tensor")

                def a4_B(t):
                    p = t % 2
                    qeT = scr[:, 512:1024] if p == 0 else scr2[:, 512:1024]
                    otb = 1 if p == 0 else 7
                    outT = qeT
                    pb.wait("vector", a4ev[t])
                    V(lambda e, d=outT, s=pp[otb][:, 0:512]: e.tensor_copy(d, s), 1)
                    ovt = pb.tok("vector")
                    bank_free[otb] = ovt
                    pb.wait("tensor", ovt)
                    bk2 = (2, 3) if p == 0 else (4, 5)
                    pb.wait("tensor", bank_free[bk2[0]])
                    pb.wait("tensor", bank_free[bk2[1]])
                    for half in range(2):
                        for j in range(4):
                            mm(pp[bk2[half]][:, 0:512], outT[:, 128 * j:128 * (j + 1)],
                               wo_sl[:, j, 512 * half:512 * half + 512],
                               j == 0, g == 1 and j == 3,
                               inc=(1 if (g == 1 and j == 3 and half == 1) else 0))
                        if g == 0:
                            biasfold(pp[bk2[half]][:, 0:512],
                                     bor[0:1, 512 * half:512 * half + 512],
                                     stop=True, inc=(1 if half == 1 else 0))
                    wt = pb.tok("tensor")
                    pb.wait("vector", wt)
                    for half in range(2):
                        V(lambda e, d=xt(t)[:, 512 * half:512 * half + 512],
                          a=pp[bk2[half]][:, 0:512],
                          b=xt(t)[:, 512 * half:512 * half + 512]:
                          e.tensor_add(d, a, b), (1 if half == 1 else 0))
                    xvt = pb.tok("vector")
                    bank_free[bk2[0]] = xvt
                    bank_free[bk2[1]] = xvt
                    a4cons[t] = (wt, xvt)

                a4_A(0)
                for t in range(NT):
                    if t + 1 < NT:
                        a4_A(t + 1)
                    a4_B(t)
                pb.barrier()

        def phase_ffn():
            row_reset()
            ab_reset()
            aw_reset()
            eb_sb = aw(1024)
            ldeb = DMA("sync", eb_sb[0:8, 0:1024], deb)
            gwsl = aw(64).rearrange("p (j n) -> p j n", n=8)
            ldgw = DMA("sync", gwsl, dgw.rearrange("(j p) o -> p j o", p=128))
            gbr = row(8)
            ldgb = DMA("sync", gbr, dgb)
            gwb = aw(64).bitcast(BF16).rearrange("p (v j n) -> p v j n", v=2, n=8)
            pb.wait("scalar", ldgw)
            SC(lambda e, d=gwb[:, 0], s=gwsl: e.copy(d, s), 1)
            pb.wait("vector", pb.tok("scalar"))
            V(lambda e, d=gwb[:, 1], a=gwsl, c=gwb[:, 0]:
              e.scalar_tensor_tensor(d, a, 1.0, c, op0=ALU.mult, op1=ALU.subtract), 1)
            gsp = pb.tok("vector")
            gatv = gat.rearrange("p (t h) -> p t h", h=8)
            for t in range(NT):
                pb.tick()
                emit_stats(t, mcol[:, t:t + 1], rcol[:, t:t + 1])
                stok = emit_xnt_bf(t, mcol[:, t:t + 1], rcol[:, t:t + 1])
                if t == 0:
                    pb.wait("tensor", ldeb)
                    pb.wait("tensor", ldgb)
                    pb.wait("tensor", gsp)
                pb.wait("tensor", stok)
                pb.wait("tensor", bank_free[2])
                for j in range(8):
                    mm(pp[2][:, 0:8], nTh(t)[:, 128 * j:128 * (j + 1)], gwb[:, 0, j, :],
                       j == 0, False)
                    mm(pp[2][:, 0:8], nTh(t)[:, 128 * j:128 * (j + 1)], gwb[:, 1, j, :],
                       False, False)
                    mm(pp[2][:, 0:8], nTl(t)[:, 128 * j:128 * (j + 1)], gwb[:, 0, j, :],
                       False, False)
                biasfold(pp[2][:, 0:8], gbr, stop=True, inc=1)
                gt = pb.tok("tensor")
                pb.wait("vector", gt)
                lg = scr2[:, 32:40]
                V(lambda e, d=lg, s=pp[2][:, 0:8]: e.tensor_copy(d, s), 1)
                bank_free[2] = pb.tok("vector")
                mx = scr2[:, 40:41]
                V(lambda e, d=mx, s=lg: e.reduce_max(d, s, axis=AX))
                V(lambda e, d=mx, s=mx: e.tensor_scalar_mul(d, s, -1.0), 1)
                pb.wait("scalar", pb.tok("vector"))
                eg = scr2[:, 41:49]
                SC(lambda e, d=eg, s=lg, b=mx: e.activation(d, s, AF.Exp, bias=b), 1)
                pb.wait("vector", pb.tok("scalar"))
                sm = scr2[:, 49:50]
                V(lambda e, d=sm, s=eg: e.reduce_sum(d, s, axis=AX))
                V(lambda e, d=sm, s=sm: e.reciprocal(d, s))
                gg = scr2[:, 50:58]
                V(lambda e, d=gg, a=eg, b=sm: e.tensor_scalar_mul(d, a, b))
                sel = scr2[:, 58:66]
                V(lambda e, d=sel, s=gg: e.tensor_single_scalar(d, s, 0.1, op=ALU.is_ge))
                V(lambda e, d=gg, a=gg, b=sel: e.tensor_mul(d, a, b))
                ws = scr2[:, 66:67]
                V(lambda e, d=ws, s=gg: e.reduce_sum(d, s, axis=AX))
                V(lambda e, d=ws, s=ws: e.reciprocal(d, s))
                V(lambda e, d=gatv[:, t, :], a=gg, b=ws: e.tensor_scalar_mul(d, a, b), 1)
                gvt = pb.tok("vector")
                pb.wait("tensor", gvt)
                pb.wait("tensor", bank_free[3])
                tr(pp[3][0:8, 0:128], gatv[:, t, :], inc=1)
                wtt = pb.tok("tensor")
                pb.wait("scalar", wtt)
                wT = scr2[:, 128:256]
                SC(lambda e, d=wT[0:8, 0:128], s=pp[3][0:8, 0:128]: e.copy(d, s), 1)
                stt = pb.tok("scalar")
                bank_free[3] = stt
                pb.wait("tensor", stt)
                bk2 = (4, 5) if t % 2 == 0 else (6, 7)
                pb.wait("tensor", bank_free[bk2[0]]); pb.wait("tensor", bank_free[bk2[1]])
                for half in range(2):
                    mm(pp[bk2[half]][:, 0:512], wT[0:8, 0:128],
                       eb_sb[0:8, 512 * half:512 * half + 512], True, True,
                       inc=(1 if half == 1 else 0))
                wbt = pb.tok("tensor")
                pb.wait("vector", wbt)
                for half in range(2):
                    V(lambda e, d=xt(t)[:, 512 * half:512 * half + 512],
                      a=pp[bk2[half]][:, 0:512], b=xt(t)[:, 512 * half:512 * half + 512]:
                      e.tensor_add(d, a, b), (1 if half == 1 else 0))
                bank_free[bk2[0]] = pb.tok("vector")
                bank_free[bk2[1]] = bank_free[bk2[0]]
            pb.barrier()
            stage_w = aB[:, 0:2048].rearrange("p (j n) -> p j n", n=256)
            wqb = [aw(2048).bitcast(BF16).rearrange("p (v j n) -> p v j n", v=2, n=256)
                   for _ in range(2)]
            order = [(e_, q_) for e_ in range(8) for q_ in range(4)]
            loads = {}
            splits = {}
            mmdone = {}
            dre = dew.rearrange("e (j p) o -> e p j o", p=128)

            def f1_load(idx):
                e_, q_ = order[idx]
                pb.wait("sync", splits.get(idx - 1))
                loads[idx] = DMA("sync", stage_w, dre[e_][:, :, 256 * q_:256 * q_ + 256])

            def f1_split(idx):
                wb = wqb[idx % 2]
                pb.wait("scalar", loads[idx])
                pb.wait("scalar", mmdone.get(idx - 2))
                SC(lambda e, d=wb[:, 0].rearrange("p j n -> p (j n)"),
                   s=stage_w.rearrange("p j n -> p (j n)"): e.copy(d, s), 1)
                hs = pb.tok("scalar")
                pb.wait("vector", hs)
                pb.wait("vector", mmdone.get(idx - 2))
                V(lambda e, d=wb[:, 1].rearrange("p j n -> p (j n)"),
                  a=stage_w.rearrange("p j n -> p (j n)"),
                  c=wb[:, 0].rearrange("p j n -> p (j n)"):
                  e.scalar_tensor_tensor(d, a, 1.0, c, op0=ALU.mult, op1=ALU.subtract), 1)
                splits[idx] = pb.tok("vector")

            f1_load(0)
            f1_split(0)
            f1_load(1)
            for idx, (e_, q_) in enumerate(order):
                wb = wqb[idx % 2]
                if idx + 1 < len(order):
                    f1_split(idx + 1)
                if idx + 2 < len(order):
                    f1_load(idx + 2)
                pb.wait("tensor", splits[idx])
                for t in range(NT):
                    bk = 2 if (idx * NT + t) % 2 == 0 else 3
                    pb.wait("tensor", bank_free[bk])
                    emit_proj3(t, pp[bk][:, 0:256], wb[:, 0], wb[:, 1], inc=1)
                    pt = pb.tok("tensor")
                    pb.wait("vector", pt)
                    c0 = 256 * q_
                    V(lambda e, d=xt(t)[:, c0:c0 + 256], a=pp[bk][:, 0:256],
                      s=gatv[:, t, e_:e_ + 1], b=xt(t)[:, c0:c0 + 256]:
                      e.scalar_tensor_tensor(d, a, s, b, op0=ALU.mult, op1=ALU.add), 1)
                    bank_free[bk] = pb.tok("vector")
                mmdone[idx] = pb.tok("tensor")
            pb.barrier()

        def phase_attn_b():
            row_reset()
            DMA("sync", sp_x2, xsb)
            for t in range(NT):
                pb.tick()
                var = emit_stats(t, mcol[:, t:t + 1], rcol[:, t:t + 1])
                tv2 = scr2[:, 16:17]
                V(lambda e, d=tv2, s=var: e.tensor_scalar_add(d, s, 1e-5))
                V(lambda e, d=tv2, s=tv2: e.reciprocal(d, s))
                V(lambda e, d=tv2, a=var, b=tv2: e.tensor_mul(d, a, b))
                r1 = scr2[:, 17:18]
                rsqrt_newton(r1, tv2, 1e-5, scr2[:, 18:19], scr2[:, 19:20])
                rt_ = scr2[:, 20:21]
                V(lambda e, d=rt_, a=rcol[:, t:t + 1], b=r1: e.tensor_mul(d, a, b), 1)
                emit_xnt_bf(t, mcol[:, t:t + 1], rt_)
            pb.barrier()
            l1r = row(1024)
            ld1 = DMA("sync", l1r, dln1w)
            pb.wait("tensor", ld1)
            for j in range(8):
                row2col(l1r[0:1, 128 * j:128 * (j + 1)], l1wc[:, j:j + 1], 128, bank=2)
            pb.wait("sync", pb.tok("tensor"))
            ld2 = DMA("sync", l1r, dln1b)
            pb.wait("tensor", ld2)
            for j in range(8):
                row2col(l1r[0:1, 128 * j:128 * (j + 1)], l1bc[:, j:j + 1], 128, bank=2)
            pb.barrier()

            for g in range(4):
                ab_reset()
                spfx = ab(4096)
                rbk = [ab(512), ab(512)]
                rbv = [ab(256), ab(256)]
                S_run = scr[0:64, 512:768]
                gmat = scr2[0:64, 0:256]
                c0 = 256 * g
                dmv = dmask.rearrange("p (h n) -> p h n", n=128)
                for i in range(4):
                    h = 4 * g + i
                    G(lambda e, d=dmv[:, i, :]: e.iota(
                        d, pattern=[[1, 128]], base=0, channel_multiplier=-1,
                        allow_small_or_imprecise_dtypes=True), 1)
                    pb.wait("scalar", pb.tok("gpsimd"))
                    SC(lambda e, d=dmv[:, i, :], s=dmv[:, i, :], sc=lngb[:, h:h + 1]:
                       e.activation(d, s, AF.Exp, scale=sc), 1)
                    pb.wait("gpsimd", pb.tok("scalar"))
                    G(lambda e, d=dmv[:, i, :]: e.affine_select(
                        out=d, in_=d, compare_op=ALU.is_ge, fill=0.0, base=0,
                        pattern=[[1, 128]], channel_multiplier=-1), 1)
                iof = scr2[:, 0:128]
                G(lambda e, d=iof: e.iota(d, pattern=[[1, 128]], base=1,
                                          channel_multiplier=0,
                                          allow_small_or_imprecise_dtypes=True), 1)
                pb.wait("scalar", pb.tok("gpsimd"))
                for i in range(4):
                    h = 4 * g + i
                    SC(lambda e, d=gqnew[:, 128 * i:128 * (i + 1)], s=iof[0:64, :],
                       sc=lngb[0:64, h:h + 1]:
                       e.activation(d, s, AF.Exp, scale=sc), 1)
                pb.wait("vector", pb.tok("scalar"))
                for i in range(4):
                    hc = 4 * g + i
                    V(lambda e, d=gmat[:, 64 * i:64 * (i + 1)],
                      s=g128h[0:64, hc:hc + 1].broadcast_to([64, 64]):
                      e.tensor_copy(d, s), (1 if i == 3 else 0))
                pb.barrier()

                aw_reset()
                pb.wait("sync", pb.tok("tensor"))
                wk_b = aw(2048).bitcast(BF16).rearrange("p (v j n) -> p v j n", v=2, n=256)
                wv_b = aw(2048).bitcast(BF16).rearrange("p (v j n) -> p v j n", v=2, n=256)
                bstage = aB[:, 5632:7680].rearrange("p (j n) -> p j n", n=256)
                row_reset()
                ckr = row(256)
                cvr = row(256)
                bspl = None
                for wi_, (wd_, wb_, crow_, cc_) in enumerate(
                        [(dbwk, wk_b, ckr, slice(0, 256)), (dbwv, wv_b, cvr, slice(256, 512))]):
                    pb.wait("sync", bspl)
                    ldw_ = DMA("sync", bstage,
                               wd_.rearrange("(j p) o -> p j o", p=128)[:, :, c0:c0 + 256])
                    pb.wait("tensor", ldw_)
                    if wi_ == 0:
                        pb.wait("tensor", bank_free[4])
                    for j in range(8):
                        mm(pp[4][0:1, cc_], l1bc[:, j:j + 1], bstage[:, j, :],
                           wi_ == 0 and j == 0, wi_ == 1 and j == 7,
                           inc=(1 if j == 7 else 0))
                    ct_ = pb.tok("tensor")
                    pb.wait("vector", ct_)
                    V(lambda e, d=crow_, s=pp[4][0:1, cc_]: e.tensor_copy(d, s), 1)
                    if wi_ == 1:
                        bank_free[4] = pb.tok("vector")
                    for j in range(8):
                        V(lambda e, d=bstage[:, j, :], a=bstage[:, j, :], s=l1wc[:, j:j + 1]:
                          e.tensor_scalar_mul(d, a, s), (1 if j == 7 else 0))
                    wst_ = pb.tok("vector")
                    bspl = split_w(bstage.rearrange("p j n -> p (j n)"),
                                   wb_[:, 0].rearrange("p j n -> p (j n)"),
                                   wb_[:, 1].rearrange("p j n -> p (j n)"), wst_)
                pb.wait("tensor", bspl)
                V(lambda e, d=S_run: e.memset(d, 0.0), 1)
                spdma = {}
                kvcons = {}
                evtoks = {}
                prtoks = {}
                pbuf = [aB[:, 5632 + 1280 * p:5632 + 1280 * (p + 1)] for p in range(2)]

                def p1a_A(t):
                    p = t % 2
                    bk = 2 if p == 0 else 3
                    pb.wait("tensor", bank_free[bk])
                    emit_proj3(t, pp[bk][:, 0:256], wk_b[:, 0], wk_b[:, 1])
                    biasfold(pp[bk][:, 0:256], ckr, stop=True, inc=0)
                    emit_proj3(t, pp[bk][:, 256:512], wv_b[:, 0], wv_b[:, 1], start=True)
                    biasfold(pp[bk][:, 256:512], cvr, stop=True, inc=1)
                    pt = pb.tok("tensor")
                    prtoks[t] = pt
                    pb.wait("scalar", pt)
                    for tk_ in kvcons.get(t - 2, ()):
                        pb.wait("scalar", tk_)
                    ktile = pbuf[p][:, 0:256]
                    vtile = pbuf[p][:, 256:512]
                    SC(lambda e, d=ktile, s=pp[bk][:, 0:256]: e.copy(d, s), 1)
                    SC(lambda e, d=vtile, s=pp[bk][:, 256:512]: e.copy(d, s), 1)
                    ek = pb.tok("scalar")
                    bank_free[bk] = ek
                    evtoks[t] = ek

                def p1a_B(t):
                    p = t % 2
                    ktile = pbuf[p][:, 0:256]
                    vtile = pbuf[p][:, 256:512]
                    vd = pbuf[p][:, 512:768]
                    kT = pbuf[p][0:64, 768:1280]
                    trb = 4 if p == 0 else 6
                    cb = 5 if p == 0 else 7
                    pb.wait("tensor", evtoks[t])
                    pb.wait("tensor", bank_free[trb])
                    for i in range(4):
                        tr(pp[trb][0:64, 128 * i:128 * (i + 1)],
                           ktile[:, 64 * i:64 * i + 64], inc=(1 if i == 3 else 0))
                    ttk = pb.tok("tensor")
                    pb.wait("scalar", ttk)
                    for tk_ in spdma.get(t - 2, ()):
                        pb.wait("scalar", tk_)
                    SC(lambda e, d=kT, s=pp[trb][0:64, 0:512]: e.copy(d, s), 1)
                    ktt = pb.tok("scalar")
                    bank_free[trb] = ktt
                    pb.wait("sync", ktt)
                    spk_ = DMA("sync",
                               sp_k[p * 64:p * 64 + 64, 512 * (t // 2):512 * (t // 2) + 512],
                               kT)
                    spv_ = DMA("sync", sp_b[:, 256 * t:256 * t + 256], vtile)
                    spdma[t] = (spk_, spv_)
                    pb.wait("vector", evtoks[t])
                    for tk_ in kvcons.get(t - 2, ()):
                        pb.wait("vector", tk_)
                    V(lambda e, d=spfx[0:64, 256 * t:256 * t + 256], s=S_run:
                      e.tensor_copy(d, s))
                    ttb(vd, vtile, gvcol[:, 4 * g:4 * g + 4], 4, ALU.mult, 1)
                    cvt = pb.tok("vector")
                    pb.wait("tensor", cvt)
                    pb.wait("tensor", bank_free[cb])
                    for i in range(4):
                        mm(pp[cb][0:64, 64 * i:64 * i + 64],
                           ktile[:, 64 * i:64 * i + 64], vd[:, 64 * i:64 * i + 64],
                           True, True, inc=(1 if i == 3 else 0))
                    ctk = pb.tok("tensor")
                    pb.wait("vector", ctk)
                    V(lambda e, d=S_run, a=S_run, b=gmat:
                      e.tensor_tensor(d, a, b, op=ALU.mult))
                    V(lambda e, d=S_run, a=pp[cb][0:64, 0:256], b=S_run:
                      e.tensor_add(d, a, b), 1)
                    svt = pb.tok("vector")
                    bank_free[cb] = svt
                    kvcons[t] = (ctk, svt, spk_, spv_)

                p1a_A(0)
                for t in range(NT):
                    if t + 1 < NT:
                        p1a_A(t + 1)
                    p1a_B(t)
                pb.dmaq("vector")
                ccsvi = ccsi.rearrange("(a b) n -> a (b n)", b=2)
                ccsvo = ccso.rearrange("(a b) n -> a (b n)", b=2)
                S_msk = scr[0:64, 0:256]
                V(lambda e, d=S_msk, a=S_run, s=m1c[0:64, :]:
                  e.tensor_scalar_mul(d, a, s), 1)
                pb.wait("gpsimd", pb.tok("vector"))
                pb.xwait("gpsimd", ("gpsimd", pb.cnt["gpsimd"]))
                ob = pb.dma("gpsimd", lambda e, d=ccsvi, s=S_msk: e.dma_start(out=d, in_=s))
                pb.xwait("gpsimd", ob)
                cct = collective(ccsi, ccso)
                pb.xwait("gpsimd", cct)
                S_init = scr[0:64, 256:512]
                rbt = pb.dma("gpsimd", lambda e, d=S_init, s=ccsvo: e.dma_start(out=d, in_=s))
                pb.wait("vector", rbt)
                V(lambda e, d=S_init, a=S_init, s=m2c[0:64, :]:
                  e.tensor_scalar_mul(d, a, s), 1)
                gm4 = pbsc[0:64, 0:4]
                g4 = g128h[0:64, 4 * g:4 * g + 4]
                V(lambda e, d=gm4: e.memset(d, 1.0), 1)
                for t in range(NT):
                    if t > 0:
                        V(lambda e, d=gm4, a=gm4, b=g4: e.tensor_mul(d, a, b))
                    for i in range(4):
                        V(lambda e, d=spfx[0:64, 256 * t + 64 * i:256 * t + 64 * i + 64],
                          a=S_init[:, 64 * i:64 * i + 64], s=gm4[:, i:i + 1],
                          b=spfx[0:64, 256 * t + 64 * i:256 * t + 64 * i + 64]:
                          e.scalar_tensor_tensor(d, a, s, b, op0=ALU.mult, op1=ALU.add),
                          (1 if i == 3 else 0))
                pb.barrier()

                aw_reset()
                pb.wait("sync", pb.tok("tensor"))
                wq_b = aw(2048).bitcast(BF16).rearrange("p (v j n) -> p v j n", v=2, n=256)
                wg_b = aw(2048).bitcast(BF16).rearrange("p (v j n) -> p v j n", v=2, n=256)
                wo_sl = aw(2048).rearrange("p (j n) -> p j n", n=1024)
                ldo = DMA("sync", wo_sl,
                          dbwo.rearrange("(j p) o -> p j o", p=128)[:, 2 * g:2 * g + 2, :])
                cqr = row(256); cgr = row(256)
                bstage = aB[:, 5632:7680].rearrange("p (j n) -> p j n", n=256)
                bspl = None
                for wi_, (wd_, wb_, crow_, cc_) in enumerate(
                        [(dbwq, wq_b, cqr, slice(0, 256)), (dbwg, wg_b, cgr, slice(256, 512))]):
                    pb.wait("sync", bspl)
                    ldw_ = DMA("sync", bstage,
                               wd_.rearrange("(j p) o -> p j o", p=128)[:, :, c0:c0 + 256])
                    pb.wait("tensor", ldw_)
                    if wi_ == 0:
                        pb.wait("tensor", bank_free[4])
                    for j in range(8):
                        mm(pp[4][0:1, cc_], l1bc[:, j:j + 1], bstage[:, j, :],
                           wi_ == 0 and j == 0, wi_ == 1 and j == 7,
                           inc=(1 if j == 7 else 0))
                    ct_ = pb.tok("tensor")
                    pb.wait("vector", ct_)
                    V(lambda e, d=crow_, s=pp[4][0:1, cc_]: e.tensor_copy(d, s), 1)
                    if wi_ == 1:
                        bank_free[4] = pb.tok("vector")
                    for j in range(8):
                        V(lambda e, d=bstage[:, j, :], a=bstage[:, j, :], s=l1wc[:, j:j + 1]:
                          e.tensor_scalar_mul(d, a, s), (1 if j == 7 else 0))
                    wst_ = pb.tok("vector")
                    bspl = split_w(bstage.rearrange("p j n -> p (j n)"),
                                   wb_[:, 0].rearrange("p j n -> p (j n)"),
                                   wb_[:, 1].rearrange("p j n -> p (j n)"), wst_)
                pb.wait("tensor", bspl)
                pb.wait("tensor", ldo)
                rdk = {0: (DMA("sync", rbk[0][0:64, :], sp_k[0:64, 0:512]),
                           DMA("sync", rbv[0], sp_b[:, 0:256])),
                       1: (DMA("sync", rbk[1][0:64, :], sp_k[64:128, 0:512]),
                           DMA("sync", rbv[1], sp_b[:, 256:512]))}
                rcons = {}
                bcons = {}
                evtoks = {}
                sqtoks = {}
                qbuf = [aB[:, 5632 + 512 * p:5632 + 512 * (p + 1)] for p in range(2)]

                def p1b_A(t):
                    p = t % 2
                    bk = 2 if p == 0 else 4
                    pb.wait("tensor", bank_free[bk])
                    emit_proj3(t, pp[bk][:, 0:256], wq_b[:, 0], wq_b[:, 1])
                    biasfold(pp[bk][:, 0:256], cqr, stop=True, inc=0)
                    emit_proj3(t, pp[bk][:, 256:512], wg_b[:, 0], wg_b[:, 1], start=True)
                    biasfold(pp[bk][:, 256:512], cgr, stop=True, inc=1)
                    pt = pb.tok("tensor")
                    pb.wait("scalar", pt)
                    for tk_ in bcons.get(t - 2, ()):
                        pb.wait("scalar", tk_)
                    qtile = qbuf[p][:, 0:256]
                    sgt = qbuf[p][:, 256:512]
                    SC(lambda e, d=qtile, s=pp[bk][:, 0:256]:
                       e.activation(d, s, AF.Copy, scale=float(HD) ** -0.5))
                    SC(lambda e, d=sgt, s=pp[bk][:, 256:512]:
                       e.activation(d, s, AF.Silu), 1)
                    qst_ = pb.tok("scalar")
                    bank_free[bk] = qst_
                    evtoks[t] = qst_

                def p1b_B(t):
                    p = t % 2
                    qtile = qbuf[p][:, 0:256]
                    sgt = qbuf[p][:, 256:512]
                    pb.wait("tensor", evtoks[t])
                    pb.wait("tensor", bank_free[3])
                    for i in range(4):
                        tr(pp[3][0:64, 128 * i:128 * (i + 1)],
                           qtile[:, 64 * i:64 * i + 64], inc=(1 if i == 3 else 0))
                    qtt = pb.tok("tensor")
                    pb.wait("vector", qtt)
                    qT = scr[0:64, 0:512]
                    qTg = scr2[0:64, 0:512]
                    V(lambda e, d=qT, s=pp[3][0:64, 0:512]: e.tensor_copy(d, s))
                    V(lambda e, d=qTg, a=pp[3][0:64, 0:512], b=gqnew:
                      e.tensor_tensor(d, a, b, op=ALU.mult), 1)
                    qvt = pb.tok("vector")
                    bank_free[3] = qvt
                    pb.wait("tensor", bank_free[5])
                    pb.wait("tensor", qvt)
                    pb.wait("tensor", rdk[t][0])
                    pb.wait("tensor", rdk[t][1])
                    kbuf = rbk[p]
                    vbuf = rbv[p]
                    for i in range(4):
                        mm(pp[5][:, 128 * i:128 * (i + 1)],
                           kbuf[0:64, 128 * i:128 * (i + 1)],
                           qT[:, 128 * i:128 * (i + 1)], True, True,
                           inc=(1 if i == 3 else 0))
                    sct = pb.tok("tensor")
                    pb.wait("vector", sct)
                    for tk_ in sqtoks.get(t - 1, ()):
                        pb.wait("vector", tk_)
                    sD = scr2[:, 512:1024]
                    V(lambda e, d=sD, a=pp[5][:, 0:512], b=dmask:
                      e.tensor_tensor(d, a, b, op=ALU.mult), 1)
                    sdt = pb.tok("vector")
                    bank_free[5] = sdt
                    pb.wait("tensor", bank_free[6])
                    pb.wait("tensor", sdt)
                    for i in range(4):
                        mm(pp[6][:, 64 * i:64 * i + 64], sD[:, 128 * i:128 * (i + 1)],
                           vbuf[:, 64 * i:64 * i + 64], True, False)
                        mm(pp[6][:, 64 * i:64 * i + 64],
                           qTg[:, 128 * i:128 * (i + 1)],
                           spfx[0:64, 256 * t + 64 * i:256 * t + 64 * i + 64],
                           False, True, inc=(1 if i == 3 else 0))
                    ot = pb.tok("tensor")
                    rcons[t] = ot
                    pb.wait("vector", ot)
                    ov = scr[:, 512:768]
                    V(lambda e, d=ov, s=pp[6][:, 0:256]: e.tensor_copy(d, s), 1)
                    bank_free[6] = pb.tok("vector")
                    mu = pbsc[:, 0:4]
                    V(lambda e, d=mu, s=ov.rearrange("p (h d) -> p h d", d=64):
                      e.reduce_sum(d, s, axis=AX))
                    V(lambda e, d=mu, s=mu: e.tensor_scalar_mul(d, s, 1.0 / 64.0), 1)
                    pb.wait("scalar", pb.tok("vector"))
                    sq = scr2[:, 512:768]
                    SC(lambda e, d=sq, s=ov: e.activation(d, s, AF.Square), 1)
                    sqtoks[t] = (pb.tok("scalar"),)
                    pb.wait("vector", pb.tok("scalar"))
                    sqm = pbsc[:, 4:8]
                    V(lambda e, d=sqm, s=sq.rearrange("p (h d) -> p h d", d=64):
                      e.reduce_sum(d, s, axis=AX))
                    V(lambda e, d=sqm, s=sqm: e.tensor_scalar_mul(d, s, 1.0 / 64.0))
                    mu2 = pbsc[:, 8:12]
                    V(lambda e, d=mu2, a=mu, b=mu: e.tensor_mul(d, a, b))
                    V(lambda e, d=sqm, a=sqm, b=mu2: e.tensor_sub(d, a, b), 1)
                    rs = pbsc[:, 12:16]
                    rsqrt_newton(rs, sqm, 1e-6, pbsc[:, 16:20], pbsc[:, 20:24])
                    gated = scr[:, 512:768]
                    ttb(gated, ov, mu, 4, ALU.subtract)
                    ttb(gated, gated, rs, 4, ALU.mult)
                    V(lambda e, d=gated, a=gated, b=sgt: e.tensor_mul(d, a, b), 1)
                    gvt = pb.tok("vector")
                    pb.wait("tensor", gvt)
                    pb.wait("tensor", bank_free[3])
                    for c_ in range(2):
                        tr(pp[3][:, 128 * c_:128 * (c_ + 1)],
                           gated[:, 128 * c_:128 * (c_ + 1)], inc=(1 if c_ == 1 else 0))
                    gtt = pb.tok("tensor")
                    pb.wait("scalar", gtt)
                    gT = scr[:, 768:1024]
                    for c_ in range(2):
                        SC(lambda e, d=gT[:, 128 * c_:128 * (c_ + 1)],
                           s=pp[3][:, 128 * c_:128 * (c_ + 1)]: e.copy(d, s),
                           (1 if c_ == 1 else 0))
                    gst2 = pb.tok("scalar")
                    bank_free[3] = gst2
                    pb.wait("tensor", gst2)
                    bk2 = (7, 0)
                    pb.wait("tensor", bank_free[bk2[0]]); pb.wait("tensor", bank_free[bk2[1]])
                    for half in range(2):
                        for c_ in range(2):
                            mm(pp[bk2[half]][:, 0:512], gT[:, 128 * c_:128 * (c_ + 1)],
                               wo_sl[:, c_, 512 * half:512 * half + 512], c_ == 0, c_ == 1,
                               inc=(1 if (half == 1 and c_ == 1) else 0))
                    wt = pb.tok("tensor")
                    pb.wait("vector", wt)
                    if g == 0:
                        for half in range(2):
                            zt = scr2[:, 0:512]
                            V(lambda e, d=zt, s=xt(t)[:, 512 * half:512 * half + 512],
                              m=mcol[:, t:t + 1], r=rcol[:, t:t + 1]:
                              e.tensor_scalar(d, s, m, r, op0=ALU.subtract, op1=ALU.mult))
                            V(lambda e, d=xt(t)[:, 512 * half:512 * half + 512],
                              a=zt, b=xt(t)[:, 512 * half:512 * half + 512]:
                              e.tensor_add(d, a, b))
                            V(lambda e, d=xt(t)[:, 512 * half:512 * half + 512],
                              a=pp[bk2[half]][:, 0:512],
                              b=xt(t)[:, 512 * half:512 * half + 512]:
                              e.tensor_add(d, a, b), (1 if half == 1 else 0))
                    else:
                        for half in range(2):
                            V(lambda e, d=xt(t)[:, 512 * half:512 * half + 512],
                              a=pp[bk2[half]][:, 0:512],
                              b=xt(t)[:, 512 * half:512 * half + 512]:
                              e.tensor_add(d, a, b), (1 if half == 1 else 0))
                    xvt = pb.tok("vector")
                    bank_free[bk2[0]] = xvt
                    bank_free[bk2[1]] = xvt
                    bcons[t] = (wt, xvt)
                    if t + 2 < NT:
                        pb.wait("sync", rcons[t])
                        c2 = 512 * ((t + 2) // 2)
                        rdk[t + 2] = (DMA("sync", rbk[p][0:64, :],
                                          sp_k[p * 64:p * 64 + 64, c2:c2 + 512]),
                                      DMA("sync", rbv[p],
                                          sp_b[:, 256 * (t + 2):256 * (t + 2) + 256]))

                p1b_A(0)
                for t in range(NT):
                    if t + 1 < NT:
                        p1b_A(t + 1)
                    p1b_B(t)
                pb.barrier()

        def phase_swiglu():
            row_reset()
            ab_reset()
            x2rb = [ab(1024), ab(1024)]
            dump = ab(1024)
            mid0 = ab(384); mid1 = ab(384)
            mids = [mid0, mid1]
            l2wb = kapb.rearrange("p n -> p n")
            l2wb2 = kapb2
            l2r = row(1024)
            ld1 = DMA("sync", l2r, dln2w)
            pb.wait("tensor", ld1)
            bcast_row(l2r[0:1, 0:512], kapb, 512, bank=2)
            bcast_row(l2r[0:1, 512:1024], kapb2, 512, bank=3)
            pb.wait("sync", pb.tok("tensor"))
            ld2 = DMA("sync", l2r, dln2b)
            pb.wait("tensor", ld2)
            l2bc = l1bc
            for j in range(8):
                row2col(l2r[0:1, 128 * j:128 * (j + 1)], l2bc[:, j:j + 1], 128, bank=2)
            pb.wait("sync", pb.tok("tensor"))
            boutr = l2r
            DMA("sync", boutr, dbout)
            rd = {0: DMA("sync", x2rb[0], sp_x2[:, 0:1024])}
            ucons = {}
            for t in range(NT):
                pb.tick()
                if t + 1 < NT:
                    pb.wait("sync", ucons.get(t - 1))
                    rd[t + 1] = DMA("sync", x2rb[(t + 1) % 2],
                                    sp_x2[:, 1024 * (t + 1):1024 * (t + 2)])
                pb.wait("vector", rd[t])
                u = scr[:, 0:1024]
                V(lambda e, d=u, a=xt(t), b=x2rb[t % 2]: e.tensor_sub(d, a, b), 1)
                ucons[t] = pb.tok("vector")
                sums = scr2[:, 0:1]; sqs = scr2[:, 1:2]
                V(lambda e, d=sums, s=u: e.reduce_sum(d, s, axis=AX), 1)
                pb.wait("scalar", pb.tok("vector"))
                SC(lambda e, d=dump, s=u: e.activation(d, s, AF.Square), 1)
                pb.wait("vector", pb.tok("scalar"))
                V(lambda e, d=sqs, s=dump: e.reduce_sum(d, s, axis=AX))
                V(lambda e, d=mcol[:, t:t + 1], s=sums: e.tensor_scalar_mul(d, s, 1.0 / 1024.0))
                tv = scr2[:, 2:3]
                V(lambda e, d=tv, s=sqs: e.tensor_scalar_mul(d, s, 1.0 / 1024.0))
                tm = scr2[:, 3:4]
                V(lambda e, d=tm, a=mcol[:, t:t + 1], b=mcol[:, t:t + 1]: e.tensor_mul(d, a, b))
                V(lambda e, d=tv, a=tv, b=tm: e.tensor_sub(d, a, b))
                rsqrt_newton(rcol[:, t:t + 1], tv, 1e-5, scr2[:, 4:5], scr2[:, 5:6])
                un = scr[:, 0:1024]
                V(lambda e, d=un, s=u, m=mcol[:, t:t + 1], r=rcol[:, t:t + 1]:
                  e.tensor_scalar(d, s, m, r, op0=ALU.subtract, op1=ALU.mult))
                V(lambda e, d=un[:, 0:512], a=un[:, 0:512], b=kapb: e.tensor_mul(d, a, b))
                V(lambda e, d=un[:, 512:1024], a=un[:, 512:1024], b=kapb2:
                  e.tensor_mul(d, a, b), 1)
                vtok = pb.tok("vector")
                pb.wait("tensor", vtok)
                pb.wait("tensor", bank_free[0]); pb.wait("tensor", bank_free[1])
                for j in range(8):
                    b = 0 if j < 4 else 1
                    tr(pp[b][:, 128 * (j % 4):128 * (j % 4 + 1)],
                       un[:, 128 * j:128 * (j + 1)], inc=(1 if j in (3, 7) else 0))
                ttok = pb.tok("tensor")
                pb.wait("scalar", ttok)
                for j in range(8):
                    b = 0 if j < 4 else 1
                    SC(lambda e, d=nT(t)[:, 128 * j:128 * (j + 1)],
                       s=pp[b][:, 128 * (j % 4):128 * (j % 4 + 1)]: e.copy(d, s),
                       inc=(1 if j in (3, 7) else 0))
                stok = pb.tok("scalar")
                bank_free[0] = stok; bank_free[1] = stok
            pb.barrier()

            sgs = [list(range(s, min(s + 3, NT))) for s in range(0, NT, 3)]
            aw_reset()
            wia = [aw(1024).rearrange("p (j n) -> p j n", n=128) for _ in range(2)]
            wib = [aw(1024).rearrange("p (j n) -> p j n", n=128) for _ in range(2)]
            wou = [aw(1024) for _ in range(2)]
            dwr = dwin.rearrange("(j p) o -> p j o", p=128)
            mccons = {}
            ccons = {}
            for sgi, sg in enumerate(sgs):
                n_t = len(sg)
                for mc_i in range(NMC):
                    gi = sgi * NMC + mc_i
                    w = MC[mc_i]
                    pbuf = gi % 2
                    pb.wait("sync", mccons.get(gi - 2))
                    lda = DMA("sync", wia[pbuf][:, :, 0:w],
                              dwr[:, :, 128 * mc_i:128 * mc_i + w])
                    ldb_ = DMA("sync", wib[pbuf][:, :, 0:w],
                               dwr[:, :, HID + 128 * mc_i:HID + 128 * mc_i + w])
                    ldo = DMA("sync", wou[pbuf][0:w, 0:1024],
                              dwout[128 * mc_i:128 * mc_i + w, :])
                    ca = ccst[:, 2 * mc_i:2 * mc_i + 1]
                    cb = ccst[:, 2 * mc_i + 1:2 * mc_i + 2]
                    pb.wait("tensor", lda); pb.wait("tensor", ldb_)
                    pb.wait("tensor", ldo)
                    if sgi == 0:
                        bra = scr2[0:1, 8:136]
                        brb = scr2[0:1, 136:264]
                        pb.wait("sync", ccons.get(gi - 1))
                        la = DMA("sync", bra[0:1, 0:w], dbin[:, 128 * mc_i:128 * mc_i + w])
                        lb = DMA("sync", brb[0:1, 0:w],
                                 dbin[:, HID + 128 * mc_i:HID + 128 * mc_i + w])
                        pb.wait("tensor", la); pb.wait("tensor", lb)
                        pb.wait("tensor", bank_free[0]); pb.wait("tensor", bank_free[1])
                        for j in range(8):
                            mm(pp[0][0:w, 0:1], wia[pbuf][:, j, 0:w], l2bc[:, j:j + 1],
                               j == 0, False)
                        for j in range(8):
                            mm(pp[0][0:w, 1:2], wib[pbuf][:, j, 0:w], l2bc[:, j:j + 1],
                               False, j == 7)
                        mm(pp[0][0:w, 2:3], bra[0:1, 0:w], ones11, False, True)
                        mm(pp[0][0:w, 3:4], brb[0:1, 0:w], ones11, False, True, inc=1)
                        cct = pb.tok("tensor")
                        ccons[gi] = cct
                        pb.wait("vector", cct)
                        V(lambda e, d=ca[0:w, :], s=pp[0][0:w, 0:1]: e.tensor_copy(d, s))
                        V(lambda e, d=ca[0:w, :], a=pp[0][0:w, 2:3], b=ca[0:w, :]:
                          e.tensor_add(d, a, b))
                        V(lambda e, d=cb[0:w, :], s=pp[0][0:w, 1:2]: e.tensor_copy(d, s))
                        V(lambda e, d=cb[0:w, :], a=pp[0][0:w, 3:4], b=cb[0:w, :]:
                          e.tensor_add(d, a, b), 1)
                        cvt = pb.tok("vector")
                        pb.wait("tensor", cvt)
                    pb.wait("tensor", bank_free[0]); pb.wait("tensor", bank_free[1])
                    for ti, t in enumerate(sg):
                        for j in range(8):
                            mm(pp[0][0:w, 128 * ti:128 * ti + 128], wia[pbuf][:, j, 0:w],
                               nT(t)[:, 128 * j:128 * (j + 1)], j == 0, j == 7)
                        for j in range(8):
                            mm(pp[1][0:w, 128 * ti:128 * ti + 128], wib[pbuf][:, j, 0:w],
                               nT(t)[:, 128 * j:128 * (j + 1)], j == 0,
                               j == 7,
                               inc=(1 if (j == 7 and ti == n_t - 1) else 0))
                    abt = pb.tok("tensor")
                    pb.wait("scalar", abt)
                    sb_ = scr[:, 0:384]
                    SC(lambda e, d=sb_[0:w, 0:128 * n_t], s=pp[1][0:w, 0:128 * n_t],
                       b=cb[0:w, :]: e.activation(d, s, AF.Silu, bias=b), 1)
                    sbt = pb.tok("scalar")
                    pb.wait("vector", sbt)
                    mtile = mids[pbuf]
                    V(lambda e, d=mtile[0:w, 0:128 * n_t], a=pp[0][0:w, 0:128 * n_t],
                      s=ca[0:w, :], b=sb_[0:w, 0:128 * n_t]:
                      e.scalar_tensor_tensor(d, a, s, b, op0=ALU.add, op1=ALU.mult), 1)
                    mvt = pb.tok("vector")
                    bank_free[0] = mvt; bank_free[1] = mvt
                    pb.wait("tensor", mvt)
                    for ti, t in enumerate(sg):
                        for half in range(2):
                            bk = 2 + 2 * ti + half
                            if mc_i == 0:
                                pb.wait("tensor", bank_free[bk])
                            mm(pp[bk][:, 0:512], mtile[0:w, 128 * ti:128 * ti + 128],
                               wou[pbuf][0:w, 512 * half:512 * half + 512],
                               mc_i == 0, mc_i == NMC - 1,
                               inc=(1 if (mc_i == NMC - 1 and half == 1 and ti == n_t - 1) else 0))
                    mccons[gi] = pb.tok("tensor")
                ot = pb.tok("tensor")
                pb.wait("vector", ot)
                for ti, t in enumerate(sg):
                    for half in range(2):
                        bk = 2 + 2 * ti + half
                        V(lambda e, d=xt(t)[:, 512 * half:512 * half + 512],
                          a=pp[bk][:, 0:512], b=xt(t)[:, 512 * half:512 * half + 512]:
                          e.tensor_add(d, a, b),
                          (1 if (ti == n_t - 1 and half == 1) else 0))
                vt = pb.tok("vector")
                for i_ in range(2, 8):
                    bank_free[i_] = vt
            pb.barrier()
            bcast_row(boutr[0:1, 0:512], kapb, 512, bank=2)
            bcast_row(boutr[0:1, 512:1024], kapb2, 512, bank=3)
            for t in range(NT):
                V(lambda e, d=xt(t)[:, 0:512], a=kapb, b=xt(t)[:, 0:512]:
                  e.tensor_add(d, a, b))
                V(lambda e, d=xt(t)[:, 512:1024], a=kapb2, b=xt(t)[:, 512:1024]:
                  e.tensor_add(d, a, b), (1 if t == NT - 1 else 0))
            pb.barrier()

        if stage >= 1:
            phase_attn_a()
        if stage >= 2:
            phase_ffn()
        if stage >= 3:
            phase_attn_b()
        if stage >= 4:
            phase_swiglu()
        if stage >= 5:
            phase_ffn()

        pb.barrier()
        otok = DMA("sync", dy.rearrange("(t p) d -> p t d", p=128),
                   xsb.rearrange("p (t d) -> p t d", d=1024))
        pb.xwait("sync", otok)

        blk = ctx.enter_context(nc.Block())
        for e in ENGS:
            def mk(e=e):
                def emit(eng):
                    for entry in pb.q[e]:
                        if entry[0] == "op":
                            _, a, sname, amt = entry
                            inst = a(eng)
                            if sname is not None:
                                inst.then_inc(sems[sname], amt)
                        else:
                            _, sname, v = entry
                            eng.wait_ge(sems[sname], v)
                return emit
            getattr(blk, e)(mk())

    print(f"[kernel] built stage={stage} ops={pb.nops} " +
          " ".join(f"{e}:{len(pb.q[e])}" for e in ENGS))
    return nc


def kernel(**inputs):
    nc = build()
    x = np.ascontiguousarray(inputs["L0_attn_a_x"], dtype=np.float32)
    base = {
        "aWq": inputs["L0_attn_a_Wq"], "abq": inputs["L0_attn_a_bq"].reshape(1, -1),
        "aWk": inputs["L0_attn_a_Wk"], "abk": inputs["L0_attn_a_bk"].reshape(1, -1),
        "aWv": inputs["L0_attn_a_Wv"], "abv": inputs["L0_attn_a_bv"].reshape(1, -1),
        "aWo": inputs["L0_attn_a_Wo"], "abo": inputs["L0_attn_a_bo"].reshape(1, -1),
        "gw": inputs["L0_ffn_gate_w"], "gb": inputs["L0_ffn_gate_b"].reshape(1, -1),
        "ew": inputs["L0_ffn_expert_w"], "eb": inputs["L0_ffn_expert_b"],
        "ln1w": inputs["L0_attn_b_ln1_w"].reshape(1, -1),
        "ln1b": inputs["L0_attn_b_ln1_b"].reshape(1, -1),
        "bwq": inputs["L0_attn_b_wq"], "bwk": inputs["L0_attn_b_wk"],
        "bwv": inputs["L0_attn_b_wv"], "bwg": inputs["L0_attn_b_wg"],
        "bwo": inputs["L0_attn_b_wo"],
        "ln2w": inputs["L0_attn_b_ln2_w"].reshape(1, -1),
        "ln2b": inputs["L0_attn_b_ln2_b"].reshape(1, -1),
        "win": inputs["L0_attn_b_w_in"], "bin": inputs["L0_attn_b_b_in"].reshape(1, -1),
        "wout": inputs["L0_attn_b_w_out"], "bout": inputs["L0_attn_b_b_out"].reshape(1, -1),
    }
    base = {k: np.ascontiguousarray(v, dtype=np.float32) for k, v in base.items()}
    in_maps = []
    for c in range(NCORE):
        m = dict(base)
        m["x"] = np.ascontiguousarray(x[c // 2, (c % 2) * TPC:(c % 2 + 1) * TPC])
        m["m1"] = np.full((128, 1), 1.0 if c % 2 == 0 else 0.0, np.float32)
        m["m2"] = np.full((128, 1), 0.0 if c % 2 == 0 else 1.0, np.float32)
        in_maps.append(m)
    res = run_bass_kernel_spmd(nc, in_maps, core_ids=list(range(NCORE)))
    out = np.empty((Bb, L, D), np.float32)
    for c in range(NCORE):
        out[c // 2, (c % 2) * TPC:(c % 2 + 1) * TPC] = res.results[c]["y"]
    if os.environ.get("MK_TAP"):
        global DBG
        DBG = [res.results[c].get("dbg") for c in range(NCORE)]
    return out


if __name__ == "__main__":
    build()


# version 09 (final)
# speedup vs baseline: 1.0320x; 1.0175x over previous
import os
import math
from contextlib import ExitStack
import numpy as np
import concourse.bass as bass
import concourse.mybir as mybir
from concourse.bass_utils import run_bass_kernel_spmd

F32 = mybir.dt.float32
BF16 = mybir.dt.bfloat16
AF = mybir.ActivationFunctionType
ALU = mybir.AluOpType
AX = mybir.AxisListType.X

STAGE = int(os.environ.get("MK_STAGE", "5"))

NCORE = 8
Bb, L, D, H, HD = 4, 4096, 1024, 16, 64
TPC, NT = 2048, 16
HID = 2730
NMC = 22
MC = [128] * 21 + [42]
EPSA = 1e-6
RG = [[0, 1], [2, 3], [4, 5], [6, 7]]
ENGS = ("tensor", "vector", "scalar", "sync", "gpsimd")


class PB:
    def __init__(self):
        self.dsems = {"sync": [f"d{i}" for i in range(6)],
                      "gpsimd": [f"g{i}" for i in range(3)]}
        self.SEMS = list(ENGS) + self.dsems["sync"] + self.dsems["gpsimd"]
        self.q = {e: [] for e in ENGS}
        self.cnt = {s: 0 for s in self.SEMS}
        self.dirty = {e: False for e in ENGS}
        self.hw = {(a, b): 0 for a in ENGS for b in self.SEMS}
        self.rr = {"sync": 0, "gpsimd": 0}
        self.nops = 0

    def do(self, e, fn, inc=0):
        if e in ("vector", "scalar") and inc == 0:
            inc = 1
        self.q[e].append(("op", fn, e if inc else None, inc))
        self.nops += 1
        if inc:
            self.cnt[e] += inc
            self.dirty[e] = False
            if e in ("vector", "scalar"):
                self.q[e].append(("wait", e, self.cnt[e]))
        else:
            self.dirty[e] = True
        return (e, self.cnt[e])

    def dma(self, e, fn):
        pool = self.dsems[e]
        s = pool[self.rr[e] % len(pool)]
        self.rr[e] += 1
        self.xwait(e, (s, self.cnt[s]))
        self.q[e].append(("op", fn, s, 16))
        self.nops += 1
        self.cnt[s] += 16
        return (s, self.cnt[s])

    def tok(self, e):
        if self.dirty[e]:
            for i in range(len(self.q[e]) - 1, -1, -1):
                ent = self.q[e][i]
                if ent[0] == "op" and ent[2] is None:
                    self.cnt[e] += 1
                    self.q[e][i] = ("op", ent[1], e, 1)
                    if e in ("vector", "scalar"):
                        self.q[e].append(("wait", e, self.cnt[e]))
                    break
            self.dirty[e] = False
        return (e, self.cnt[e])

    def wait(self, w, tokn):
        if tokn is None:
            return
        e, v = tokn
        if e == w or v <= 0 or self.hw[(w, e)] >= v:
            return
        self.hw[(w, e)] = v
        self.q[w].append(("wait", e, v))

    def xwait(self, w, tokn):
        e, v = tokn
        if v <= 0:
            return
        if e != w and self.hw[(w, e)] >= v:
            return
        if e != w:
            self.hw[(w, e)] = v
        self.q[w].append(("wait", e, v))

    def barrier(self):
        toks = [self.tok(e) for e in ENGS]
        toks += [(s, self.cnt[s]) for s in self.SEMS if s not in ENGS]
        for w in ENGS:
            for t in toks:
                self.xwait(w, t) if t[0] == w else self.wait(w, t)

    def dmaq(self, w):
        for s_ in self.dsems["sync"] + self.dsems["gpsimd"]:
            self.wait(w, (s_, self.cnt[s_]))

    def tick(self, engines=("tensor", "vector", "scalar")):
        toks = [self.tok(e) for e in engines]
        for w in engines:
            for t in toks:
                self.wait(w, t)


def build(stage=STAGE):
    nc = bass.Bass("TRN2", target_bir_lowering=False)

    dx = nc.dram_tensor("x", [TPC, D], F32, kind="ExternalInput").ap()
    dy = nc.dram_tensor("y", [TPC, D], F32, kind="ExternalOutput").ap()

    def din(name, shape):
        return nc.dram_tensor(name, shape, F32, kind="ExternalInput").ap()

    dWq = din("aWq", [D, D]); dbq = din("abq", [1, D])
    dWk = din("aWk", [D, D]); dbk = din("abk", [1, D])
    dWv = din("aWv", [D, D]); dbv = din("abv", [1, D])
    dWo = din("aWo", [D, D]); dbo = din("abo", [1, D])
    dgw = din("gw", [D, 8]); dgb = din("gb", [1, 8])
    dew = din("ew", [8, D, D]); deb = din("eb", [8, D])
    dln1w = din("ln1w", [1, D]); dln1b = din("ln1b", [1, D])
    dbwq = din("bwq", [D, D]); dbwk = din("bwk", [D, D])
    dbwv = din("bwv", [D, D]); dbwg = din("bwg", [D, D])
    dbwo = din("bwo", [D, D])
    dln2w = din("ln2w", [1, D]); dln2b = din("ln2b", [1, D])
    dwin = din("win", [D, 2 * HID]); dbin = din("bin", [1, 2 * HID])
    dwout = din("wout", [HID, D]); dbout = din("bout", [1, D])
    dm1 = din("m1", [128, 1])
    dm2 = din("m2", [128, 1])

    sp_k = nc.dram_tensor("sp_k", [128, NT * 512], F32).ap()
    sp_b = nc.dram_tensor("sp_b", [128, NT * 512], F32).ap()
    sp_x2 = nc.dram_tensor("sp_x2", [128, NT * 1024], F32).ap()
    cc1i = nc.dram_tensor("cc1i", [1, 1024], F32).ap()
    cc1o = nc.dram_tensor("cc1o", [1, 1024], F32).ap()
    cc3i = nc.dram_tensor("cc3i", [128, 264], F32).ap()
    cc3o = nc.dram_tensor("cc3o", [128, 264], F32).ap()
    ccsi = nc.dram_tensor("ccsi", [128, 128], F32).ap()
    ccso = nc.dram_tensor("ccso", [128, 128], F32).ap()

    TAP = os.environ.get("MK_TAP", "")
    ddbg = (nc.dram_tensor("dbg", [128, 4096], F32, kind="ExternalOutput").ap()
            if TAP else None)

    pb = PB()

    with ExitStack() as ctx:
        sbal = lambda n, c: ctx.enter_context(nc.sbuf_tensor(n, [128, c], F32)).ap()
        xsb = sbal("xsb", NT * 1024)
        aA = sbal("aA", NT * 1024)
        aB = sbal("aB", 8192)
        aW = sbal("aW", 6144)
        aM = sbal("aM", 6088)
        pp = [ctx.enter_context(nc.psum_tensor(f"pp{i}", [128, 512], F32)).ap()
              for i in range(8)]
        sems = {n: ctx.enter_context(nc.semaphore(f"s_{n}")) for n in pb.SEMS}

        o_ = [0]
        def am(n):
            v = o_[0]; o_[0] += n
            assert o_[0] <= 6088, f"aM overflow {o_[0]}"
            return aM[:, v:v + n]
        ident = am(128)
        onesrow = am(128)
        onescol = am(1)
        lngp = am(8)
        lngb = am(16)
        g128 = am(8)
        gvcol = am(16)
        union = am(1024)
        mcol = am(16); rcol = am(16)
        rows = am(1024)
        scr = am(1024)
        scr2 = am(1024)
        kapb = am(512)
        kapb2 = am(512)
        seps = am(48)
        ccst = am(44)
        smA = am(128)
        kvsbm = am(256)
        g128h = am(16)
        pbsc = am(56)
        sfA = union[:, 0:128]; sfB = union[:, 128:256]; eAm = union[:, 256:384]
        dmask = union[:, 0:512]; gqnew = union[0:64, 512:1024]
        gat = union[:, 0:128]
        eksi = seps[:, 0:8]; ekso = seps[:, 8:16]; scb = seps[:, 16:24]
        m1c = seps[:, 24:25]; m2c = seps[:, 25:26]
        l1wc = seps[:, 26:34]; l1bc = seps[:, 34:42]
        ones11 = onesrow[0:1, 0:1]

        RW = [0]
        def row(n):
            v = RW[0]; RW[0] += n
            assert RW[0] <= 1024, f"rows overflow {RW[0]}"
            return rows[0:1, v:v + n]
        def row_reset():
            RW[0] = 0

        oB = [0]
        def ab(n):
            v = oB[0]; oB[0] += n
            assert oB[0] <= 8192, f"aB overflow {oB[0]}"
            return aB[:, v:v + n]
        def ab_reset():
            oB[0] = 0

        oW = [0]
        def aw(n):
            v = oW[0]; oW[0] += n
            assert oW[0] <= 6144, f"aW overflow {oW[0]}"
            return aW[:, v:v + n]
        def aw_reset():
            oW[0] = 0

        bank_free = {i: None for i in range(8)}

        def mm(dst, lhsT, rhs, start, stop, inc=0):
            return pb.do("tensor",
                         lambda e, d=dst, l=lhsT, r=rhs, s1=start, s2=stop:
                         e.matmul(d, l, r, start=s1, stop=s2, skip_group_check=True),
                         inc)

        def tr(dst, src, inc=0):
            return pb.do("tensor",
                         lambda e, d=dst, s=src: e.transpose(d, s, ident), inc)

        def V(fn, inc=0):
            return pb.do("vector", fn, inc)

        def SC(fn, inc=0):
            return pb.do("scalar", fn, inc)

        def G(fn, inc=0):
            pb.xwait("gpsimd", ("gpsimd", pb.cnt["gpsimd"]))
            return pb.do("gpsimd", fn, inc if inc else 1)

        def DMA(e, dst, src):
            return pb.dma(e, lambda eng, d=dst, s=src: eng.dma_start(out=d, in_=s))

        def tap(off, ap, n):
            if not TAP:
                return
            pb.barrier()
            DMA("sync", ddbg[0:ap.shape[0], off:off + n], ap)
            pb.barrier()

        def collective(ci, co):
            t = pb.do("gpsimd",
                      lambda e, i=ci, o=co: e.collective_compute(
                          "AllReduce", ALU.add, replica_groups=RG,
                          ins=[i.opt()], outs=[o.opt()]), 1)
            return t

        def ttb(dst, a, colap, nh, op, inc=0, dd=64):
            d3 = dst.rearrange("p (h d) -> p h d", d=dd)
            a3 = a.rearrange("p (h d) -> p h d", d=dd)
            b3 = colap.rearrange("p (h o) -> p h o", o=1).broadcast_to([128, nh, dd])
            return V(lambda e, d=d3, x=a3, y=b3, o=op: e.tensor_tensor(d, x, y, op=o),
                     inc)

        def xt(t):
            return xsb[:, 1024 * t:1024 * (t + 1)]

        def nT(t):
            return aA[:, 1024 * t:1024 * (t + 1)]

        def rsqrt_newton(dst, src, eps, t1, t2):
            V(lambda e, d=t1, s=src, ep=eps: e.tensor_scalar_add(d, s, ep), 1)
            pb.wait("scalar", pb.tok("vector"))
            SC(lambda e, d=t2, s=t1: e.activation(d, s, AF.Sqrt), 1)
            pb.wait("vector", pb.tok("scalar"))
            V(lambda e, d=dst, s=t2: e.reciprocal(d, s))
            V(lambda e, d=t2, a=dst, b=dst: e.tensor_mul(d, a, b))
            V(lambda e, d=t2, a=t1, b=t2: e.tensor_mul(d, a, b))
            V(lambda e, d=t2, a=t2: e.tensor_scalar(d, a, -0.5, 1.5,
                                                    op0=ALU.mult, op1=ALU.add))
            V(lambda e, d=dst, a=dst, b=t2: e.tensor_mul(d, a, b))

        def emit_stats(t, mdst, rdst, src_fn=xt, dbg=False):
            sums = scr2[:, 0:1]; sqs = scr2[:, 1:2]
            V(lambda e, d=sums, s=src_fn(t): e.reduce_sum(d, s, axis=AX), 1)
            pb.wait("scalar", pb.tok("vector"))
            SC(lambda e, d=scr[:, 0:1024], s=src_fn(t):
               e.activation(d, s, AF.Square), 1)
            pb.wait("vector", pb.tok("scalar"))
            V(lambda e, d=sqs, s=scr[:, 0:1024]: e.reduce_sum(d, s, axis=AX))
            V(lambda e, d=mdst, s=sums: e.tensor_scalar_mul(d, s, 1.0 / 1024.0))
            tv = scr2[:, 2:3]
            V(lambda e, d=tv, s=sqs: e.tensor_scalar_mul(d, s, 1.0 / 1024.0))
            tm = scr2[:, 3:4]
            V(lambda e, d=tm, a=mdst, b=mdst: e.tensor_mul(d, a, b))
            V(lambda e, d=tv, a=tv, b=tm: e.tensor_sub(d, a, b))
            rsqrt_newton(rdst, tv, 1e-5, scr2[:, 4:5], scr2[:, 5:6])
            return tv

        def emit_xnt(t, mdst, rdst):
            xn = scr[:, 0:1024]
            pb.wait("vector", pb.tok("scalar"))
            V(lambda e, d=xn, s=xt(t), m=mdst, r=rdst:
              e.tensor_scalar(d, s, m, r, op0=ALU.subtract, op1=ALU.mult), 1)
            vtok = pb.tok("vector")
            pb.wait("tensor", vtok)
            pb.wait("tensor", bank_free[0])
            pb.wait("tensor", bank_free[1])
            for j in range(8):
                b = 0 if j < 4 else 1
                tr(pp[b][:, 128 * (j % 4):128 * (j % 4 + 1)],
                   xn[:, 128 * j:128 * (j + 1)], inc=(1 if j in (3, 7) else 0))
            ttok = pb.tok("tensor")
            pb.wait("scalar", ttok)
            for j in range(8):
                b = 0 if j < 4 else 1
                SC(lambda e, d=nT(t)[:, 128 * j:128 * (j + 1)],
                   s=pp[b][:, 128 * (j % 4):128 * (j % 4 + 1)]: e.copy(d, s),
                   inc=(1 if j in (3, 7) else 0))
            stok = pb.tok("scalar")
            bank_free[0] = stok
            bank_free[1] = stok
            return stok

        aAb = aA.bitcast(BF16)

        def nTh(t):
            return aAb[:, 1024 * t:1024 * (t + 1)]

        def nTl(t):
            return aAb[:, 16384 + 1024 * t:16384 + 1024 * (t + 1)]

        def emit_xnt_bf(t, mdst, rdst):
            xn = scr[:, 0:1024]
            pb.wait("vector", pb.tok("scalar"))
            V(lambda e, d=xn, s=xt(t), m=mdst, r=rdst:
              e.tensor_scalar(d, s, m, r, op0=ALU.subtract, op1=ALU.mult), 1)
            vtok = pb.tok("vector")
            pb.wait("tensor", vtok)
            pb.wait("tensor", bank_free[0])
            pb.wait("tensor", bank_free[1])
            for j in range(8):
                b = 0 if j < 4 else 1
                tr(pp[b][:, 128 * (j % 4):128 * (j % 4 + 1)],
                   xn[:, 128 * j:128 * (j + 1)], inc=(1 if j in (3, 7) else 0))
            ttok = pb.tok("tensor")
            pb.wait("scalar", ttok)
            for j in range(8):
                b = 0 if j < 4 else 1
                SC(lambda e, d=nTh(t)[:, 128 * j:128 * (j + 1)],
                   s=pp[b][:, 128 * (j % 4):128 * (j % 4 + 1)]: e.copy(d, s),
                   inc=(1 if j in (3, 7) else 0))
            hstok = pb.tok("scalar")
            pb.wait("vector", hstok)
            for j in range(8):
                b = 0 if j < 4 else 1
                V(lambda e, d=nTl(t)[:, 128 * j:128 * (j + 1)],
                  a=pp[b][:, 128 * (j % 4):128 * (j % 4 + 1)],
                  c=nTh(t)[:, 128 * j:128 * (j + 1)]:
                  e.scalar_tensor_tensor(d, a, 1.0, c, op0=ALU.mult, op1=ALU.subtract),
                  inc=(1 if j in (3, 7) else 0))
            lstok = pb.tok("vector")
            bank_free[0] = lstok
            bank_free[1] = lstok
            return lstok

        def emit_proj3(t, psum_ap, whi, wlo, start=True, inc=0):
            tk = None
            for j in range(8):
                tk = mm(psum_ap, nTh(t)[:, 128 * j:128 * (j + 1)], whi[:, j, :],
                        start and j == 0, False)
                tk = mm(psum_ap, nTh(t)[:, 128 * j:128 * (j + 1)], wlo[:, j, :],
                        False, False)
                tk = mm(psum_ap, nTl(t)[:, 128 * j:128 * (j + 1)], whi[:, j, :],
                        False, False, inc=(inc if j == 7 else 0))
            return tk

        def split_w(staging_flat, hi_flat, lo_flat, ld_tok, cons=()):
            pb.wait("scalar", ld_tok)
            for tk_ in cons:
                pb.wait("scalar", tk_)
            SC(lambda e, d=hi_flat, s=staging_flat: e.copy(d, s), 1)
            hs = pb.tok("scalar")
            pb.wait("vector", hs)
            for tk_ in cons:
                pb.wait("vector", tk_)
            V(lambda e, d=lo_flat, a=staging_flat, c=hi_flat:
              e.scalar_tensor_tensor(d, a, 1.0, c, op0=ALU.mult, op1=ALU.subtract), 1)
            return pb.tok("vector")

        def emit_proj(t, psum_ap, wsl, start=True, inc=0, nj=8):
            tk = None
            for j in range(nj):
                tk = mm(psum_ap, nT(t)[:, 128 * j:128 * (j + 1)], wsl[:, j, :],
                        start and j == 0, False, inc=(inc if j == nj - 1 else 0))
            return tk

        def biasfold(psum_ap, brow, stop=True, inc=1):
            return mm(psum_ap, onesrow[0:1, 0:128], brow, False, stop, inc=inc)

        def row2col(rowap, colap, n, bank=2):
            nchunks = (n + 127) // 128
            pb.wait("tensor", bank_free[bank])
            for j in range(nchunks):
                w = min(128, n - 128 * j)
                mm(pp[bank][0:w, j:j + 1], rowap[0:1, 128 * j:128 * j + w], ones11,
                   True, True, inc=(1 if j == nchunks - 1 else 0))
            tt = pb.tok("tensor")
            pb.wait("vector", tt)
            for j in range(nchunks):
                w = min(128, n - 128 * j)
                V(lambda e, d=colap[0:w, j:j + 1], s=pp[bank][0:w, j:j + 1]:
                  e.tensor_copy(d, s), inc=(1 if j == nchunks - 1 else 0))
            bank_free[bank] = pb.tok("vector")

        def bcast_row(rowap, dst, n, bank=2):
            pb.wait("tensor", bank_free[bank])
            mm(pp[bank][:, 0:n], onesrow[0:1, 0:128], rowap, True, True, inc=1)
            tt = pb.tok("tensor")
            pb.wait("vector", tt)
            V(lambda e, d=dst, s=pp[bank][:, 0:n]: e.tensor_copy(d, s), 1)
            bank_free[bank] = pb.tok("vector")

        G(lambda e: e.memset(ident, 0.0), 1)
        pb.xwait("gpsimd", pb.tok("gpsimd"))
        G(lambda e: e.affine_select(out=ident, in_=ident, compare_op=ALU.not_equal,
                                    fill=1.0, base=0, pattern=[[-1, 128]],
                                    channel_multiplier=1))
        G(lambda e: e.memset(onesrow[0:1, :], 1.0))
        G(lambda e: e.memset(onescol, 1.0))
        G(lambda e: e.memset(rows, 0.0))
        gr = row(16); lgr = row(16)
        for h in range(16):
            gam = 1.0 - 2.0 ** (-5.0 - h)
            G(lambda e, d=gr[0:1, h:h + 1], v=gam: e.memset(d, v),
              (1 if h == 15 else 0))
        pb.wait("scalar", pb.tok("gpsimd"))
        SC(lambda e, d=lgr, s=gr: e.activation(d, s, AF.Ln), 1)
        pb.wait("tensor", pb.tok("scalar"))
        pb.wait("tensor", pb.tok("gpsimd"))
        bcast_row(lgr, lngb, 16, bank=2)
        for j in range(8):
            V(lambda e, d=lngp[0:64, j:j + 1], s=lngb[0:64, 2 * j:2 * j + 1]:
              e.tensor_copy(d, s))
            V(lambda e, d=lngp[64:128, j:j + 1], s=lngb[64:128, 2 * j + 1:2 * j + 2]:
              e.tensor_copy(d, s), (1 if j == 7 else 0))
        pb.wait("scalar", pb.tok("vector"))
        SC(lambda e, d=g128, s=lngp: e.activation(d, s, AF.Exp, scale=128.0), 1)
        SC(lambda e, d=g128h, s=lngb: e.activation(d, s, AF.Exp, scale=128.0), 1)
        c127 = scr2[:, 8:9]
        G(lambda e, d=c127: e.iota(d, pattern=[[0, 1]], base=127, channel_multiplier=-1,
                                   allow_small_or_imprecise_dtypes=True), 1)
        pb.wait("vector", pb.tok("gpsimd"))
        V(lambda e, d=gvcol, a=c127.broadcast_to([128, 16]), b=lngb:
          e.tensor_tensor(d, a, b, op=ALU.mult), 1)
        pb.wait("scalar", pb.tok("vector"))
        SC(lambda e, d=gvcol, s=gvcol: e.activation(d, s, AF.Exp), 1)
        row_reset()
        pb.barrier()

        DMA("sync", xsb.rearrange("p (t d) -> p t d", d=1024),
            dx.rearrange("(t p) d -> p t d", p=128))
        DMA("sync", m1c, dm1)
        DMA("sync", m2c, dm2)
        pb.barrier()

        def phase_attn_a():
            for t in range(NT):
                emit_stats(t, mcol[:, t:t + 1], rcol[:, t:t + 1])
                emit_xnt_bf(t, mcol[:, t:t + 1], rcol[:, t:t + 1])
            pb.barrier()
            tap(0, nT(0), 1024)
            tap(3072, mcol, 16)
            tap(3088, rcol, 16)
            for g in range(2):
                row_reset()
                ab_reset()
                qst = ab(8192).rearrange("p (t n) -> p t n", n=512)
                aw_reset()
                wstage = aw(2048).rearrange("p (j n) -> p j n", n=256)
                wbfa = aw(2048).bitcast(BF16).rearrange("p (v j n) -> p v j n", v=2, n=256)
                wbfb = aw(2048).bitcast(BF16).rearrange("p (v j n) -> p v j n", v=2, n=256)
                slices = [(dWq, dbq, 0), (dWq, dbq, 1), (dWk, dbk, 0), (dWk, dbk, 1)]
                rowsum_slc = {0: (4, 0), 1: (4, 256), 2: (5, 0), 3: (5, 256)}
                slice_cons = {}
                spl_prev = {}
                klist = []
                for si_, (wd, bd, half) in enumerate(slices):
                    wsl_bf = wbfa if si_ % 2 == 0 else wbfb
                    c0 = g * 512 + half * 256
                    pb.wait("sync", spl_prev.get(si_ - 1))
                    ld = DMA("sync", wstage,
                             wd.rearrange("(j p) o -> p j o", p=128)[:, :, c0:c0 + 256])
                    br = scr2[0:1, 768:1024]
                    pb.wait("sync", slice_cons.get(si_ - 1))
                    ldb = DMA("sync", br, bd[:, c0:c0 + 256])
                    spl = split_w(wstage.rearrange("p j n -> p (j n)"),
                                  wsl_bf[:, 0].rearrange("p j n -> p (j n)"),
                                  wsl_bf[:, 1].rearrange("p j n -> p (j n)"),
                                  ld, cons=(slice_cons.get(si_ - 2),) if si_ >= 2 else ())
                    spl_prev[si_] = spl
                    pb.wait("tensor", spl)
                    pb.wait("tensor", ldb)
                    rb, rc = rowsum_slc[si_]
                    if si_ == 0:
                        pb.wait("tensor", bank_free[4])
                        pb.wait("tensor", bank_free[5])
                    last = None
                    for t in range(NT):
                        bk = 2 if t % 2 == 0 else 3
                        pb.wait("tensor", bank_free[bk])
                        emit_proj3(t, pp[bk][:, 0:256], wsl_bf[:, 0], wsl_bf[:, 1])
                        biasfold(pp[bk][:, 0:256], br, stop=True, inc=1)
                        mtok = pb.tok("tensor")
                        pb.wait("scalar", mtok)
                        if si_ < 2:
                            dst = qst[:, t, half * 256:half * 256 + 256]
                        else:
                            dst = scr[:, 0:256] if t % 2 == 0 else scr[:, 256:512]
                            if len(klist) >= 2:
                                for tk_ in klist[-2]:
                                    pb.wait("scalar", tk_)
                        SC(lambda e, d=dst, s=pp[bk][:, 0:256]:
                           e.activation(d, s, AF.Sigmoid), 1)
                        etok = pb.tok("scalar")
                        bank_free[bk] = etok
                        if si_ >= 2:
                            pb.wait("sync", etok)
                            kdma = DMA("sync",
                                       sp_k[:, 512 * t + (si_ - 2) * 256:512 * t + (si_ - 2) * 256 + 256],
                                       dst)
                        pb.wait("tensor", etok)
                        last = mm(pp[rb][0:1, rc:rc + 256], onescol, dst,
                                  t == 0, t == NT - 1, inc=(1 if t == NT - 1 else 0))
                        if si_ >= 2:
                            klist.append((pb.tok("tensor"), kdma))
                    slice_cons[si_] = pb.tok("tensor")
                rtok = pb.tok("tensor")
                pb.wait("vector", rtok)
                rqk = row(1024)
                V(lambda e, d=rqk[0:1, 0:512], s=pp[4][0:1, 0:512]: e.tensor_copy(d, s))
                V(lambda e, d=rqk[0:1, 512:1024], s=pp[5][0:1, 0:512]:
                  e.tensor_copy(d, s), 1)
                bank_free[4] = pb.tok("vector"); bank_free[5] = bank_free[4]
                pb.wait("gpsimd", bank_free[4])
                pb.xwait("gpsimd", ("gpsimd", pb.cnt["gpsimd"]))
                ob = pb.dma("gpsimd", lambda e, d=cc1i, s=rqk: e.dma_start(out=d, in_=s))
                pb.xwait("gpsimd", ob)
                cct = collective(cc1i, cc1o)
                pb.xwait("gpsimd", cct)
                rbk = pb.dma("gpsimd", lambda e, d=rqk, s=cc1o: e.dma_start(out=d, in_=s))
                pb.wait("vector", rbk)
                V(lambda e, d=rqk[0:1, 512:1024], s=rqk[0:1, 512:1024]:
                  e.tensor_scalar_add(d, s, EPSA))
                V(lambda e, d=rqk[0:1, 0:512], s=rqk[0:1, 0:512]:
                  e.tensor_scalar_add(d, s, EPSA), 1)
                er1 = scr2[0:1, 16:24]; er2 = scr2[0:1, 24:32]
                V(lambda e, d=er1, s=rqk[0:1, 512:1024].rearrange("o (h d) -> o h d", d=64):
                  e.reduce_sum(d, s, axis=AX))
                V(lambda e, d=er2, s=rqk[0:1, 0:512].rearrange("o (h d) -> o h d", d=64):
                  e.reduce_sum(d, s, axis=AX))
                V(lambda e, d=er1, s=er1: e.tensor_scalar_mul(d, s, EPSA))
                V(lambda e, d=er2, s=er2: e.tensor_scalar_mul(d, s, EPSA), 1)
                pb.wait("tensor", pb.tok("vector"))
                bcast_row(rqk[0:1, 512:1024], kapb, 512, bank=2)
                bcast_row(rqk[0:1, 0:512], kapb2, 512, bank=3)
                bcast_row(er1, eksi, 8, bank=2)
                bcast_row(er2, ekso, 8, bank=3)
                if g == 0:
                    tap(1024, qst[:, 0, :], 512)
                    tap(1536, kapb, 512)
                    tap(2048, kapb2, 512)

                pb.dmaq("vector"); pb.dmaq("scalar"); pb.dmaq("tensor")
                sf = sfA if g == 0 else sfB
                sfv = sf.rearrange("p (t h) -> p t h", h=8)
                aw_reset()
                krb = [aw(512), aw(512)]
                rdt = {}
                vcons = {}
                rdt[0] = DMA("sync", krb[0], sp_k[:, 0:512])
                rdt[1] = DMA("sync", krb[1], sp_k[:, 512:1024])
                pb.wait("tensor", bank_free[4]); pb.wait("tensor", bank_free[5])
                for t in range(NT):
                    pb.tick()
                    ktile = krb[t % 2]
                    pb.wait("vector", rdt[t])
                    qtile = qst[:, t, :]
                    tq = scr[:, 0:512]
                    V(lambda e, d=tq, a=qtile, b=kapb: e.tensor_mul(d, a, b))
                    th8 = scr2[:, 32:40]
                    V(lambda e, d=th8, s=tq.rearrange("p (h d) -> p h d", d=64):
                      e.reduce_sum(d, s, axis=AX))
                    V(lambda e, d=th8, a=th8, b=eksi: e.tensor_add(d, a, b))
                    V(lambda e, d=sfv[:, t, :], s=th8: e.reciprocal(d, s))
                    tk2 = scr[:, 512:1024]
                    V(lambda e, d=tk2, a=ktile, b=kapb2: e.tensor_mul(d, a, b))
                    th8b = scr2[:, 40:48]
                    V(lambda e, d=th8b, s=tk2.rearrange("p (h d) -> p h d", d=64):
                      e.reduce_sum(d, s, axis=AX))
                    so8 = scr2[:, 48:56]
                    V(lambda e, d=th8b, a=th8b, b=ekso: e.tensor_add(d, a, b))
                    V(lambda e, d=so8, s=th8b: e.reciprocal(d, s))
                    kso = scr[:, 512:1024]
                    ttb(kso, ktile, so8, 8, ALU.mult)
                    qsi = scr[:, 0:512]
                    ttb(qsi, qtile, sfv[:, t, :], 8, ALU.mult, 1)
                    vtk = pb.tok("vector")
                    vcons[t] = vtk
                    pb.wait("tensor", vtk)
                    mm(pp[4][0:1, 0:512], onescol, kso, t == 0, t == NT - 1)
                    mm(pp[5][0:1, 0:512], onescol, qsi, t == 0, t == NT - 1,
                       inc=(1 if t == NT - 1 else 0))
                    if t + 2 < NT:
                        pb.wait("sync", vcons[t])
                        rdt[t + 2] = DMA("sync", krb[t % 2],
                                         sp_k[:, 512 * (t + 2):512 * (t + 3)])
                rtok = pb.tok("tensor")
                pb.wait("vector", rtok)
                rab = rqk
                V(lambda e, d=rab[0:1, 0:512], s=pp[4][0:1, 0:512]: e.tensor_copy(d, s))
                V(lambda e, d=rab[0:1, 512:1024], s=pp[5][0:1, 0:512]:
                  e.tensor_copy(d, s), 1)
                bank_free[4] = pb.tok("vector"); bank_free[5] = bank_free[4]
                pb.wait("gpsimd", bank_free[4])
                pb.xwait("gpsimd", ("gpsimd", pb.cnt["gpsimd"]))
                ob = pb.dma("gpsimd", lambda e, d=cc1i, s=rab: e.dma_start(out=d, in_=s))
                pb.xwait("gpsimd", ob)
                cct = collective(cc1i, cc1o)
                pb.xwait("gpsimd", cct)
                rbk = pb.dma("gpsimd", lambda e, d=rab, s=cc1o: e.dma_start(out=d, in_=s))
                pb.wait("vector", rbk)
                V(lambda e, d=rab[0:1, 0:512], s=rab[0:1, 0:512]:
                  e.tensor_scalar_add(d, s, EPSA))
                V(lambda e, d=rab[0:1, 512:1024], s=rab[0:1, 512:1024]:
                  e.tensor_scalar_add(d, s, EPSA), 1)
                V(lambda e, d=er1, s=rab[0:1, 0:512].rearrange("o (h d) -> o h d", d=64):
                  e.reduce_sum(d, s, axis=AX))
                V(lambda e, d=er2, s=rab[0:1, 512:1024].rearrange("o (h d) -> o h d", d=64):
                  e.reduce_sum(d, s, axis=AX))
                V(lambda e, d=er1, s=er1: e.tensor_scalar_mul(d, s, EPSA))
                V(lambda e, d=er2, s=er2: e.tensor_scalar_mul(d, s, EPSA), 1)
                pb.wait("tensor", pb.tok("vector"))
                bcast_row(rab[0:1, 0:512], kapb, 512, bank=2)
                bcast_row(rab[0:1, 512:1024], kapb2, 512, bank=3)
                bcast_row(er1, eksi, 8, bank=2)
                bcast_row(er2, ekso, 8, bank=3)

                if g == 0:
                    tap(2560, sfv[:, 0, :], 8)
                    tap(2880, rab, 1024)
                pb.dmaq("vector"); pb.dmaq("scalar"); pb.dmaq("tensor")
                aw_reset()
                row_reset()
                pb.wait("sync", pb.tok("tensor"))
                wv0b = aw(2048).bitcast(BF16).rearrange("p (v j n) -> p v j n", v=2, n=256)
                wv1b = aw(2048).bitcast(BF16).rearrange("p (v j n) -> p v j n", v=2, n=256)
                krb = [aw(512), aw(512)]
                vstage = aw(1024).rearrange("p (j n) -> p j n", n=256)
                dwvr = dWv.rearrange("(j p) o -> p j o", p=128)
                spv = None
                for half in range(2):
                    wvb = wv0b if half == 0 else wv1b
                    cw = g * 512 + half * 256
                    for ch in range(2):
                        pb.wait("sync", spv)
                        ldc = DMA("sync", vstage,
                                  dwvr[:, 4 * ch:4 * ch + 4, cw:cw + 256])
                        spv = split_w(
                            vstage.rearrange("p j n -> p (j n)"),
                            wvb[:, 0, 4 * ch:4 * ch + 4, :].rearrange("p j n -> p (j n)"),
                            wvb[:, 1, 4 * ch:4 * ch + 4, :].rearrange("p j n -> p (j n)"),
                            ldc)
                brv = row(512)
                ldb = DMA("sync", brv, dbv[:, g * 512:g * 512 + 512])
                pb.wait("tensor", spv); pb.wait("tensor", ldb)
                evv = eAm.rearrange("p (t h) -> p t h", h=8)
                kvsb = kvsbm
                V(lambda e, d=kvsb: e.memset(d, 0.0), 1)
                rdt = {}
                rdt[0] = DMA("sync", krb[0], sp_k[:, 0:512])
                rdt[1] = DMA("sync", krb[1], sp_k[:, 512:1024])
                pb.wait("tensor", bank_free[6]); pb.wait("tensor", bank_free[7])
                kvcons = {}
                mtoks = {}
                etoks = {}

                def a3_A(t):
                    p = t % 2
                    ktile = krb[p]
                    blk_ = scr if p == 0 else scr2
                    sm_ = smA[:, 32 + 32 * p:32 + 32 * p + 24]
                    pb.wait("vector", rdt[t])
                    for tk_ in kvcons.get(t - 2, ()):
                        pb.wait("vector", tk_)
                    qtile = qst[:, t, :]
                    tq = blk_[:, 0:512]
                    V(lambda e, d=tq, a=qtile, b=kapb: e.tensor_mul(d, a, b))
                    th8 = sm_[:, 0:8]
                    V(lambda e, d=th8, s=tq.rearrange("p (h d) -> p h d", d=64):
                      e.reduce_sum(d, s, axis=AX))
                    V(lambda e, d=th8, a=th8, b=eksi: e.tensor_add(d, a, b), 1)
                    pb.wait("scalar", pb.tok("vector"))
                    sa8 = sm_[:, 8:16]
                    SC(lambda e, d=sa8, s=th8: e.activation(d, s, AF.Sigmoid), 1)
                    pb.wait("vector", pb.tok("scalar"))
                    V(lambda e, d=sfv[:, t, :], a=sfv[:, t, :], b=sa8: e.tensor_mul(d, a, b))
                    tk2 = blk_[:, 0:512]
                    V(lambda e, d=tk2, a=ktile, b=kapb2: e.tensor_mul(d, a, b))
                    th8b = sm_[:, 16:24]
                    V(lambda e, d=th8b, s=tk2.rearrange("p (h d) -> p h d", d=64):
                      e.reduce_sum(d, s, axis=AX))
                    V(lambda e, d=th8b, a=th8b, b=ekso: e.tensor_add(d, a, b))
                    V(lambda e, d=th8b, a=th8b: e.tensor_scalar(d, a, -1.0, 1.0,
                                                                op0=ALU.max, op1=ALU.min), 1)
                    pb.wait("scalar", pb.tok("vector"))
                    SC(lambda e, d=evv[:, t, :], s=th8b: e.activation(d, s, AF.Exp), 1)
                    etoks[t] = pb.tok("scalar")
                    hts = []
                    for half in range(2):
                        bk = (2 + half) if p == 0 else (4 + half)
                        wvb = wv0b if half == 0 else wv1b
                        pb.wait("tensor", bank_free[bk])
                        emit_proj3(t, pp[bk][:, 0:256], wvb[:, 0], wvb[:, 1])
                        biasfold(pp[bk][:, 0:256], brv[0:1, 256 * half:256 * half + 256],
                                 stop=True, inc=1)
                        hts.append(pb.tok("tensor"))
                    mtoks[t] = hts

                def a3_B(t):
                    p = t % 2
                    ktile = krb[p]
                    blk_ = scr if p == 0 else scr2
                    pb.wait("tensor", etoks[t])
                    mm(pp[7][0:1, 0:8], onescol, evv[:, t, :], t == 0, t == NT - 1)
                    vs = []
                    for half in range(2):
                        bk = (2 + half) if p == 0 else (4 + half)
                        pb.wait("vector", mtoks[t][half])
                        pb.wait("vector", etoks[t])
                        vsl = blk_[:, 512 + 256 * half:512 + 256 * half + 256]
                        ttb(vsl, pp[bk][:, 0:256], evv[:, t, 4 * half:4 * half + 4],
                            4, ALU.mult, 1)
                        vtok2 = pb.tok("vector")
                        bank_free[bk] = vtok2
                        vs.append((vsl, vtok2))
                    pb.wait("tensor", bank_free[6])
                    for half in range(2):
                        vsl, vtok2 = vs[half]
                        pb.wait("tensor", vtok2)
                        for hh in range(4):
                            h = half * 4 + hh
                            dst = pp[6][(h % 2) * 64:(h % 2) * 64 + 64,
                                        128 * half + (hh // 2) * 64:128 * half + (hh // 2) * 64 + 64]
                            mm(dst, ktile[:, 64 * h:64 * h + 64], vsl[:, 64 * hh:64 * hh + 64],
                               True, True, inc=(1 if hh == 3 else 0))
                    kvt = pb.tok("tensor")
                    pb.wait("vector", kvt)
                    V(lambda e, d=kvsb, a=pp[6][:, 0:256], b=kvsb: e.tensor_add(d, a, b), 1)
                    bank_free[6] = pb.tok("vector")
                    kvcons[t] = (kvt, bank_free[6])
                    if t + 2 < NT:
                        pb.wait("sync", kvt)
                        rdt[t + 2] = DMA("sync", krb[p], sp_k[:, 512 * (t + 2):512 * (t + 3)])

                a3_A(0)
                for t in range(NT):
                    if t + 1 < NT:
                        a3_A(t + 1)
                    a3_B(t)
                ktok = pb.tok("tensor")
                pb.wait("vector", ktok)
                sstage = scr2[:, 0:264]
                V(lambda e, d=sstage[:, 256:264]: e.memset(d, 0.0))
                V(lambda e, d=sstage[:, 0:256], s=kvsb: e.tensor_copy(d, s))
                V(lambda e, d=sstage[0:1, 256:264], s=pp[7][0:1, 0:8]:
                  e.tensor_copy(d, s), 1)
                bank_free[6] = pb.tok("vector"); bank_free[7] = bank_free[6]
                pb.wait("gpsimd", bank_free[6])
                pb.xwait("gpsimd", ("gpsimd", pb.cnt["gpsimd"]))
                ob = pb.dma("gpsimd", lambda e, d=cc3i, s=sstage: e.dma_start(out=d, in_=s))
                pb.xwait("gpsimd", ob)
                cct = collective(cc3i, cc3o)
                pb.xwait("gpsimd", cct)
                pb.wait("gpsimd", pb.tok("vector"))
                pb.wait("gpsimd", pb.tok("tensor"))
                kvg = kapb[:, 0:256]
                rb1 = pb.dma("gpsimd", lambda e, d=kvg, s=cc3o[:, 0:256]:
                             e.dma_start(out=d, in_=s))
                esr = scr2[0:1, 16:24]
                rb2 = pb.dma("gpsimd", lambda e, d=esr, s=cc3o[0:1, 256:264]:
                             e.dma_start(out=d, in_=s))
                pb.wait("vector", rb2)
                if g == 0:
                    tap(2848, esr, 8)
                V(lambda e, d=esr, s=esr: e.reciprocal(d, s))
                V(lambda e, d=esr, s=esr: e.tensor_scalar_mul(d, s, float(L)), 1)
                pb.wait("tensor", pb.tok("vector"))
                bcast_row(esr, scb, 8, bank=2)
                if g == 0:
                    tap(2568, evv[:, 0, :], 8)
                    tap(2576, kvg, 256)
                    tap(2832, scb, 8)
                    tap(2840, sfv[:, 0, :], 8)

                pb.dmaq("vector"); pb.dmaq("scalar"); pb.dmaq("tensor")
                aw_reset()
                row_reset()
                pb.wait("sync", pb.tok("tensor"))
                wo_sl = aw(4096).rearrange("p (j n) -> p j n", n=1024)
                ldo = DMA("sync", wo_sl,
                          dWo.rearrange("(j p) o -> p j o", p=128)[:, 4 * g:4 * g + 4, :])
                bor = row(1024) if g == 0 else None
                pb.wait("tensor", ldo)
                if g == 0:
                    ldb2 = DMA("sync", bor, dbo)
                    pb.wait("tensor", ldb2)
                pb.wait("vector", rb1)
                a4cons = {}
                a4ev = {}

                def a4_A(t):
                    p = t % 2
                    qe = scr[:, 0:512] if p == 0 else scr2[:, 0:512]
                    qeT = scr[:, 512:1024] if p == 0 else scr2[:, 512:1024]
                    trb = 0 if p == 0 else 6
                    otb = 1 if p == 0 else 7
                    f2 = smA[:, 8 * p:8 * p + 8]
                    for tk_ in a4cons.get(t - 2, ()):
                        pb.wait("vector", tk_)
                    V(lambda e, d=f2, a=sfv[:, t, :], b=scb: e.tensor_mul(d, a, b))
                    ttb(qe, qst[:, t, :], f2, 8, ALU.mult, 1)
                    vtok3 = pb.tok("vector")
                    pb.wait("tensor", vtok3)
                    pb.wait("tensor", bank_free[trb])
                    for j in range(4):
                        tr(pp[trb][:, 128 * j:128 * (j + 1)], qe[:, 128 * j:128 * (j + 1)],
                           inc=(1 if j == 3 else 0))
                    trt = pb.tok("tensor")
                    pb.wait("scalar", trt)
                    for tk_ in a4cons.get(t - 2, ()):
                        pb.wait("scalar", tk_)
                    for j in range(4):
                        SC(lambda e, d=qeT[:, 128 * j:128 * (j + 1)],
                           s=pp[trb][:, 128 * j:128 * (j + 1)]: e.copy(d, s),
                           inc=(1 if j == 3 else 0))
                    et = pb.tok("scalar")
                    bank_free[trb] = et
                    pb.wait("tensor", et)
                    pb.wait("tensor", bank_free[otb])
                    for h in range(8):
                        kv_h = kvg[(h % 2) * 64:(h % 2) * 64 + 64,
                                   (h // 2) * 64:(h // 2) * 64 + 64]
                        qh = qeT[(h % 2) * 64:(h % 2) * 64 + 64,
                                 128 * (h // 2):128 * (h // 2) + 128]
                        dst = pp[otb][(h % 2) * 64:(h % 2) * 64 + 64,
                                      128 * (h // 2):128 * (h // 2) + 128]
                        mm(dst, kv_h, qh, True, True, inc=(1 if h == 7 else 0))
                    a4ev[t] = pb.tok("tensor")

                def a4_B(t):
                    p = t % 2
                    qeT = scr[:, 512:1024] if p == 0 else scr2[:, 512:1024]
                    otb = 1 if p == 0 else 7
                    outT = qeT
                    pb.wait("vector", a4ev[t])
                    V(lambda e, d=outT, s=pp[otb][:, 0:512]: e.tensor_copy(d, s), 1)
                    ovt = pb.tok("vector")
                    bank_free[otb] = ovt
                    pb.wait("tensor", ovt)
                    bk2 = (2, 3) if p == 0 else (4, 5)
                    pb.wait("tensor", bank_free[bk2[0]])
                    pb.wait("tensor", bank_free[bk2[1]])
                    for half in range(2):
                        for j in range(4):
                            mm(pp[bk2[half]][:, 0:512], outT[:, 128 * j:128 * (j + 1)],
                               wo_sl[:, j, 512 * half:512 * half + 512],
                               j == 0, g == 1 and j == 3,
                               inc=(1 if (g == 1 and j == 3 and half == 1) else 0))
                        if g == 0:
                            biasfold(pp[bk2[half]][:, 0:512],
                                     bor[0:1, 512 * half:512 * half + 512],
                                     stop=True, inc=(1 if half == 1 else 0))
                    wt = pb.tok("tensor")
                    pb.wait("vector", wt)
                    for half in range(2):
                        V(lambda e, d=xt(t)[:, 512 * half:512 * half + 512],
                          a=pp[bk2[half]][:, 0:512],
                          b=xt(t)[:, 512 * half:512 * half + 512]:
                          e.tensor_add(d, a, b), (1 if half == 1 else 0))
                    xvt = pb.tok("vector")
                    bank_free[bk2[0]] = xvt
                    bank_free[bk2[1]] = xvt
                    a4cons[t] = (wt, xvt)

                a4_A(0)
                for t in range(NT):
                    if t + 1 < NT:
                        a4_A(t + 1)
                    a4_B(t)
                pb.barrier()

        def phase_ffn():
            row_reset()
            ab_reset()
            aw_reset()
            eb_sb = aw(1024)
            ldeb = DMA("sync", eb_sb[0:8, 0:1024], deb)
            gwsl = aw(64).rearrange("p (j n) -> p j n", n=8)
            ldgw = DMA("sync", gwsl, dgw.rearrange("(j p) o -> p j o", p=128))
            gbr = row(8)
            ldgb = DMA("sync", gbr, dgb)
            gwb = aw(64).bitcast(BF16).rearrange("p (v j n) -> p v j n", v=2, n=8)
            pb.wait("scalar", ldgw)
            SC(lambda e, d=gwb[:, 0], s=gwsl: e.copy(d, s), 1)
            pb.wait("vector", pb.tok("scalar"))
            V(lambda e, d=gwb[:, 1], a=gwsl, c=gwb[:, 0]:
              e.scalar_tensor_tensor(d, a, 1.0, c, op0=ALU.mult, op1=ALU.subtract), 1)
            gsp = pb.tok("vector")
            gatv = gat.rearrange("p (t h) -> p t h", h=8)
            for t in range(NT):
                pb.tick()
                emit_stats(t, mcol[:, t:t + 1], rcol[:, t:t + 1])
                stok = emit_xnt_bf(t, mcol[:, t:t + 1], rcol[:, t:t + 1])
                if t == 0:
                    pb.wait("tensor", ldeb)
                    pb.wait("tensor", ldgb)
                    pb.wait("tensor", gsp)
                pb.wait("tensor", stok)
                pb.wait("tensor", bank_free[2])
                for j in range(8):
                    mm(pp[2][:, 0:8], nTh(t)[:, 128 * j:128 * (j + 1)], gwb[:, 0, j, :],
                       j == 0, False)
                    mm(pp[2][:, 0:8], nTh(t)[:, 128 * j:128 * (j + 1)], gwb[:, 1, j, :],
                       False, False)
                    mm(pp[2][:, 0:8], nTl(t)[:, 128 * j:128 * (j + 1)], gwb[:, 0, j, :],
                       False, False)
                biasfold(pp[2][:, 0:8], gbr, stop=True, inc=1)
                gt = pb.tok("tensor")
                pb.wait("vector", gt)
                lg = scr2[:, 32:40]
                V(lambda e, d=lg, s=pp[2][:, 0:8]: e.tensor_copy(d, s), 1)
                bank_free[2] = pb.tok("vector")
                mx = scr2[:, 40:41]
                V(lambda e, d=mx, s=lg: e.reduce_max(d, s, axis=AX))
                V(lambda e, d=mx, s=mx: e.tensor_scalar_mul(d, s, -1.0), 1)
                pb.wait("scalar", pb.tok("vector"))
                eg = scr2[:, 41:49]
                SC(lambda e, d=eg, s=lg, b=mx: e.activation(d, s, AF.Exp, bias=b), 1)
                pb.wait("vector", pb.tok("scalar"))
                sm = scr2[:, 49:50]
                V(lambda e, d=sm, s=eg: e.reduce_sum(d, s, axis=AX))
                V(lambda e, d=sm, s=sm: e.reciprocal(d, s))
                gg = scr2[:, 50:58]
                V(lambda e, d=gg, a=eg, b=sm: e.tensor_scalar_mul(d, a, b))
                sel = scr2[:, 58:66]
                V(lambda e, d=sel, s=gg: e.tensor_single_scalar(d, s, 0.1, op=ALU.is_ge))
                V(lambda e, d=gg, a=gg, b=sel: e.tensor_mul(d, a, b))
                ws = scr2[:, 66:67]
                V(lambda e, d=ws, s=gg: e.reduce_sum(d, s, axis=AX))
                V(lambda e, d=ws, s=ws: e.reciprocal(d, s))
                V(lambda e, d=gatv[:, t, :], a=gg, b=ws: e.tensor_scalar_mul(d, a, b), 1)
                gvt = pb.tok("vector")
                pb.wait("tensor", gvt)
                pb.wait("tensor", bank_free[3])
                tr(pp[3][0:8, 0:128], gatv[:, t, :], inc=1)
                wtt = pb.tok("tensor")
                pb.wait("scalar", wtt)
                wT = scr2[:, 128:256]
                SC(lambda e, d=wT[0:8, 0:128], s=pp[3][0:8, 0:128]: e.copy(d, s), 1)
                stt = pb.tok("scalar")
                bank_free[3] = stt
                pb.wait("tensor", stt)
                bk2 = (4, 5) if t % 2 == 0 else (6, 7)
                pb.wait("tensor", bank_free[bk2[0]]); pb.wait("tensor", bank_free[bk2[1]])
                for half in range(2):
                    mm(pp[bk2[half]][:, 0:512], wT[0:8, 0:128],
                       eb_sb[0:8, 512 * half:512 * half + 512], True, True,
                       inc=(1 if half == 1 else 0))
                wbt = pb.tok("tensor")
                pb.wait("vector", wbt)
                for half in range(2):
                    V(lambda e, d=xt(t)[:, 512 * half:512 * half + 512],
                      a=pp[bk2[half]][:, 0:512], b=xt(t)[:, 512 * half:512 * half + 512]:
                      e.tensor_add(d, a, b), (1 if half == 1 else 0))
                bank_free[bk2[0]] = pb.tok("vector")
                bank_free[bk2[1]] = bank_free[bk2[0]]
            pb.barrier()
            stage_w = aB[:, 0:2048].rearrange("p (j n) -> p j n", n=256)
            wqb = [aw(2048).bitcast(BF16).rearrange("p (v j n) -> p v j n", v=2, n=256)
                   for _ in range(2)]
            order = [(e_, q_) for e_ in range(8) for q_ in range(4)]
            loads = {}
            splits = {}
            mmdone = {}
            dre = dew.rearrange("e (j p) o -> e p j o", p=128)

            def f1_load(idx):
                e_, q_ = order[idx]
                pb.wait("sync", splits.get(idx - 1))
                loads[idx] = DMA("sync", stage_w, dre[e_][:, :, 256 * q_:256 * q_ + 256])

            def f1_split(idx):
                wb = wqb[idx % 2]
                pb.wait("scalar", loads[idx])
                pb.wait("scalar", mmdone.get(idx - 2))
                SC(lambda e, d=wb[:, 0].rearrange("p j n -> p (j n)"),
                   s=stage_w.rearrange("p j n -> p (j n)"): e.copy(d, s), 1)
                hs = pb.tok("scalar")
                pb.wait("vector", hs)
                pb.wait("vector", mmdone.get(idx - 2))
                V(lambda e, d=wb[:, 1].rearrange("p j n -> p (j n)"),
                  a=stage_w.rearrange("p j n -> p (j n)"),
                  c=wb[:, 0].rearrange("p j n -> p (j n)"):
                  e.scalar_tensor_tensor(d, a, 1.0, c, op0=ALU.mult, op1=ALU.subtract), 1)
                splits[idx] = pb.tok("vector")

            f1_load(0)
            f1_split(0)
            f1_load(1)
            for idx, (e_, q_) in enumerate(order):
                wb = wqb[idx % 2]
                if idx + 1 < len(order):
                    f1_split(idx + 1)
                if idx + 2 < len(order):
                    f1_load(idx + 2)
                pb.wait("tensor", splits[idx])
                for t in range(NT):
                    bk = 2 if (idx * NT + t) % 2 == 0 else 3
                    pb.wait("tensor", bank_free[bk])
                    emit_proj3(t, pp[bk][:, 0:256], wb[:, 0], wb[:, 1], inc=1)
                    pt = pb.tok("tensor")
                    pb.wait("vector", pt)
                    c0 = 256 * q_
                    V(lambda e, d=xt(t)[:, c0:c0 + 256], a=pp[bk][:, 0:256],
                      s=gatv[:, t, e_:e_ + 1], b=xt(t)[:, c0:c0 + 256]:
                      e.scalar_tensor_tensor(d, a, s, b, op0=ALU.mult, op1=ALU.add), 1)
                    bank_free[bk] = pb.tok("vector")
                mmdone[idx] = pb.tok("tensor")
            pb.barrier()

        def phase_attn_b():
            row_reset()
            DMA("sync", sp_x2, xsb)
            for t in range(NT):
                pb.tick()
                var = emit_stats(t, mcol[:, t:t + 1], rcol[:, t:t + 1])
                tv2 = scr2[:, 16:17]
                V(lambda e, d=tv2, s=var: e.tensor_scalar_add(d, s, 1e-5))
                V(lambda e, d=tv2, s=tv2: e.reciprocal(d, s))
                V(lambda e, d=tv2, a=var, b=tv2: e.tensor_mul(d, a, b))
                r1 = scr2[:, 17:18]
                rsqrt_newton(r1, tv2, 1e-5, scr2[:, 18:19], scr2[:, 19:20])
                rt_ = scr2[:, 20:21]
                V(lambda e, d=rt_, a=rcol[:, t:t + 1], b=r1: e.tensor_mul(d, a, b), 1)
                emit_xnt_bf(t, mcol[:, t:t + 1], rt_)
            pb.barrier()
            l1r = row(1024)
            ld1 = DMA("sync", l1r, dln1w)
            pb.wait("tensor", ld1)
            for j in range(8):
                row2col(l1r[0:1, 128 * j:128 * (j + 1)], l1wc[:, j:j + 1], 128, bank=2)
            pb.wait("sync", pb.tok("tensor"))
            ld2 = DMA("sync", l1r, dln1b)
            pb.wait("tensor", ld2)
            for j in range(8):
                row2col(l1r[0:1, 128 * j:128 * (j + 1)], l1bc[:, j:j + 1], 128, bank=2)
            rrz = pbsc[:, 24:40]
            mrz = pbsc[:, 40:56]
            V(lambda e, d=rrz, s=rcol: e.tensor_scalar_add(d, s, 1.0))
            V(lambda e, d=mrz, a=mcol, b=rcol: e.tensor_mul(d, a, b), 1)
            pb.barrier()

            for g in range(4):
                ab_reset()
                spfx = ab(4096)
                rbk = [ab(512), ab(512)]
                rbv = [ab(256), ab(256)]
                S_run = scr[0:64, 512:768]
                gmat = scr2[0:64, 0:256]
                c0 = 256 * g
                dmv = dmask.rearrange("p (h n) -> p h n", n=128)
                for i in range(4):
                    h = 4 * g + i
                    G(lambda e, d=dmv[:, i, :]: e.iota(
                        d, pattern=[[1, 128]], base=0, channel_multiplier=-1,
                        allow_small_or_imprecise_dtypes=True), 1)
                    pb.wait("scalar", pb.tok("gpsimd"))
                    SC(lambda e, d=dmv[:, i, :], s=dmv[:, i, :], sc=lngb[:, h:h + 1]:
                       e.activation(d, s, AF.Exp, scale=sc), 1)
                    pb.wait("gpsimd", pb.tok("scalar"))
                    G(lambda e, d=dmv[:, i, :]: e.affine_select(
                        out=d, in_=d, compare_op=ALU.is_ge, fill=0.0, base=0,
                        pattern=[[1, 128]], channel_multiplier=-1), 1)
                iof = scr2[:, 0:128]
                G(lambda e, d=iof: e.iota(d, pattern=[[1, 128]], base=1,
                                          channel_multiplier=0,
                                          allow_small_or_imprecise_dtypes=True), 1)
                pb.wait("scalar", pb.tok("gpsimd"))
                for i in range(4):
                    h = 4 * g + i
                    SC(lambda e, d=gqnew[:, 128 * i:128 * (i + 1)], s=iof[0:64, :],
                       sc=lngb[0:64, h:h + 1]:
                       e.activation(d, s, AF.Exp, scale=sc), 1)
                pb.wait("vector", pb.tok("scalar"))
                for i in range(4):
                    hc = 4 * g + i
                    V(lambda e, d=gmat[:, 64 * i:64 * (i + 1)],
                      s=g128h[0:64, hc:hc + 1].broadcast_to([64, 64]):
                      e.tensor_copy(d, s), (1 if i == 3 else 0))
                pb.barrier()

                aw_reset()
                pb.wait("sync", pb.tok("tensor"))
                wk_b = aw(2048).bitcast(BF16).rearrange("p (v j n) -> p v j n", v=2, n=256)
                wv_b = aw(2048).bitcast(BF16).rearrange("p (v j n) -> p v j n", v=2, n=256)
                bstage = aB[:, 5632:7680].rearrange("p (j n) -> p j n", n=256)
                row_reset()
                ckr = row(256)
                cvr = row(256)
                bspl = None
                for wi_, (wd_, wb_, crow_, cc_) in enumerate(
                        [(dbwk, wk_b, ckr, slice(0, 256)), (dbwv, wv_b, cvr, slice(256, 512))]):
                    pb.wait("sync", bspl)
                    ldw_ = DMA("sync", bstage,
                               wd_.rearrange("(j p) o -> p j o", p=128)[:, :, c0:c0 + 256])
                    pb.wait("tensor", ldw_)
                    if wi_ == 0:
                        pb.wait("tensor", bank_free[4])
                    for j in range(8):
                        mm(pp[4][0:1, cc_], l1bc[:, j:j + 1], bstage[:, j, :],
                           wi_ == 0 and j == 0, wi_ == 1 and j == 7,
                           inc=(1 if j == 7 else 0))
                    ct_ = pb.tok("tensor")
                    pb.wait("vector", ct_)
                    V(lambda e, d=crow_, s=pp[4][0:1, cc_]: e.tensor_copy(d, s), 1)
                    if wi_ == 1:
                        bank_free[4] = pb.tok("vector")
                    for j in range(8):
                        V(lambda e, d=bstage[:, j, :], a=bstage[:, j, :], s=l1wc[:, j:j + 1]:
                          e.tensor_scalar_mul(d, a, s), (1 if j == 7 else 0))
                    wst_ = pb.tok("vector")
                    bspl = split_w(bstage.rearrange("p j n -> p (j n)"),
                                   wb_[:, 0].rearrange("p j n -> p (j n)"),
                                   wb_[:, 1].rearrange("p j n -> p (j n)"), wst_)
                pb.wait("tensor", bspl)
                V(lambda e, d=S_run: e.memset(d, 0.0), 1)
                spdma = {}
                kvcons = {}
                evtoks = {}
                prtoks = {}
                pbuf = [aB[:, 5632 + 1280 * p:5632 + 1280 * (p + 1)] for p in range(2)]

                def p1a_A(t):
                    p = t % 2
                    bk = 2 if p == 0 else 3
                    pb.wait("tensor", bank_free[bk])
                    emit_proj3(t, pp[bk][:, 0:256], wk_b[:, 0], wk_b[:, 1])
                    biasfold(pp[bk][:, 0:256], ckr, stop=True, inc=0)
                    emit_proj3(t, pp[bk][:, 256:512], wv_b[:, 0], wv_b[:, 1], start=True)
                    biasfold(pp[bk][:, 256:512], cvr, stop=True, inc=1)
                    pt = pb.tok("tensor")
                    prtoks[t] = pt
                    pb.wait("scalar", pt)
                    for tk_ in kvcons.get(t - 2, ()):
                        pb.wait("scalar", tk_)
                    ktile = pbuf[p][:, 0:256]
                    vtile = pbuf[p][:, 256:512]
                    SC(lambda e, d=ktile, s=pp[bk][:, 0:256]: e.copy(d, s), 1)
                    SC(lambda e, d=vtile, s=pp[bk][:, 256:512]: e.copy(d, s), 1)
                    ek = pb.tok("scalar")
                    bank_free[bk] = ek
                    evtoks[t] = ek

                def p1a_B(t):
                    p = t % 2
                    ktile = pbuf[p][:, 0:256]
                    vtile = pbuf[p][:, 256:512]
                    vd = pbuf[p][:, 512:768]
                    kT = pbuf[p][0:64, 768:1280]
                    trb = 4 if p == 0 else 6
                    cb = 5 if p == 0 else 7
                    pb.wait("tensor", evtoks[t])
                    pb.wait("tensor", bank_free[trb])
                    for i in range(4):
                        tr(pp[trb][0:64, 128 * i:128 * (i + 1)],
                           ktile[:, 64 * i:64 * i + 64], inc=(1 if i == 3 else 0))
                    ttk = pb.tok("tensor")
                    pb.wait("scalar", ttk)
                    for tk_ in spdma.get(t - 2, ()):
                        pb.wait("scalar", tk_)
                    SC(lambda e, d=kT, s=pp[trb][0:64, 0:512]: e.copy(d, s), 1)
                    ktt = pb.tok("scalar")
                    bank_free[trb] = ktt
                    pb.wait("sync", ktt)
                    spk_ = DMA("sync",
                               sp_k[p * 64:p * 64 + 64, 512 * (t // 2):512 * (t // 2) + 512],
                               kT)
                    spv_ = DMA("sync", sp_b[:, 256 * t:256 * t + 256], vtile)
                    spdma[t] = (spk_, spv_)
                    pb.wait("vector", evtoks[t])
                    for tk_ in kvcons.get(t - 2, ()):
                        pb.wait("vector", tk_)
                    V(lambda e, d=spfx[0:64, 256 * t:256 * t + 256], s=S_run:
                      e.tensor_copy(d, s))
                    ttb(vd, vtile, gvcol[:, 4 * g:4 * g + 4], 4, ALU.mult, 1)
                    cvt = pb.tok("vector")
                    pb.wait("tensor", cvt)
                    pb.wait("tensor", bank_free[cb])
                    for i in range(4):
                        mm(pp[cb][0:64, 64 * i:64 * i + 64],
                           ktile[:, 64 * i:64 * i + 64], vd[:, 64 * i:64 * i + 64],
                           True, True, inc=(1 if i == 3 else 0))
                    ctk = pb.tok("tensor")
                    pb.wait("vector", ctk)
                    V(lambda e, d=S_run, a=S_run, b=gmat:
                      e.tensor_tensor(d, a, b, op=ALU.mult))
                    V(lambda e, d=S_run, a=pp[cb][0:64, 0:256], b=S_run:
                      e.tensor_add(d, a, b), 1)
                    svt = pb.tok("vector")
                    bank_free[cb] = svt
                    kvcons[t] = (ctk, svt, spk_, spv_)

                p1a_A(0)
                for t in range(NT):
                    if t + 1 < NT:
                        p1a_A(t + 1)
                    p1a_B(t)
                pb.dmaq("vector")
                ccsvi = ccsi.rearrange("(a b) n -> a (b n)", b=2)
                ccsvo = ccso.rearrange("(a b) n -> a (b n)", b=2)
                S_msk = scr[0:64, 0:256]
                V(lambda e, d=S_msk, a=S_run, s=m1c[0:64, :]:
                  e.tensor_scalar_mul(d, a, s), 1)
                pb.wait("gpsimd", pb.tok("vector"))
                pb.xwait("gpsimd", ("gpsimd", pb.cnt["gpsimd"]))
                ob = pb.dma("gpsimd", lambda e, d=ccsvi, s=S_msk: e.dma_start(out=d, in_=s))
                pb.xwait("gpsimd", ob)
                cct = collective(ccsi, ccso)
                pb.xwait("gpsimd", cct)
                S_init = scr[0:64, 256:512]
                rbt = pb.dma("gpsimd", lambda e, d=S_init, s=ccsvo: e.dma_start(out=d, in_=s))
                pb.wait("vector", rbt)
                V(lambda e, d=S_init, a=S_init, s=m2c[0:64, :]:
                  e.tensor_scalar_mul(d, a, s), 1)
                gm4 = pbsc[0:64, 0:4]
                g4 = g128h[0:64, 4 * g:4 * g + 4]
                V(lambda e, d=gm4: e.memset(d, 1.0), 1)
                for t in range(NT):
                    if t > 0:
                        V(lambda e, d=gm4, a=gm4, b=g4: e.tensor_mul(d, a, b))
                    for i in range(4):
                        V(lambda e, d=spfx[0:64, 256 * t + 64 * i:256 * t + 64 * i + 64],
                          a=S_init[:, 64 * i:64 * i + 64], s=gm4[:, i:i + 1],
                          b=spfx[0:64, 256 * t + 64 * i:256 * t + 64 * i + 64]:
                          e.scalar_tensor_tensor(d, a, s, b, op0=ALU.mult, op1=ALU.add),
                          (1 if i == 3 else 0))
                pb.barrier()

                aw_reset()
                pb.wait("sync", pb.tok("tensor"))
                wq_b = aw(2048).bitcast(BF16).rearrange("p (v j n) -> p v j n", v=2, n=256)
                wg_b = aw(2048).bitcast(BF16).rearrange("p (v j n) -> p v j n", v=2, n=256)
                wo_sl = aw(2048).rearrange("p (j n) -> p j n", n=1024)
                ldo = DMA("sync", wo_sl,
                          dbwo.rearrange("(j p) o -> p j o", p=128)[:, 2 * g:2 * g + 2, :])
                cqr = row(256); cgr = row(256)
                bstage = aB[:, 5632:7680].rearrange("p (j n) -> p j n", n=256)
                bspl = None
                for wi_, (wd_, wb_, crow_, cc_) in enumerate(
                        [(dbwq, wq_b, cqr, slice(0, 256)), (dbwg, wg_b, cgr, slice(256, 512))]):
                    pb.wait("sync", bspl)
                    ldw_ = DMA("sync", bstage,
                               wd_.rearrange("(j p) o -> p j o", p=128)[:, :, c0:c0 + 256])
                    pb.wait("tensor", ldw_)
                    if wi_ == 0:
                        pb.wait("tensor", bank_free[4])
                    for j in range(8):
                        mm(pp[4][0:1, cc_], l1bc[:, j:j + 1], bstage[:, j, :],
                           wi_ == 0 and j == 0, wi_ == 1 and j == 7,
                           inc=(1 if j == 7 else 0))
                    ct_ = pb.tok("tensor")
                    pb.wait("vector", ct_)
                    V(lambda e, d=crow_, s=pp[4][0:1, cc_]: e.tensor_copy(d, s), 1)
                    if wi_ == 1:
                        bank_free[4] = pb.tok("vector")
                    for j in range(8):
                        V(lambda e, d=bstage[:, j, :], a=bstage[:, j, :], s=l1wc[:, j:j + 1]:
                          e.tensor_scalar_mul(d, a, s), (1 if j == 7 else 0))
                    wst_ = pb.tok("vector")
                    bspl = split_w(bstage.rearrange("p j n -> p (j n)"),
                                   wb_[:, 0].rearrange("p j n -> p (j n)"),
                                   wb_[:, 1].rearrange("p j n -> p (j n)"), wst_)
                pb.wait("tensor", bspl)
                pb.wait("tensor", ldo)
                rdk = {0: (DMA("sync", rbk[0][0:64, :], sp_k[0:64, 0:512]),
                           DMA("sync", rbv[0], sp_b[:, 0:256])),
                       1: (DMA("sync", rbk[1][0:64, :], sp_k[64:128, 0:512]),
                           DMA("sync", rbv[1], sp_b[:, 256:512]))}
                rcons = {}
                bcons = {}
                evtoks = {}
                sqtoks = {}
                qbuf = [aB[:, 5632 + 512 * p:5632 + 512 * (p + 1)] for p in range(2)]

                def p1b_A(t):
                    p = t % 2
                    bk = 2 if p == 0 else 4
                    pb.wait("tensor", bank_free[bk])
                    emit_proj3(t, pp[bk][:, 0:256], wq_b[:, 0], wq_b[:, 1])
                    biasfold(pp[bk][:, 0:256], cqr, stop=True, inc=0)
                    emit_proj3(t, pp[bk][:, 256:512], wg_b[:, 0], wg_b[:, 1], start=True)
                    biasfold(pp[bk][:, 256:512], cgr, stop=True, inc=1)
                    pt = pb.tok("tensor")
                    pb.wait("scalar", pt)
                    if t - 2 in gvtoks:
                        pb.wait("scalar", gvtoks[t - 2])
                    qtile = qbuf[p][:, 0:256]
                    sgt = qbuf[p][:, 256:512]
                    SC(lambda e, d=qtile, s=pp[bk][:, 0:256]:
                       e.activation(d, s, AF.Copy, scale=float(HD) ** -0.5))
                    SC(lambda e, d=sgt, s=pp[bk][:, 256:512]:
                       e.activation(d, s, AF.Silu), 1)
                    qst_ = pb.tok("scalar")
                    bank_free[bk] = qst_
                    evtoks[t] = qst_

                ovb = [scr[:, 512:768], kvsbm]
                gvtoks = {}

                def p1b_B1a(t):
                    p = t % 2
                    qtile = qbuf[p][:, 0:256]
                    pb.wait("tensor", evtoks[t])
                    pb.wait("tensor", bank_free[3])
                    for i in range(4):
                        tr(pp[3][0:64, 128 * i:128 * (i + 1)],
                           qtile[:, 64 * i:64 * i + 64], inc=(1 if i == 3 else 0))
                    qtt = pb.tok("tensor")
                    pb.wait("vector", qtt)
                    qT = scr[0:64, 0:512]
                    qTg = scr2[0:64, 0:512]
                    V(lambda e, d=qT, s=pp[3][0:64, 0:512]: e.tensor_copy(d, s))
                    V(lambda e, d=qTg, a=pp[3][0:64, 0:512], b=gqnew:
                      e.tensor_tensor(d, a, b, op=ALU.mult), 1)
                    qvt = pb.tok("vector")
                    bank_free[3] = qvt
                    pb.wait("tensor", bank_free[5])
                    pb.wait("tensor", qvt)
                    pb.wait("tensor", rdk[t][0])
                    pb.wait("tensor", rdk[t][1])
                    kbuf = rbk[p]
                    for i in range(4):
                        mm(pp[5][:, 128 * i:128 * (i + 1)],
                           kbuf[0:64, 128 * i:128 * (i + 1)],
                           qT[:, 128 * i:128 * (i + 1)], True, True,
                           inc=(1 if i == 3 else 0))
                    sct = pb.tok("tensor")
                    pb.wait("vector", sct)
                    for tk_ in sqtoks.get(t - 1, ()):
                        pb.wait("vector", tk_)
                    sD = scr2[:, 512:1024]
                    V(lambda e, d=sD, a=pp[5][:, 0:512], b=dmask:
                      e.tensor_tensor(d, a, b, op=ALU.mult), 1)
                    sdt = pb.tok("vector")
                    bank_free[5] = sdt

                def p1b_B1b(t):
                    p = t % 2
                    sgt = qbuf[p][:, 256:512]
                    qTg = scr2[0:64, 0:512]
                    sD = scr2[:, 512:1024]
                    vbuf = rbv[p]
                    pb.wait("tensor", bank_free[6])
                    pb.wait("tensor", bank_free[5])
                    for i in range(4):
                        mm(pp[6][:, 64 * i:64 * i + 64], sD[:, 128 * i:128 * (i + 1)],
                           vbuf[:, 64 * i:64 * i + 64], True, False)
                        mm(pp[6][:, 64 * i:64 * i + 64],
                           qTg[:, 128 * i:128 * (i + 1)],
                           spfx[0:64, 256 * t + 64 * i:256 * t + 64 * i + 64],
                           False, True, inc=(1 if i == 3 else 0))
                    ot = pb.tok("tensor")
                    rcons[t] = ot
                    pb.wait("vector", ot)
                    ov = ovb[p]
                    V(lambda e, d=ov, s=pp[6][:, 0:256]: e.tensor_copy(d, s), 1)
                    bank_free[6] = pb.tok("vector")
                    mu = pbsc[:, 0:4]
                    V(lambda e, d=mu, s=ov.rearrange("p (h d) -> p h d", d=64):
                      e.reduce_sum(d, s, axis=AX))
                    V(lambda e, d=mu, s=mu: e.tensor_scalar_mul(d, s, 1.0 / 64.0), 1)
                    pb.wait("scalar", pb.tok("vector"))
                    sq = scr2[:, 512:768]
                    SC(lambda e, d=sq, s=ov: e.activation(d, s, AF.Square), 1)
                    sqtoks[t] = (pb.tok("scalar"),)
                    pb.wait("vector", pb.tok("scalar"))
                    sqm = pbsc[:, 4:8]
                    V(lambda e, d=sqm, s=sq.rearrange("p (h d) -> p h d", d=64):
                      e.reduce_sum(d, s, axis=AX))
                    V(lambda e, d=sqm, s=sqm: e.tensor_scalar_mul(d, s, 1.0 / 64.0))
                    mu2 = pbsc[:, 8:12]
                    V(lambda e, d=mu2, a=mu, b=mu: e.tensor_mul(d, a, b))
                    V(lambda e, d=sqm, a=sqm, b=mu2: e.tensor_sub(d, a, b), 1)
                    rs = pbsc[:, 12:16]
                    rsqrt_newton(rs, sqm, 1e-6, pbsc[:, 16:20], pbsc[:, 20:24])
                    gated = ov
                    ttb(gated, ov, mu, 4, ALU.subtract)
                    ttb(gated, gated, rs, 4, ALU.mult)
                    V(lambda e, d=gated, a=gated, b=sgt: e.tensor_mul(d, a, b), 1)
                    gvtoks[t] = pb.tok("vector")
                    if t + 2 < NT:
                        pb.wait("sync", rcons[t])
                        c2 = 512 * ((t + 2) // 2)
                        rdk[t + 2] = (DMA("sync", rbk[p][0:64, :],
                                          sp_k[p * 64:p * 64 + 64, c2:c2 + 512]),
                                      DMA("sync", rbv[p],
                                          sp_b[:, 256 * (t + 2):256 * (t + 2) + 256]))

                def p1b_B2(t):
                    p = t % 2
                    gated = ovb[p]
                    pb.wait("tensor", gvtoks[t])
                    pb.wait("tensor", bank_free[3])
                    for c_ in range(2):
                        tr(pp[3][:, 128 * c_:128 * (c_ + 1)],
                           gated[:, 128 * c_:128 * (c_ + 1)], inc=(1 if c_ == 1 else 0))
                    gtt = pb.tok("tensor")
                    pb.wait("scalar", gtt)
                    gT = scr[:, 768:1024]
                    for c_ in range(2):
                        SC(lambda e, d=gT[:, 128 * c_:128 * (c_ + 1)],
                           s=pp[3][:, 128 * c_:128 * (c_ + 1)]: e.copy(d, s),
                           (1 if c_ == 1 else 0))
                    gst2 = pb.tok("scalar")
                    bank_free[3] = gst2
                    pb.wait("tensor", gst2)
                    bk2 = (7, 0)
                    pb.wait("tensor", bank_free[bk2[0]]); pb.wait("tensor", bank_free[bk2[1]])
                    for half in range(2):
                        for c_ in range(2):
                            mm(pp[bk2[half]][:, 0:512], gT[:, 128 * c_:128 * (c_ + 1)],
                               wo_sl[:, c_, 512 * half:512 * half + 512], c_ == 0, c_ == 1,
                               inc=(1 if (half == 1 and c_ == 1) else 0))
                    wt = pb.tok("tensor")
                    pb.wait("vector", wt)
                    if g == 0:
                        for half in range(2):
                            V(lambda e, d=xt(t)[:, 512 * half:512 * half + 512],
                              s=xt(t)[:, 512 * half:512 * half + 512],
                              m=rrz[:, t:t + 1], r=mrz[:, t:t + 1]:
                              e.tensor_scalar(d, s, m, r, op0=ALU.mult, op1=ALU.subtract))
                            V(lambda e, d=xt(t)[:, 512 * half:512 * half + 512],
                              a=pp[bk2[half]][:, 0:512],
                              b=xt(t)[:, 512 * half:512 * half + 512]:
                              e.tensor_add(d, a, b), (1 if half == 1 else 0))
                    else:
                        for half in range(2):
                            V(lambda e, d=xt(t)[:, 512 * half:512 * half + 512],
                              a=pp[bk2[half]][:, 0:512],
                              b=xt(t)[:, 512 * half:512 * half + 512]:
                              e.tensor_add(d, a, b), (1 if half == 1 else 0))
                    xvt = pb.tok("vector")
                    bank_free[bk2[0]] = xvt
                    bank_free[bk2[1]] = xvt
                    bcons[t] = (wt, xvt)

                p1b_A(0)
                p1b_A(1)
                for t in range(NT):
                    p1b_B1a(t)
                    if t >= 1:
                        p1b_B2(t - 1)
                    p1b_B1b(t)
                    if t + 2 < NT:
                        p1b_A(t + 2)
                p1b_B2(NT - 1)
                pb.barrier()

        def phase_swiglu():
            row_reset()
            ab_reset()
            x2rb = [ab(1024), ab(1024)]
            dump = ab(1024)
            mid0 = ab(384); mid1 = ab(384)
            mids = [mid0, mid1]
            l2wb = kapb.rearrange("p n -> p n")
            l2wb2 = kapb2
            l2r = row(1024)
            ld1 = DMA("sync", l2r, dln2w)
            pb.wait("tensor", ld1)
            bcast_row(l2r[0:1, 0:512], kapb, 512, bank=2)
            bcast_row(l2r[0:1, 512:1024], kapb2, 512, bank=3)
            pb.wait("sync", pb.tok("tensor"))
            ld2 = DMA("sync", l2r, dln2b)
            pb.wait("tensor", ld2)
            l2bc = l1bc
            for j in range(8):
                row2col(l2r[0:1, 128 * j:128 * (j + 1)], l2bc[:, j:j + 1], 128, bank=2)
            pb.wait("sync", pb.tok("tensor"))
            boutr = l2r
            DMA("sync", boutr, dbout)
            rd = {0: DMA("sync", x2rb[0], sp_x2[:, 0:1024])}
            ucons = {}
            for t in range(NT):
                pb.tick()
                if t + 1 < NT:
                    pb.wait("sync", ucons.get(t - 1))
                    rd[t + 1] = DMA("sync", x2rb[(t + 1) % 2],
                                    sp_x2[:, 1024 * (t + 1):1024 * (t + 2)])
                pb.wait("vector", rd[t])
                u = scr[:, 0:1024]
                V(lambda e, d=u, a=xt(t), b=x2rb[t % 2]: e.tensor_sub(d, a, b), 1)
                ucons[t] = pb.tok("vector")
                sums = scr2[:, 0:1]; sqs = scr2[:, 1:2]
                V(lambda e, d=sums, s=u: e.reduce_sum(d, s, axis=AX), 1)
                pb.wait("scalar", pb.tok("vector"))
                SC(lambda e, d=dump, s=u: e.activation(d, s, AF.Square), 1)
                pb.wait("vector", pb.tok("scalar"))
                V(lambda e, d=sqs, s=dump: e.reduce_sum(d, s, axis=AX))
                V(lambda e, d=mcol[:, t:t + 1], s=sums: e.tensor_scalar_mul(d, s, 1.0 / 1024.0))
                tv = scr2[:, 2:3]
                V(lambda e, d=tv, s=sqs: e.tensor_scalar_mul(d, s, 1.0 / 1024.0))
                tm = scr2[:, 3:4]
                V(lambda e, d=tm, a=mcol[:, t:t + 1], b=mcol[:, t:t + 1]: e.tensor_mul(d, a, b))
                V(lambda e, d=tv, a=tv, b=tm: e.tensor_sub(d, a, b))
                rsqrt_newton(rcol[:, t:t + 1], tv, 1e-5, scr2[:, 4:5], scr2[:, 5:6])
                un = scr[:, 0:1024]
                V(lambda e, d=un, s=u, m=mcol[:, t:t + 1], r=rcol[:, t:t + 1]:
                  e.tensor_scalar(d, s, m, r, op0=ALU.subtract, op1=ALU.mult))
                V(lambda e, d=un[:, 0:512], a=un[:, 0:512], b=kapb: e.tensor_mul(d, a, b))
                V(lambda e, d=un[:, 512:1024], a=un[:, 512:1024], b=kapb2:
                  e.tensor_mul(d, a, b), 1)
                vtok = pb.tok("vector")
                pb.wait("tensor", vtok)
                pb.wait("tensor", bank_free[0]); pb.wait("tensor", bank_free[1])
                for j in range(8):
                    b = 0 if j < 4 else 1
                    tr(pp[b][:, 128 * (j % 4):128 * (j % 4 + 1)],
                       un[:, 128 * j:128 * (j + 1)], inc=(1 if j in (3, 7) else 0))
                ttok = pb.tok("tensor")
                pb.wait("scalar", ttok)
                for j in range(8):
                    b = 0 if j < 4 else 1
                    SC(lambda e, d=nT(t)[:, 128 * j:128 * (j + 1)],
                       s=pp[b][:, 128 * (j % 4):128 * (j % 4 + 1)]: e.copy(d, s),
                       inc=(1 if j in (3, 7) else 0))
                stok = pb.tok("scalar")
                bank_free[0] = stok; bank_free[1] = stok
            pb.barrier()

            sgs = [list(range(s, min(s + 3, NT))) for s in range(0, NT, 3)]
            aw_reset()
            wia = [aw(1024).rearrange("p (j n) -> p j n", n=128) for _ in range(2)]
            wib = [aw(1024).rearrange("p (j n) -> p j n", n=128) for _ in range(2)]
            wou = [aw(1024) for _ in range(2)]
            dwr = dwin.rearrange("(j p) o -> p j o", p=128)
            mccons = {}
            ccons = {}
            for sgi, sg in enumerate(sgs):
                n_t = len(sg)
                for mc_i in range(NMC):
                    gi = sgi * NMC + mc_i
                    w = MC[mc_i]
                    pbuf = gi % 2
                    pb.wait("sync", mccons.get(gi - 2))
                    lda = DMA("sync", wia[pbuf][:, :, 0:w],
                              dwr[:, :, 128 * mc_i:128 * mc_i + w])
                    ldb_ = DMA("sync", wib[pbuf][:, :, 0:w],
                               dwr[:, :, HID + 128 * mc_i:HID + 128 * mc_i + w])
                    ldo = DMA("sync", wou[pbuf][0:w, 0:1024],
                              dwout[128 * mc_i:128 * mc_i + w, :])
                    ca = ccst[:, 2 * mc_i:2 * mc_i + 1]
                    cb = ccst[:, 2 * mc_i + 1:2 * mc_i + 2]
                    pb.wait("tensor", lda); pb.wait("tensor", ldb_)
                    pb.wait("tensor", ldo)
                    if sgi == 0:
                        bra = scr2[0:1, 8:136]
                        brb = scr2[0:1, 136:264]
                        pb.wait("sync", ccons.get(gi - 1))
                        la = DMA("sync", bra[0:1, 0:w], dbin[:, 128 * mc_i:128 * mc_i + w])
                        lb = DMA("sync", brb[0:1, 0:w],
                                 dbin[:, HID + 128 * mc_i:HID + 128 * mc_i + w])
                        pb.wait("tensor", la); pb.wait("tensor", lb)
                        pb.wait("tensor", bank_free[0]); pb.wait("tensor", bank_free[1])
                        for j in range(8):
                            mm(pp[0][0:w, 0:1], wia[pbuf][:, j, 0:w], l2bc[:, j:j + 1],
                               j == 0, False)
                        for j in range(8):
                            mm(pp[0][0:w, 1:2], wib[pbuf][:, j, 0:w], l2bc[:, j:j + 1],
                               False, j == 7)
                        mm(pp[0][0:w, 2:3], bra[0:1, 0:w], ones11, False, True)
                        mm(pp[0][0:w, 3:4], brb[0:1, 0:w], ones11, False, True, inc=1)
                        cct = pb.tok("tensor")
                        ccons[gi] = cct
                        pb.wait("vector", cct)
                        V(lambda e, d=ca[0:w, :], s=pp[0][0:w, 0:1]: e.tensor_copy(d, s))
                        V(lambda e, d=ca[0:w, :], a=pp[0][0:w, 2:3], b=ca[0:w, :]:
                          e.tensor_add(d, a, b))
                        V(lambda e, d=cb[0:w, :], s=pp[0][0:w, 1:2]: e.tensor_copy(d, s))
                        V(lambda e, d=cb[0:w, :], a=pp[0][0:w, 3:4], b=cb[0:w, :]:
                          e.tensor_add(d, a, b), 1)
                        cvt = pb.tok("vector")
                        pb.wait("tensor", cvt)
                    pb.wait("tensor", bank_free[0]); pb.wait("tensor", bank_free[1])
                    for ti, t in enumerate(sg):
                        for j in range(8):
                            mm(pp[0][0:w, 128 * ti:128 * ti + 128], wia[pbuf][:, j, 0:w],
                               nT(t)[:, 128 * j:128 * (j + 1)], j == 0, j == 7)
                        for j in range(8):
                            mm(pp[1][0:w, 128 * ti:128 * ti + 128], wib[pbuf][:, j, 0:w],
                               nT(t)[:, 128 * j:128 * (j + 1)], j == 0,
                               j == 7,
                               inc=(1 if (j == 7 and ti == n_t - 1) else 0))
                    abt = pb.tok("tensor")
                    pb.wait("scalar", abt)
                    sb_ = scr[:, 0:384]
                    SC(lambda e, d=sb_[0:w, 0:128 * n_t], s=pp[1][0:w, 0:128 * n_t],
                       b=cb[0:w, :]: e.activation(d, s, AF.Silu, bias=b), 1)
                    sbt = pb.tok("scalar")
                    pb.wait("vector", sbt)
                    mtile = mids[pbuf]
                    V(lambda e, d=mtile[0:w, 0:128 * n_t], a=pp[0][0:w, 0:128 * n_t],
                      s=ca[0:w, :], b=sb_[0:w, 0:128 * n_t]:
                      e.scalar_tensor_tensor(d, a, s, b, op0=ALU.add, op1=ALU.mult), 1)
                    mvt = pb.tok("vector")
                    bank_free[0] = mvt; bank_free[1] = mvt
                    pb.wait("tensor", mvt)
                    for ti, t in enumerate(sg):
                        for half in range(2):
                            bk = 2 + 2 * ti + half
                            if mc_i == 0:
                                pb.wait("tensor", bank_free[bk])
                            mm(pp[bk][:, 0:512], mtile[0:w, 128 * ti:128 * ti + 128],
                               wou[pbuf][0:w, 512 * half:512 * half + 512],
                               mc_i == 0, mc_i == NMC - 1,
                               inc=(1 if (mc_i == NMC - 1 and half == 1 and ti == n_t - 1) else 0))
                    mccons[gi] = pb.tok("tensor")
                ot = pb.tok("tensor")
                pb.wait("vector", ot)
                for ti, t in enumerate(sg):
                    for half in range(2):
                        bk = 2 + 2 * ti + half
                        V(lambda e, d=xt(t)[:, 512 * half:512 * half + 512],
                          a=pp[bk][:, 0:512], b=xt(t)[:, 512 * half:512 * half + 512]:
                          e.tensor_add(d, a, b),
                          (1 if (ti == n_t - 1 and half == 1) else 0))
                vt = pb.tok("vector")
                for i_ in range(2, 8):
                    bank_free[i_] = vt
            pb.barrier()
            bcast_row(boutr[0:1, 0:512], kapb, 512, bank=2)
            bcast_row(boutr[0:1, 512:1024], kapb2, 512, bank=3)
            for t in range(NT):
                V(lambda e, d=xt(t)[:, 0:512], a=kapb, b=xt(t)[:, 0:512]:
                  e.tensor_add(d, a, b))
                V(lambda e, d=xt(t)[:, 512:1024], a=kapb2, b=xt(t)[:, 512:1024]:
                  e.tensor_add(d, a, b), (1 if t == NT - 1 else 0))
            pb.barrier()

        if stage >= 1:
            phase_attn_a()
        if stage >= 2:
            phase_ffn()
        if stage >= 3:
            phase_attn_b()
        if stage >= 4:
            phase_swiglu()
        if stage >= 5:
            phase_ffn()

        pb.barrier()
        otok = DMA("sync", dy.rearrange("(t p) d -> p t d", p=128),
                   xsb.rearrange("p (t d) -> p t d", d=1024))
        pb.xwait("sync", otok)

        blk = ctx.enter_context(nc.Block())
        for e in ENGS:
            def mk(e=e):
                def emit(eng):
                    for entry in pb.q[e]:
                        if entry[0] == "op":
                            _, a, sname, amt = entry
                            inst = a(eng)
                            if sname is not None:
                                inst.then_inc(sems[sname], amt)
                        else:
                            _, sname, v = entry
                            eng.wait_ge(sems[sname], v)
                return emit
            getattr(blk, e)(mk())

    print(f"[kernel] built stage={stage} ops={pb.nops} " +
          " ".join(f"{e}:{len(pb.q[e])}" for e in ENGS))
    return nc


def kernel(**inputs):
    nc = build()
    x = np.ascontiguousarray(inputs["L0_attn_a_x"], dtype=np.float32)
    base = {
        "aWq": inputs["L0_attn_a_Wq"], "abq": inputs["L0_attn_a_bq"].reshape(1, -1),
        "aWk": inputs["L0_attn_a_Wk"], "abk": inputs["L0_attn_a_bk"].reshape(1, -1),
        "aWv": inputs["L0_attn_a_Wv"], "abv": inputs["L0_attn_a_bv"].reshape(1, -1),
        "aWo": inputs["L0_attn_a_Wo"], "abo": inputs["L0_attn_a_bo"].reshape(1, -1),
        "gw": inputs["L0_ffn_gate_w"], "gb": inputs["L0_ffn_gate_b"].reshape(1, -1),
        "ew": inputs["L0_ffn_expert_w"], "eb": inputs["L0_ffn_expert_b"],
        "ln1w": inputs["L0_attn_b_ln1_w"].reshape(1, -1),
        "ln1b": inputs["L0_attn_b_ln1_b"].reshape(1, -1),
        "bwq": inputs["L0_attn_b_wq"], "bwk": inputs["L0_attn_b_wk"],
        "bwv": inputs["L0_attn_b_wv"], "bwg": inputs["L0_attn_b_wg"],
        "bwo": inputs["L0_attn_b_wo"],
        "ln2w": inputs["L0_attn_b_ln2_w"].reshape(1, -1),
        "ln2b": inputs["L0_attn_b_ln2_b"].reshape(1, -1),
        "win": inputs["L0_attn_b_w_in"], "bin": inputs["L0_attn_b_b_in"].reshape(1, -1),
        "wout": inputs["L0_attn_b_w_out"], "bout": inputs["L0_attn_b_b_out"].reshape(1, -1),
    }
    base = {k: np.ascontiguousarray(v, dtype=np.float32) for k, v in base.items()}
    in_maps = []
    for c in range(NCORE):
        m = dict(base)
        m["x"] = np.ascontiguousarray(x[c // 2, (c % 2) * TPC:(c % 2 + 1) * TPC])
        m["m1"] = np.full((128, 1), 1.0 if c % 2 == 0 else 0.0, np.float32)
        m["m2"] = np.full((128, 1), 0.0 if c % 2 == 0 else 1.0, np.float32)
        in_maps.append(m)
    res = run_bass_kernel_spmd(nc, in_maps, core_ids=list(range(NCORE)))
    out = np.empty((Bb, L, D), np.float32)
    for c in range(NCORE):
        out[c // 2, (c % 2) * TPC:(c % 2 + 1) * TPC] = res.results[c]["y"]
    if os.environ.get("MK_TAP"):
        global DBG
        DBG = [res.results[c].get("dbg") for c in range(NCORE)]
    return out


if __name__ == "__main__":
    build()
